# version 23
# baseline (speedup 1.0000x reference)
import itertools
import numpy as np
import ml_dtypes
from contextlib import ExitStack

import concourse.bass as bass
import concourse.mybir as mybir
from concourse.bass_utils import run_bass_kernel_spmd
from concourse.bass_core_rust import IndirectOffsetOnAxis

F32 = mybir.dt.float32
BF16 = mybir.dt.bfloat16
I32 = mybir.dt.int32
AF = mybir.ActivationFunctionType
ALU = mybir.AluOpType
AX = mybir.AxisListType.X
BF = ml_dtypes.bfloat16

B_PER_CORE = 2
T = 2048
D = 1024
NH = 16
NTILE = T // 128
E = 8
MOE_CHUNK = 256
NSEG = 24
EPS = 1e-5

N_SUBLAYERS = 4
B_RUN = B_PER_CORE
DEBUG_DUMP = False
DBG = {}

ENGINES = ("tensor", "vector", "scalar", "gpsimd", "sync")


class Sem:
    __slots__ = ("h", "name", "n")

    def __init__(self, handle, name):
        self.h, self.name, self.n = handle, name, 0

    def bump(self, k):
        self.n += k
        return self.n

    def tok(self):
        return (self, self.n)


class Prog:
    def __init__(self, nc):
        self.nc = nc
        self.streams = {e: [] for e in ENGINES}

    def raw(self, eng, fn):
        self.streams[eng].append(fn)

    def op(self, eng, emit, waits=(), inc=None, inc_k=1):
        waits = tuple(w for w in waits if w is not None)
        for s, v in waits:
            assert v <= s.n, f"wait on future token {s.name}:{v} > {s.n}"
        if inc is not None:
            inc.bump(inc_k)

        def fn(e, waits=waits, emit=emit, inc=inc, inc_k=inc_k):
            for s, v in waits:
                if v > 0:
                    e.wait_ge(s.h, v)
            ins = emit(e)
            if inc is not None:
                ins.then_inc(inc.h, inc_k)
        self.raw(eng, fn)
        return inc.tok() if inc is not None else None

    def dma(self, q, out_ap, in_ap, sem, waits=(), **kw):
        return self.op(q, lambda e: e.dma_start(out=out_ap, in_=in_ap, **kw),
                       waits=waits, inc=sem, inc_k=16)

    def wait(self, eng, *toks):
        toks = tuple(t for t in toks if t is not None)
        for s, v in toks:
            assert v <= s.n

        def fn(e, toks=toks):
            for s, v in toks:
                if v > 0:
                    e.wait_ge(s.h, v)
        self.raw(eng, fn)


class Pool:

    def __init__(self, nc, base_off, nbytes):
        self.nc, self.base, self.size = nc, base_off, nbytes
        self.off, self.marks, self.uid, self.hiwater = 0, [], 0, 0

    def alloc(self, shape, dtype, name=None):
        isz = {F32: 4, BF16: 2, I32: 4}[dtype]
        nb = shape[1] * isz
        self.off = (self.off + 31) // 32 * 32
        o = self.off
        self.off += nb
        self.hiwater = max(self.hiwater, self.off)
        assert self.off <= self.size, f"pool overflow at {name}: {self.off} > {self.size}"
        self.uid += 1
        h = self.nc.alloc_sbuf_tensor_at(f"pl{self.uid}_{name or 'b'}", list(shape), dtype,
                                         offset=self.base + o)
        return h.ap()

    def mark(self):
        self.marks.append(self.off)

    def release(self):
        self.off = self.marks.pop()


RACE_DETECT = True
GATE_ONEPASS = False
MOE_STATIC_LOADER = False
MOE_STATIC_TWAIT = False
MOE_D_LEVEL = 4
MOE_NO_LOADER = False
MOE_NO_GATHER = False
MOE_NO_NOPS = False
MOE_LOADER_SC_ONLY = False
MOE_NO_WAITS = False
FFN_PHASES = "ABCDE"


def build_nc():
    nc = bass.Bass("TRN2", target_bir_lowering=False, debug=False,
                   detect_race_conditions=RACE_DETECT)
    dr = {}

    def din(name, shape, dtype=F32):
        dr[name] = nc.dram_tensor(name, list(shape), dtype, kind="ExternalInput").ap()
        return dr[name]

    x_in = din("x_in", [B_PER_CORE, T, D])
    out = nc.dram_tensor("out", [B_PER_CORE, T, D], F32, kind="ExternalOutput").ap()

    din("wqaTp", [D, D]); din("wkaTp", [D, D]); din("wvaT", [D, D]); din("woaT", [D, D])
    din("bqe_s", [128, 4]); din("bqo_s", [128, 4]); din("bke_s", [128, 4]); din("bko_s", [128, 4])
    din("bv_row", [1, D]); din("boa_row", [1, D])
    din("wqb", [D, D]); din("wdb", [D, 256]); din("wub", [256, 2048]); din("wob", [D, D])
    din("bq2_s", [128, 8]); din("bd_s", [128, 2]); din("bu_row", [1, 2048]); din("bob_row", [1, D])
    din("gwT", [D, 8])
    din("w1T", [E, D, D]); din("w2T", [E, D, D]); din("w3T", [E, D, D])
    din("bstage", [E, 128, 24])
    din("sw1T", [D, 2048]); din("sw2T", [D, 2048]); din("sw3T", [2048, D])
    din("id128", [128, 128], BF16)
    din("id128f", [128, 128])
    din("id8", [8, 8])
    din("ones_row", [65, 128])
    din("ones8", [8, 1])
    din("bd4", [128, 4], BF16)
    din("mask2", [128, 128])
    din("thresh", [8, NSEG])
    din("gcmp", [8, NSEG])
    din("cos4", [128, T], BF16)
    din("sin4", [128, T], BF16)

    xs_scr = nc.dram_tensor("xs_scr", [NSEG * MOE_CHUNK, D], BF16, kind="Internal").ap()
    y_scr = nc.dram_tensor("y_scr", [NSEG * MOE_CHUNK, D], BF16, kind="Internal").ap()
    wmoe = nc.dram_tensor("wmoe", [E, 3, 2, 128, 4096], BF16, kind="Internal").ap()
    sw12_scr = nc.dram_tensor("sw12_scr", [2, 16, 128, 1024], BF16, kind="Internal").ap()
    sw3_scr = nc.dram_tensor("sw3_scr", [16, 128, 1024], BF16, kind="Internal").ap()
    wat_c = {nm: nc.dram_tensor(nm + "_c", [2, 128, 4096], BF16, kind="Internal").ap()
             for nm in ("wqaTp", "wkaTp", "wvaT", "woaT", "wqb", "wob")}
    wdb_c = nc.dram_tensor("wdb_c", [128, 2048], BF16, kind="Internal").ap()
    wub_c = nc.dram_tensor("wub_c", [128, 4096], BF16, kind="Internal").ap()

    with ExitStack() as ctx:
        def sbuf(name, shape, dtype):
            return ctx.enter_context(nc.sbuf_tensor(name, list(shape), dtype)).ap()

        x_sb = sbuf("x_sb", [128, NTILE * D], F32)
        c_id128 = sbuf("c_id128", [128, 128], BF16)
        c_id128f = sbuf("c_id128f", [128, 128], F32)
        c_id8 = sbuf("c_id8", [8, 8], F32)
        c_ones = sbuf("c_ones", [65, 128], F32)
        c_ones8 = sbuf("c_ones8", [8, 1], F32)
        c_bd4 = sbuf("c_bd4", [128, 4], BF16)
        c_mask2 = sbuf("c_mask2", [128, 128], F32)
        c_thresh = sbuf("c_thresh", [8, NSEG], F32)
        c_gcmp = sbuf("c_gcmp", [8, NSEG], F32)
        c_gw = sbuf("c_gw", [128, 64], F32)
        c_gh = sbuf("c_gh", [128, 64], BF16)
        c_gl = sbuf("c_gl", [128, 64], BF16)
        c_battn = sbuf("c_battn", [128, 26], F32)
        c_rows = sbuf("c_rows", [128, 2048], BF16)
        c_onesb = sbuf("c_onesb", [65, 128], BF16)
        tc_bb = [sbuf("tc_bb0", [128, 2048], BF16), sbuf("tc_bb1", [128, 2048], BF16)]

        rem = nc.sbuf_bytes_remaining
        pool_bytes = (rem - 3072) // 32 * 32
        pool_h = ctx.enter_context(nc.sbuf_tensor("pool", [128, pool_bytes], mybir.dt.uint8))
        pool = Pool(nc, nc.lookup_mloc(pool_h).addr, pool_bytes)

        sems = {nm: Sem(ctx.enter_context(nc.semaphore(nm)), nm)
                for nm in ("sL", "sG", "sT", "sV", "sS", "sO", "sW1", "sW3", "sW2", "sWb",
                           "sC", "sCv", "sC1", "sC3", "sRt",
                           "sE1", "sE3", "sE2", "sEb",
                           "sX", "sGs", "sTci", "sTco", "sCs", "sSw", "sSwB", "sSw3", "sSw3B", "sY", "sCsc", "sM")}
        sL, sG, sT, sV, sS, sO = (sems[k] for k in ("sL", "sG", "sT", "sV", "sS", "sO"))
        sW1, sW3, sW2, sWb = sems["sW1"], sems["sW3"], sems["sW2"], sems["sWb"]
        sC, sCv = sems["sC"], sems["sCv"]
        sC1, sC3 = sems["sC1"], sems["sC3"]
        sRt = sems["sRt"]
        sE1, sE3, sE2, sEb = sems["sE1"], sems["sE3"], sems["sE2"], sems["sEb"]
        sX, sGs, sTci, sTco = (sems[k] for k in ("sX", "sGs", "sTci", "sTco"))
        sCs = sems["sCs"]
        sSw, sY = sems["sSw"], sems["sY"]
        sCsc = sems["sCsc"]
        sw_sems = [sems["sSw"], sems["sSwB"]]
        sw3_sems = [sems["sSw3"], sems["sSw3B"]]

        P = Prog(nc)
        rd = {}

        def fence():
            toks = [sm.tok() for sm in (sems["sT"], sems["sV"], sems["sS"], sems["sG"],
                                        sems["sL"], sems["sCs"], sems["sX"], sems["sGs"],
                                        sems["sW1"], sems["sW3"], sems["sW2"], sems["sWb"],
                                        sems["sO"], sems["sC"], sems["sCv"],
                                        sems["sC1"], sems["sC3"], sems["sRt"],
                                        sems["sE1"], sems["sE3"], sems["sE2"], sems["sEb"],
                                        sems["sSw"], sems["sSwB"], sems["sSw3"],
                                        sems["sSw3B"], sems["sY"], sems["sCsc"])]
            for eng2 in ENGINES:
                P.wait(eng2, *toks)
        _uid = itertools.count()

        def mkps(stack, nm, shape, dt2):
            return stack.enter_context(
                nc.psum_tensor(f"{nm}_{next(_uid)}", list(shape), dt2)).ap()

        x_r = x_in.rearrange("b (tl p) c -> b p tl c", p=128)
        out_r = out.rearrange("b (tl p) c -> b p tl c", p=128)
        x_sb3 = x_sb.rearrange("p (tl c) -> p tl c", c=D)
        P.dma("sync", x_sb3, x_r[0], sX)
        for dst, src in [(c_id128, dr["id128"]), (c_id128f, dr["id128f"]),
                         (c_id8, dr["id8"]), (c_ones, dr["ones_row"]),
                         (c_ones8, dr["ones8"]), (c_bd4, dr["bd4"]), (c_mask2, dr["mask2"]),
                         (c_thresh, dr["thresh"]), (c_gcmp, dr["gcmp"])]:
            P.dma("sync", dst, src, sL)
        P.dma("sync", c_gw.rearrange("p (ch e) -> p ch e", e=8), dr["gwT"].rearrange("(ch p) e -> p ch e", p=128), sL)
        for col, nm, w in [(0, "bqe_s", 4), (4, "bqo_s", 4), (8, "bke_s", 4), (12, "bko_s", 4),
                           (16, "bq2_s", 8), (24, "bd_s", 2)]:
            P.dma("sync", c_battn[:, col:col + w], dr[nm], sL)
        for r0, o, nm, w in [(0, 0, "bv_row", D), (0, 1024, "boa_row", D),
                             (32, 0, "bu_row", 2048), (64, 0, "bob_row", D)]:
            P.dma("gpsimd", c_rows[r0:r0 + 1, o:o + w], dr[nm], sG)
        P.dma("gpsimd", c_onesb, dr["ones_row"], sG)
        tok_consts = sL.tok()
        t1 = P.op("vector", lambda e: e.tensor_copy(c_gh, c_gw), waits=[tok_consts], inc=sV)
        P.op("vector", lambda e: e.tensor_sub(c_gl, c_gw, c_gh), waits=[t1], inc=sV)
        tok_gate = sV.tok()

        pool.mark()
        ztile = pool.alloc([128, 2048], BF16, "ztile")
        tz = P.op("gpsimd", lambda e: e.memset(ztile, 0.0), inc=sems["sM"])
        xs_r = xs_scr.rearrange("(n a p) c -> n p a c", p=128, a=2)
        for i in range(NSEG * MOE_CHUNK // 256):
            P.dma("gpsimd", xs_r[i], ztile.rearrange("p (a c) -> p a c", c=D), sG, waits=[tz])
        pool.release()

        tok_attn = {}

        def emit_transcode():
            i = [0]
            pend = [None]

            def flush():
                if pend[0] is not None:
                    dst_ap, bv = pend[0]
                    P.dma("sync", dst_ap, bv, sTco, waits=[sTci.tok()])
                    pend[0] = None

            def piece(dst_ap, src_ap, n):
                k = i[0] % 2
                bv = tc_bb[k][:, :n].rearrange("p (a c) -> p a c", c=src_ap.shape[-1])
                P.dma("gpsimd", bv, src_ap, sTci,
                      waits=[sTco.tok()] if i[0] >= 2 else [])
                flush()
                pend[0] = (dst_ap, bv)
                i[0] += 1

            for nm in ("wqaTp", "wkaTp", "wvaT", "woaT", "wqb", "wob"):
                src = dr[nm].rearrange("(h a p) c -> h p a c", p=128, a=4)
                dc = wat_c[nm].rearrange("h p (s a c) -> h s p a c", s=2, c=1024)
                for h in range(2):
                    for sub in range(2):
                        piece(dc[h, sub], src[h][:, 2 * sub:2 * sub + 2], 2048)
                flush()
                tok_attn[nm] = sTco.tok()
            piece(wdb_c.rearrange("p (a c) -> p a c", c=256),
                  dr["wdb"].rearrange("(a p) c -> p a c", p=128), 2048)
            flush()
            tok_attn["wdb"] = sTco.tok()
            wub_v = wub_c.rearrange("p (ch c) -> ch p c", ch=2)
            srcu = dr["wub"].rearrange("(ch p) c -> ch p c", p=128)
            for ch in range(2):
                piece(wub_v[ch].unsqueeze(1), srcu[ch].unsqueeze(1), 2048)
            flush()
            tok_attn["wub"] = sTco.tok()
            for ei in range(E):
                for kind, nm in ((0, "w1T"), (1, "w3T"), (2, "w2T")):
                    src = dr[nm].rearrange("e (h a p) c -> e h p a c", p=128, a=4)
                    dstr = wmoe.rearrange("e k h p (s a c) -> e k h s p a c", s=2, c=1024)
                    for h in range(2):
                        for sub in range(2):
                            piece(dstr[ei, kind, h, sub], src[ei, h][:, 2 * sub:2 * sub + 2], 2048)
            for m, nm in ((0, "sw1T"), (1, "sw2T")):
                src = dr[nm].rearrange("(a p) (blk i) -> blk p a i", p=128, blk=16)
                for blk in range(16):
                    piece(sw12_scr[m, blk], src[blk], 1024)
            src = dr["sw3T"].rearrange("(ic p) c -> ic p c", p=128)
            for ic in range(16):
                piece(sw3_scr[ic], src[ic], 1024)
            flush()
            return (sTco.tok(),)

        emit_transcode()

        def mk_ln(pl, n=16):
            d = {nm: pl.alloc([128, n], F32, "ln_" + nm)
                 for nm in ("sum", "sq", "m2", "var", "r", "nm")}
            d["sqs"] = pl.alloc([128, 1024], BF16, "ln_sqs")
            d["n"] = n
            return d

        def ln_stats_tile(sd, t, x_tile, extra_waits=()):
            tv = P.op("vector", lambda e, t=t: e.reduce_sum(sd["sum"][:, t:t + 1], x_tile, AX),
                      waits=extra_waits, inc=sV)
            ts2 = P.op("scalar", lambda e, t=t: e.activation(out=sd["sqs"], in_=x_tile, func=AF.Square,
                                                             accum_out=sd["sq"][:, t:t + 1]),
                       waits=tuple(extra_waits) + (rd.get(("sqs", id(sd))),), inc=sS)
            rd[("sqs", id(sd))] = ts2
            return tv, ts2

        def ln_finish(sd):
            t1 = P.op("vector", lambda e: e.tensor_mul(sd["m2"], sd["sum"], sd["sum"]),
                      waits=[sV.tok(), sS.tok()], inc=sV)
            t2 = P.op("vector", lambda e: e.scalar_tensor_tensor(sd["var"], sd["m2"], -1.0 / D,
                                                                 sd["sq"], ALU.mult, ALU.add),
                      waits=[t1], inc=sV)
            t3 = P.op("vector", lambda e: e.tensor_scalar(sd["var"], sd["var"], 1.0 / D, EPS,
                                                          ALU.mult, ALU.add), waits=[t2], inc=sV)
            t4 = P.op("scalar", lambda e: e.activation(out=sd["m2"], in_=sd["var"], func=AF.Sqrt,
                                                       bias=0.0, scale=1.0), waits=[t3], inc=sS)
            t5 = P.op("vector", lambda e: e.reciprocal(sd["r"], sd["m2"]), waits=[t4], inc=sV)
            t6 = P.op("vector", lambda e: e.tensor_scalar(sd["nm"], sd["sum"], -1.0 / D, None,
                                                          ALU.mult), waits=[sV.tok()], inc=sV)
            t7 = P.op("vector", lambda e: e.tensor_mul(sd["nm"], sd["nm"], sd["r"]),
                      waits=[t6, t5], inc=sV)
            sd["tok"] = t7
            return t7

        def ln_norm(sd, t, x_tile, dst_bf, dst_f32=None, dst_lo=None, extra_waits=()):
            w = list(extra_waits) + [sd["tok"]]
            if dst_f32 is not None:
                th = P.op("vector", lambda e, t=t: e.tensor_scalar(dst_bf, x_tile,
                                                                   sd["nm"][:, t:t + 1],
                                                                   sd["r"][:, t:t + 1],
                                                                   ALU.add, ALU.mult),
                          waits=w, inc=sV)
                tf = P.op("vector", lambda e, t=t: e.tensor_scalar(dst_f32, x_tile,
                                                                   sd["nm"][:, t:t + 1],
                                                                   sd["r"][:, t:t + 1],
                                                                   ALU.add, ALU.mult),
                          waits=w, inc=sV)
                tl = P.op("vector", lambda e: e.tensor_sub(dst_lo, dst_f32, dst_bf), waits=[tf], inc=sV)
                return th, tl
            th = P.op("vector", lambda e, t=t: e.tensor_scalar(dst_bf, x_tile, sd["nm"][:, t:t + 1],
                                                               sd["r"][:, t:t + 1], ALU.add, ALU.mult),
                      waits=w, inc=sV)
            return th, None

        def transpose8(src_ap, dst_fn, banks, wait_tok, n=8, key_pfx="psTt", alt=False):
            nb = len(banks)
            lastV = lastS = None
            for j in range(n):
                key = (key_pfx, j % nb)
                pt = banks[j % nb]
                tmm = P.op("tensor", lambda e, pt=pt, j=j: e.transpose(
                    pt[:, 0:128], src_ap[:, 128 * j:128 * (j + 1)], c_id128),
                    waits=[wait_tok, sL.tok(), rd.get(key)], inc=sT)
                if alt and j % 2 == 1:
                    lastS = P.op("scalar", lambda e, pt=pt, j=j: e.copy(out=dst_fn(j), in_=pt[:, 0:128]),
                                 waits=[tmm], inc=sS)
                    rd[key] = lastS
                else:
                    lastV = P.op("vector", lambda e, pt=pt, j=j: e.tensor_copy(dst_fn(j), pt[:, 0:128]),
                                 waits=[tmm], inc=sV)
                    rd[key] = lastV
            if alt and lastS is not None:
                lastV = P.op("vector", lambda e: e.nop(), waits=[lastS, lastV], inc=sV)
            return lastV

        def transpose8g(src_fn, dst_views, banks2, wait_tok, keys2):
            ev = None
            for grp in range(2):
                bank, key, dv = banks2[grp], keys2[grp], dst_views[grp]
                for jj in range(4):
                    j = 4 * grp + jj
                    P.op("tensor", lambda e, j=j, jj=jj, bank=bank: e.matmul(
                        bank[:, 128 * jj:128 * (jj + 1)], src_fn(j), c_id128,
                        start=(jj == 0), stop=(jj == 3)),
                        waits=([wait_tok, sL.tok(), rd.get(key)] if jj == 0 else []),
                        inc=sT if jj == 3 else None)
                tmm = sT.tok()
                bk = bank
                if len(dv.shape) == 3:
                    bk = bank.rearrange("p (a c) -> p a c", c=128)
                ev = P.op("vector", lambda e, bk=bk, dv=dv: e.tensor_copy(dv, bk),
                          waits=[tmm], inc=sV)
                rd[key] = ev
            return ev

        def sublayer_attn_a(b, first):
            pool.mark()
            Qr = pool.alloc([128, 4 * T], BF16, "Qr")
            Kr = pool.alloc([128, 4 * T], BF16, "Kr")
            Vx = pool.alloc([128, NTILE * NH * 65], BF16, "Vx")
            mhat = pool.alloc([128, 1], F32, "mhat")
            Vx4 = Vx.rearrange("p (t h j) -> p t h j", h=NH, j=65)
            pc = ExitStack()
            with pc:
                psTb = [mkps(pc, "psTa", [128, 512], F32) for _ in range(2)]
                psQ = mkps(pc, "psQ", [128, 512], F32)
                psQ2 = mkps(pc, "psQ2", [128, 512], F32)
                psM = mkps(pc, "psM", [128, 512], F32)
                psM2 = mkps(pc, "psM2", [128, 512], F32)
                psM3 = mkps(pc, "psM3", [128, 512], F32)
                psM4 = mkps(pc, "psM4", [128, 512], F32)

                fence()
                pool.mark()
                wqk = pool.alloc([128, 2 * 8192], BF16, "wqk")
                fm2 = [pool.alloc([128, 8 * 512], BF16, "fmA"), pool.alloc([128, 8 * 512], BF16, "fmB")]
                nh4 = pool.alloc([128, 2048], BF16, "nh4")
                cs2 = [pool.alloc([128, 1024], BF16, "csA"), pool.alloc([128, 1024], BF16, "csB")]
                lns = mk_ln(pool)
                ev = {k: pool.alloc([128, 512], BF16, "ev" + k) for k in ("qe", "qo")}
                for tt in range(NTILE):
                    ln_stats_tile(lns, tt, x_sb[:, 1024 * tt:1024 * (tt + 1)],
                                  extra_waits=[sX.tok()] if tt == 0 else ())
                ln_finish(lns)
                for i2, wnm in enumerate(("wqaTp", "wkaTp")):
                    for h in range(2):
                        P.dma("sync", wqk[:, i2 * 8192 + h * 4096: i2 * 8192 + (h + 1) * 4096],
                              wat_c[wnm][h], sL, waits=[tok_attn[wnm]])
                tok_w = sL.tok()

                def emit_trs_i(tc):
                    par = tc % 2
                    fm = fm2[par]
                    P.wait("vector", rd.get(("fmw", par)))
                    P.dma("sync", cs2[par][:, 0:512], dr["cos4"][:, 512 * tc:512 * (tc + 1)], sCs,
                          waits=[rd.get(("cs", par))])
                    P.dma("sync", cs2[par][:, 512:1024], dr["sin4"][:, 512 * tc:512 * (tc + 1)], sCs,
                          waits=[rd.get(("cs", par))])
                    tok_fm = None
                    for ti in range(4):
                        tt = 4 * tc + ti
                        np_ = ti % 2
                        th, _ = ln_norm(lns, tt, x_sb[:, 1024 * tt:1024 * (tt + 1)],
                                        nh4[:, 1024 * np_:1024 * (np_ + 1)],
                                        extra_waits=[rd.get(("nh4", np_))])
                        fm_v = fm.rearrange("p (a c) -> p a c", c=512)[:, :, 128 * ti:128 * (ti + 1)]
                        tok_fm = transpose8g(
                            lambda j, np_=np_, nh4=nh4: nh4[:, 1024 * np_ + 128 * j: 1024 * np_ + 128 * (j + 1)],
                            (fm_v[:, 0:4], fm_v[:, 4:8]), psTb, th, (("aT", 0), ("aT", 1)))
                        rd[("nh4", np_)] = sT.tok()
                    return tok_fm

                def emit_mms_i(tc, tok_fm):
                    par = tc % 2
                    fm = fm2[par]
                    cs = cs2[par]
                    tok_cs = sCs.tok()
                    for is_q, base_w, bce, bco, dstbuf in (
                            (True, 0, 0, 4, Qr), (False, 8192, 8, 12, Kr)):
                        for rb in range(4):
                            for half, ps, pskey, bcol, d2 in (
                                    (0, psQ, "psQ", bce, "qe"), (1, psQ2, "psQ2", bco, "qo")):
                                woff = base_w + 512 * half
                                for a in range(8):
                                    P.op("tensor", lambda e, ps=ps, a=a, rb=rb, woff=woff, fm=fm: e.matmul(
                                        ps, wqk[:, woff + a * 1024 + rb * 128: woff + a * 1024 + rb * 128 + 128],
                                        fm[:, 512 * a:512 * (a + 1)], start=(a == 0), stop=(a == 7)),
                                        waits=([tok_fm, tok_w, sL.tok(), rd.get(pskey)] if a == 0 else []),
                                        inc=sT if a == 7 else None)
                                tmm = sT.tok()
                                sc2 = 0.125 if is_q else 1.0
                                tev = P.op("vector", lambda e, ps=ps, d2=d2, bc=bcol, rb=rb, sc2=sc2:
                                           e.tensor_scalar(ev[d2], ps, c_battn[:, bc + rb:bc + rb + 1],
                                                           sc2, ALU.add, ALU.mult),
                                           waits=[tmm, rd.get("evw")], inc=sV)
                                rd[pskey] = tev
                            tu = P.op("vector", lambda e, cs=cs: e.tensor_mul(ev["qe"], ev["qe"], cs[:, 0:512]),
                                      waits=[tev, tok_cs], inc=sV)
                            tv = P.op("vector", lambda e, cs=cs: e.tensor_mul(ev["qo"], ev["qo"], cs[:, 512:1024]),
                                      waits=[tu], inc=sV)
                            tr = P.op("vector", lambda e, dstbuf=dstbuf, rb=rb, tc=tc: e.tensor_sub(
                                dstbuf[:, rb * T + 512 * tc: rb * T + 512 * (tc + 1)],
                                ev["qe"], ev["qo"]), waits=[tv], inc=sV)
                            rd[("cs", par)] = tr
                            rd["evw"] = tr
                    rd[("fmw", par)] = sT.tok()

                toks_i = {}
                for tc in range(4):
                    toks_i[tc] = emit_trs_i(tc)
                    if tc > 0:
                        emit_mms_i(tc - 1, toks_i[tc - 1])
                emit_mms_i(3, toks_i[3])
                pool.release()

                fence()
                pool.mark()
                wv = pool.alloc([128, 8192], BF16, "wv")
                fm2 = [pool.alloc([128, 8 * 512], BF16, "fm2A"), pool.alloc([128, 8 * 512], BF16, "fm2B")]
                nh4 = pool.alloc([128, 2048], BF16, "nh4v")
                lns = mk_ln(pool)
                for tt in range(NTILE):
                    ln_stats_tile(lns, tt, x_sb[:, 1024 * tt:1024 * (tt + 1)])
                ln_finish(lns)
                for h in range(2):
                    P.dma("sync", wv[:, h * 4096:(h + 1) * 4096], wat_c["wvaT"][h], sL,
                          waits=[tok_attn["wvaT"]])
                tok_w = sL.tok()
                tvx = P.op("vector", lambda e: e.memset(Vx4[:, :, :, 64:65], 1.0), inc=sV)

                def emit_trs_v(tc):
                    par = tc % 2
                    fm = fm2[par]
                    P.wait("vector", rd.get(("fmwv", par)))
                    tok_fm = None
                    for ti in range(4):
                        tt = 4 * tc + ti
                        np_ = ti % 2
                        th, _ = ln_norm(lns, tt, x_sb[:, 1024 * tt:1024 * (tt + 1)],
                                        nh4[:, 1024 * np_:1024 * (np_ + 1)],
                                        extra_waits=[rd.get(("nh4v", np_))])
                        fm_v = fm.rearrange("p (a c) -> p a c", c=512)[:, :, 128 * ti:128 * (ti + 1)]
                        tok_fm = transpose8g(
                            lambda j, np_=np_, nh4=nh4: nh4[:, 1024 * np_ + 128 * j: 1024 * np_ + 128 * (j + 1)],
                            (fm_v[:, 0:4], fm_v[:, 4:8]), psTb, th, (("aT", 0), ("aT", 1)))
                        rd[("nh4v", np_)] = sT.tok()
                    return tok_fm

                def emit_mms_v(tc, tok_fm):
                    par = tc % 2
                    fm = fm2[par]
                    pmb = [psM, psM2, psQ, psQ2, psM3, psM4]
                    for ti in range(4):
                        tt = 4 * tc + ti
                        for oc in range(2):
                            pm = pmb[(2 * ti + oc) % 6]
                            for a in range(8):
                                P.op("tensor", lambda e, a=a, ti=ti, oc=oc, fm=fm, pm=pm: e.matmul(
                                    pm, fm[:, 512 * a + 128 * ti:512 * a + 128 * (ti + 1)],
                                    wv[:, a * 1024 + 512 * oc: a * 1024 + 512 * (oc + 1)],
                                    start=(a == 0), stop=False),
                                    waits=([tok_fm, tok_w, rd.get(("psM", (2 * ti + oc) % 6)), tvx] if a == 0 else []))
                            P.op("tensor", lambda e, oc=oc, pm=pm: e.matmul(
                                pm, c_onesb[0:1, :], c_rows[0:1, 512 * oc:512 * (oc + 1)],
                                start=False, stop=True), inc=sT)
                            tmm = sT.tok()
                            tev = P.op("vector", lambda e, tt=tt, oc=oc, pm=pm: e.tensor_copy(
                                Vx4[:, tt, 8 * oc:8 * (oc + 1), 0:64],
                                pm.rearrange("p (h j) -> p h j", j=64)),
                                waits=[tmm], inc=sV)
                            rd[("psM", (2 * ti + oc) % 6)] = tev
                    rd[("fmwv", par)] = sT.tok()

                toks_v = {}
                for tc in range(4):
                    toks_v[tc] = emit_trs_v(tc)
                    if tc > 0:
                        emit_mms_v(tc - 1, toks_v[tc - 1])
                emit_mms_v(3, toks_v[3])
                pool.release()

                fence()
                pool.mark()
                sqb = pool.alloc([128, 2048], BF16, "sqb")
                nrm = pool.alloc([4, 40], F32, "nrm")
                srow = pool.alloc([1, 8], F32, "srow")
                tprev = sV.tok()
                tsq_war = None
                for wi, srcb in ((0, Qr), (1, Kr)):
                    for blk in range(4):
                        tsq = P.op("scalar", lambda e, srcb=srcb, blk=blk: e.activation(
                            out=sqb, in_=srcb[:, blk * T:(blk + 1) * T], func=AF.Square),
                            waits=[tprev, tsq_war], inc=sS)
                        for qc in range(4):
                            tmm = P.op("tensor", lambda e, qc=qc: e.matmul(
                                psM[0:4, :], c_bd4, sqb[:, 512 * qc:512 * (qc + 1)],
                                start=True, stop=True), waits=[tsq, rd.get("psM")], inc=sT)
                            tprev = P.op("vector", lambda e, wi=wi, blk=blk, qc=qc: e.reduce_max(
                                nrm[:, 16 * wi + 4 * blk + qc:16 * wi + 4 * blk + qc + 1],
                                psM[0:4, :], AX), waits=[tmm], inc=sV)
                            rd["psM"] = tprev
                        tsq_war = tprev
                t1 = P.op("vector", lambda e: e.reduce_max(nrm[:, 32:33], nrm[:, 0:16], AX),
                          waits=[tprev], inc=sV)
                t2 = P.op("vector", lambda e: e.reduce_max(nrm[:, 33:34], nrm[:, 16:32], AX),
                          waits=[t1], inc=sV)
                t3 = P.op("tensor", lambda e: e.transpose(psM[0:2, 0:4], nrm[0:4, 32:34],
                                                          c_id8[0:4, 0:4]),
                          waits=[t2, rd.get("psM")], inc=sT)
                t4 = P.op("vector", lambda e: e.reduce_max(nrm[0:2, 34:35], psM[0:2, 0:4], AX),
                          waits=[t3], inc=sV)
                rd["psM"] = t4
                t5 = P.op("tensor", lambda e: e.transpose(psM[0:1, 0:2], nrm[0:2, 34:35],
                                                          c_id8[0:2, 0:2]),
                          waits=[t4, rd.get("psM")], inc=sT)
                t6 = P.op("vector", lambda e: e.tensor_copy(srow[0:1, 0:2], psM[0:1, 0:2]),
                          waits=[t5], inc=sV)
                rd["psM"] = t6
                t7 = P.op("vector", lambda e: e.tensor_mul(srow[0:1, 2:3], srow[0:1, 0:1],
                                                           srow[0:1, 1:2]), waits=[t6], inc=sV)
                t8 = P.op("scalar", lambda e: e.activation(out=srow[0:1, 3:4], in_=srow[0:1, 2:3],
                                                           func=AF.Sqrt, bias=0.0, scale=1.0),
                          waits=[t7], inc=sS)
                t9 = P.op("vector", lambda e: e.tensor_scalar(srow[0:1, 4:5], srow[0:1, 3:4],
                                                              -1.0, None, ALU.mult),
                          waits=[t8], inc=sV)
                tb = P.op("tensor", lambda e: e.matmul(psM[:, 0:1], c_ones[0:1, :], srow[0:1, 4:5],
                                                       start=True, stop=True),
                          waits=[t9, rd.get("psM")], inc=sT)
                tok_mhat = P.op("vector", lambda e: e.tensor_copy(mhat, psM[:, 0:1]),
                                waits=[tb], inc=sV)
                rd["psM"] = tok_mhat
                pool.release()

            if True:
                fence()
                pool.mark()
                wo = pool.alloc([128, 8192], BF16, "wo")
                Ut = [pool.alloc([128, 512], BF16, "Ut0"), pool.alloc([128, 512], BF16, "Ut1")]
                rlr = [pool.alloc([1, 512], F32, "rlr0"), pool.alloc([1, 512], F32, "rlr1")]
                rlbc = [pool.alloc([64, 512], F32, "rlbc0"), pool.alloc([64, 512], F32, "rlbc1")]
                yq = [pool.alloc([128, T], BF16, "yq0"), pool.alloc([128, T], BF16, "yq1")]
                for h in range(2):
                    P.dma("sync", wo[:, h * 4096:(h + 1) * 4096], wat_c["woaT"][h], sL,
                          waits=[tok_attn["woaT"]])
                tok_w = sL.tok()
                with ExitStack() as pc2:
                    psSt = [mkps(pc2, "psSt0", [128, 512], F32), mkps(pc2, "psSt1", [128, 512], F32)]
                    psPV = [mkps(pc2, "psPV0", [65, 512], F32), mkps(pc2, "psPV1", [65, 512], F32),
                            mkps(pc2, "psPV2", [65, 512], F32)]
                    psBc = mkps(pc2, "psBc0", [64, 512], F32)
                    psWo = [mkps(pc2, "psWo0", [128, 512], F32), mkps(pc2, "psWo1", [128, 512], F32)]
                    x_war = sV.tok()

                    def emit_norm(job):
                        (pvi2, ysl2, yrow2, qg2, hh2) = job
                        ri = pvi2 % 2
                        tok_pv = job_tok[0]
                        tr1 = P.op("vector", lambda e, pvi=pvi2, ri=ri: e.reciprocal(rlr[ri], psPV[pvi][64:65, :]),
                                   waits=[tok_pv, rd.get(("rlr", ri))], inc=sV)
                        tb1 = P.op("tensor", lambda e, ri=ri: e.matmul(psBc, c_ones[0:1, 0:64],
                                                                      rlr[ri], start=True, stop=True),
                                   waits=[tr1, rd.get("psBc")], inc=sT)
                        rd[("rlr", ri)] = tb1
                        tb2 = P.op("vector", lambda e, ri=ri: e.tensor_copy(rlbc[ri], psBc),
                                   waits=[tb1, rd.get(("rlbc", ri))], inc=sV)
                        rd["psBc"] = tb2
                        ty = P.op("vector", lambda e, ysl=ysl2, yrow0=yrow2, qg=qg2, pvi=pvi2, ri=ri: e.tensor_mul(
                            ysl[yrow0:yrow0 + 64, 512 * qg:512 * (qg + 1)], psPV[pvi][0:64, :], rlbc[ri]),
                            waits=[tb2, rd.get(("yq", hh2 // 2))], inc=sV)
                        rd[("psPV", pvi2)] = ty
                        rd[("rlbc", ri)] = ty
                        return ty

                    pending_norm = [None]
                    job_tok = [None]
                    gctr = 0
                    for quad in range(4):
                        yq_tok = []
                        for hh in range(4):
                            h = 4 * quad + hh
                            blk, base = h // 4, 32 * (h % 4)
                            ysl, yrow0 = yq[hh // 2], 64 * (hh % 2)
                            for qg in range(4):
                                kts = 4 * qg + 4
                                pvi = gctr % 3
                                gctr += 1
                                pend = None
                                for kt in range(kts + 1):
                                    if kt < kts:
                                        si = kt % 2
                                        ps = psSt[si]
                                        tmm = P.op("tensor", lambda e, ps=ps, blk=blk, base=base, kt=kt, qg=qg:
                                                   e.matmul(ps, Kr[base:base + 32,
                                                                   blk * T + 128 * kt: blk * T + 128 * (kt + 1)],
                                                            Qr[base:base + 32,
                                                               blk * T + 512 * qg: blk * T + 512 * (qg + 1)],
                                                            start=True, stop=True, tile_position=(base, 0)),
                                                   waits=[tok_mhat, rd.get(("st", si))], inc=sT)
                                        tmask = tmm
                                        if kt >= 4 * qg:
                                            pw = 128 * kt - 512 * qg
                                            if pw > 0:
                                                tmask = P.op("vector", lambda e, ps=ps, pw=pw: e.memset(
                                                    ps[:, 0:pw], -1e10), waits=[tmask], inc=sV)
                                            tmask = P.op("vector", lambda e, ps=ps, pw=pw: e.tensor_add(
                                                ps[:, pw:pw + 128], ps[:, pw:pw + 128], c_mask2),
                                                waits=[tmask, sL.tok()], inc=sV)
                                        texp = P.op("scalar", lambda e, ps=ps, si=si: e.activation(
                                            out=Ut[si], in_=ps, func=AF.Exp, bias=mhat[:, 0:1], scale=1.0),
                                            waits=[tmask, rd.get(("ut", si))], inc=sS)
                                        rd[("st", si)] = texp
                                    if kt == 3 and pending_norm[0] is not None:
                                        yq_tok.append(emit_norm(pending_norm[0]))
                                        pending_norm[0] = None
                                    if pend is not None:
                                        ptexp, psi, pkt = pend
                                        tpv = P.op("tensor", lambda e, psi=psi, pkt=pkt, h=h, kts=kts, pvi=pvi: e.matmul(
                                            psPV[pvi], Vx4[:, pkt, h, :].squeeze(),
                                            Ut[psi], start=(pkt == 0), stop=(pkt == kts - 1)),
                                            waits=[ptexp] + ([rd.get(("psPV", pvi))] if pkt == 0 else []), inc=sT)
                                        rd[("ut", psi)] = tpv
                                    if kt < kts:
                                        pend = (texp, si, kt)
                                job_tok[0] = sT.tok()
                                pending_norm[0] = (pvi, ysl, yrow0, qg, hh)
                        if pending_norm[0] is not None:
                            yq_tok.append(emit_norm(pending_norm[0]))
                            pending_norm[0] = None
                        tok_y = yq_tok[-1]
                        for tt2 in range(NTILE):
                            for oc in range(2):
                                pw_i = (tt2 * 2 + oc) % 2
                                psw = psWo[pw_i]
                                w0 = [tok_y, tok_w] + ([x_war] if (tt2 == 0 and oc == 0) else []) + [rd.get(("psWo", pw_i))]
                                P.op("tensor", lambda e, tt2=tt2, oc=oc, psw=psw, quad=quad: e.matmul(
                                    psw, yq[0][:, 128 * tt2:128 * (tt2 + 1)],
                                    wo[:, (2 * quad) * 1024 + 512 * oc: (2 * quad) * 1024 + 512 * (oc + 1)],
                                    start=True, stop=False), waits=w0)
                                P.op("tensor", lambda e, tt2=tt2, oc=oc, psw=psw, quad=quad: e.matmul(
                                    psw, yq[1][:, 128 * tt2:128 * (tt2 + 1)],
                                    wo[:, (2 * quad + 1) * 1024 + 512 * oc: (2 * quad + 1) * 1024 + 512 * (oc + 1)],
                                    start=False, stop=(quad != 0)),
                                    inc=sT if quad != 0 else None)
                                if quad == 0:
                                    P.op("tensor", lambda e, oc=oc, psw=psw: e.matmul(
                                        psw, c_onesb[0:1, :], c_rows[0:1, 1024 + 512 * oc: 1024 + 512 * (oc + 1)],
                                        start=False, stop=True), inc=sT)
                                tmm = sT.tok()
                                tx = P.op("vector", lambda e, tt2=tt2, oc=oc, psw=psw: e.tensor_add(
                                    x_sb[:, 1024 * tt2 + 512 * oc: 1024 * tt2 + 512 * (oc + 1)],
                                    x_sb[:, 1024 * tt2 + 512 * oc: 1024 * tt2 + 512 * (oc + 1)], psw),
                                    waits=[tmm], inc=sV)
                                rd[("psWo", pw_i)] = tx
                                x_war = tx
                        rd[("yq", 0)] = x_war
                        rd[("yq", 1)] = x_war
                pool.release()
            pool.release()


        def sublayer_ffn(b, ridx):
            J = 2 * T
            fence()
            for eng2 in ENGINES:
                P.wait(eng2, sTco.tok(), sTci.tok())
            pool.mark()
            logits = pool.alloc([8, T], F32, "logits")
            w_sb = pool.alloc([128, 32], F32, "w_sb")
            idx_sb = pool.alloc([128, 32], I32, "idx_sb")
            rsm = pool.alloc([128, 16], F32, "rsm")
            rrow = pool.alloc([8, NSEG], F32, "rrow")
            tsrow = pool.alloc([1, 16], F32, "tsrow")
            ts_i32 = pool.alloc([1, 8], I32, "ts_i32")
            eof_f = pool.alloc([1, NSEG], F32, "eof_f")
            eof_i32 = pool.alloc([1, NSEG], I32, "eof_i32")
            lgt = pool.alloc([128, 8], F32, "lgt")
            sce = pool.alloc([128, 8], F32, "sce")
            sc = pool.alloc([128, 8], F32, "sc")
            top8 = pool.alloc([128, 8], F32, "top8")
            oh0 = pool.alloc([128, 8], F32, "oh0")
            oh1 = pool.alloc([128, 8], F32, "oh1")
            xs_pp = [pool.alloc([128, 1024], BF16, "xs0"), pool.alloc([128, 1024], BF16, "xs1")]
            pool.off = (pool.off + 31) // 32 * 32
            pool.mark()
            nh_tm = pool.alloc([128, NTILE * 1024], BF16, "nh_tm")
            oh_fm = pool.alloc([8, J], BF16, "oh_fm")
            incl = pool.alloc([8, J], F32, "incl")

            pool.mark()
            fm = pool.alloc([128, 8 * 512], BF16, "ffm")
            nf32 = pool.alloc([128, 1024], F32, "nf32")
            nl_t = pool.alloc([128, 1024], BF16, "nl_t")
            nlfm = pool.alloc([128, 8 * 128], BF16, "nlfm")
            a_sb = pool.alloc([128, 512], BF16, "a_sb")
            g_sb = pool.alloc([128, 16 * 512], BF16, "g_sb")
            sw_r = [pool.alloc([128, 2 * 1024], BF16, "swr0"),
                    pool.alloc([128, 2 * 1024], BF16, "swr1")]
            sw3_r = [pool.alloc([128, 1024], BF16, "sw3r0"),
                     pool.alloc([128, 1024], BF16, "sw3r1")]
            lns = mk_ln(pool)
            for tt in range(NTILE):
                ln_stats_tile(lns, tt, x_sb[:, 1024 * tt:1024 * (tt + 1)],
                              extra_waits=[sX.tok()] if tt == 0 else ())
            ln_finish(lns)
            tc_dep = None
            with ExitStack() as pca:
                psTb = [mkps(pca, "fT", [128, 512], F32) for _ in range(2)]
                psA = mkps(pca, "fA", [128, 512], F32)
                psB = mkps(pca, "fB", [128, 512], F32)
                psZ2 = [mkps(pca, "fZ0", [128, 1024], F32), mkps(pca, "fZ1", [128, 1024], F32)]
                psG = psB[0:8, 0:128]
                psRf = psA[:, 128:256]
                pend_oh = []

                def emit_oh_pair(job):
                    tt2, jta0, jta1 = job
                    for k2, (oh, tok) in enumerate(((oh0, jta0), (oh1, jta1))):
                        tb1 = P.op("tensor", lambda e, oh=oh: e.transpose(
                            psRf[0:8, :], oh, c_id128f),
                            waits=[tok, sL.tok(), rd.get("psRf")], inc=sT)
                        tb2 = P.op("vector", lambda e, k2=k2, tt=tt2: e.tensor_copy(
                            oh_fm[0:8, k2 * T + 128 * tt: k2 * T + 128 * (tt + 1)], psRf[0:8, :]),
                            waits=[tb1], inc=sV)
                        rd["psRf"] = tb2
                        rd["psA_rt"] = tb2
                        rd["oh0" if k2 == 0 else "oh1"] = tb2
                    jta0 = oh0

                def emit_route_tile(tt, tok_lg):
                    t1 = P.op("tensor", lambda e, tt=tt: e.transpose(
                        psRf[:, 0:8], logits[0:8, 128 * tt:128 * (tt + 1)], c_id8),
                        waits=[tok_lg, rd.get("psRf")], inc=sT)
                    t2 = P.op("vector", lambda e: e.tensor_copy(lgt, psRf[:, 0:8]),
                              waits=[t1, rd.get("lgt")], inc=sV)
                    rd["psRf"] = t2
                    rd["psA_rt"] = t2
                    if pend_oh:
                        emit_oh_pair(pend_oh.pop(0))
                    t3 = P.op("vector", lambda e: e.reduce_max(rsm[:, 0:1], lgt, AX),
                              waits=[t2], inc=sV)
                    t4 = P.op("vector", lambda e: e.tensor_scalar(rsm[:, 1:2], rsm[:, 0:1], -1.0, None, ALU.mult),
                              waits=[t3], inc=sV)
                    t5 = P.op("scalar", lambda e: e.activation(out=sce, in_=lgt, func=AF.Exp,
                                                               bias=rsm[:, 1:2], scale=1.0,
                                                               accum_out=rsm[:, 2:3]),
                              waits=[t4, rd.get("sce")], inc=sS)
                    t6 = P.op("vector", lambda e: e.reciprocal(rsm[:, 3:4], rsm[:, 2:3]),
                              waits=[t5], inc=sV)
                    t7 = P.op("vector", lambda e: e.tensor_scalar(sc, sce, rsm[:, 3:4], None, ALU.mult),
                              waits=[t6], inc=sV)
                    rd["sce"] = t7
                    t8 = P.op("vector", lambda e: e.max(out=top8, in_=sc), waits=[t7], inc=sV)
                    t9a = P.op("vector", lambda e, tt=tt: e.tensor_copy(w_sb[:, 2 * tt:2 * tt + 1], top8[:, 0:1]),
                               waits=[t8], inc=sV)
                    t9b = P.op("vector", lambda e, tt=tt: e.tensor_copy(w_sb[:, 2 * tt + 1:2 * tt + 2], top8[:, 1:2]),
                               waits=[t8], inc=sV)
                    wsb_toks.append(t9b)
                    ta0 = P.op("vector", lambda e: e.tensor_single_scalar(oh0, sc, top8[:, 0:1], ALU.is_equal),
                               waits=[t8, rd.get("oh0")], inc=sV)
                    ta1 = P.op("vector", lambda e: e.tensor_single_scalar(oh1, sc, top8[:, 1:2], ALU.is_equal),
                               waits=[t8, rd.get("oh1")], inc=sV)
                    rd["lgt"] = ta1
                    pend_oh.append((tt, ta0, ta1))

                wsb_toks = []
                for tc in range(4):
                    P.wait("tensor", tc_dep)
                    P.wait("scalar", tc_dep)
                    for ti in range(4):
                        tt = 4 * tc + ti
                        th, tl = ln_norm(lns, tt, x_sb[:, 1024 * tt:1024 * (tt + 1)],
                                         nh_tm[:, 1024 * tt:1024 * (tt + 1)],
                                         dst_f32=nf32, dst_lo=nl_t,
                                         extra_waits=[tc_dep, rd.get("nh_f")])
                        fm_v = fm.rearrange("p (a c) -> p a c", c=512)[:, :, 128 * ti:128 * (ti + 1)]
                        tok_fm = transpose8g(
                            lambda j, tt=tt: nh_tm[:, 1024 * tt + 128 * j: 1024 * tt + 128 * (j + 1)],
                            (fm_v[:, 0:4], fm_v[:, 4:8]), psTb, th, (("fT", 0), ("fT", 1)))
                        tok_nl = transpose8g(
                            lambda j: nl_t[:, 128 * j:128 * (j + 1)],
                            (nlfm[:, 0:512], nlfm[:, 512:1024]), psTb, tl, (("fT", 0), ("fT", 1)))
                        rd["nh_f"] = tok_nl
                        n_mm = 0
                        N_GATE_MM = 8 if GATE_ONEPASS else 24
                        passes = ((c_gh, lambda a2, fm=fm, ti=ti: fm[:, 512 * a2 + 128 * ti:512 * a2 + 128 * (ti + 1)]),
                                  (c_gl, lambda a2, fm=fm, ti=ti: fm[:, 512 * a2 + 128 * ti:512 * a2 + 128 * (ti + 1)]),
                                  (c_gh, lambda a2, nlfm=nlfm: nlfm[:, 128 * a2:128 * (a2 + 1)]))
                        if GATE_ONEPASS:
                            passes = passes[:1]
                        for gw, rhs_fn in passes:
                            for a2 in range(8):
                                P.op("tensor", lambda e, gw=gw, rhs=rhs_fn(a2), a2=a2, st=(n_mm == 0), sp=(n_mm == N_GATE_MM - 1):
                                     e.matmul(psG, gw[:, 8 * a2:8 * (a2 + 1)], rhs, start=st, stop=sp),
                                     waits=([tok_fm, tok_nl, sV.tok(), sL.tok(), rd.get("psG")] if n_mm == 0 else []),
                                     inc=sT if n_mm == N_GATE_MM - 1 else None)
                                n_mm += 1
                        tgl = P.op("scalar", lambda e, tt=tt: e.copy(
                            out=logits[0:8, 128 * tt:128 * (tt + 1)], in_=psG), waits=[sT.tok()], inc=sS)
                        rd["psG"] = tgl
                        emit_route_tile(tt, tgl)
                    tok_ab = None
                    tsa = None
                    for ib in range(16):
                        ring = ib % 2
                        P.dma("sync", sw_r[ring].rearrange("p (m c) -> p m c", m=2),
                              sw12_scr.transpose([1, 2, 0, 3])[ib], sw_sems[ring],
                              waits=[rd.get(("swr", ring))])
                        for wsel, ps in ((0, psA), (1, psB)):
                            for a2 in range(8):
                                P.op("tensor", lambda e, wsel=wsel, ps=ps, a2=a2, ring=ring, fm=fm: e.matmul(
                                    ps, sw_r[ring][:, wsel * 1024 + 128 * a2: wsel * 1024 + 128 * (a2 + 1)],
                                    fm[:, 512 * a2:512 * (a2 + 1)], start=(a2 == 0), stop=(a2 == 7)),
                                    waits=([sw_sems[ring].tok(), tok_fm, sS.tok(), rd.get(("psAB", wsel))]
                                           + ([rd.get("psA_rt")] if wsel == 0 else []) if a2 == 0 else []),
                                    inc=sT if a2 == 7 else None)
                            if wsel == 0:
                                ta = sT.tok()
                                tsa = P.op("scalar", lambda e: e.activation(
                                    out=a_sb, in_=psA, func=AF.Silu, bias=0.0, scale=1.0),
                                    waits=[ta, rd.get("a_sb")], inc=sS)
                                rd[("psAB", 0)] = tsa
                        tb = sT.tok()
                        tg = P.op("vector", lambda e, ib=ib, g_sb=g_sb: e.tensor_mul(
                            g_sb[:, 512 * ib:512 * (ib + 1)], a_sb, psB), waits=[tb, tsa], inc=sV)
                        rd[("psAB", 1)] = tg
                        rd["a_sb"] = tg
                        rd[("swr", ring)] = tg
                        tok_ab = tg
                    rd["psG"] = tok_ab

                    def emit_z(tc, tok_ab):
                        tzl = None
                        for zp in range(2):
                            for ic in range(16):
                                ring = ic % 2
                                P.dma("sync", sw3_r[ring], sw3_scr[ic], sw3_sems[ring],
                                      waits=[rd.get(("sw3r", ring))])
                                for tj in range(2):
                                    for dc in range(2):
                                        w0 = []
                                        if tj == 0 and dc == 0:
                                            w0 = [sw3_sems[ring].tok()]
                                            if ic == 0:
                                                w0 += [tok_ab, rd.get("psZ"), sS.tok()]
                                        last = P.op("tensor", lambda e, ic=ic, tj=tj, dc=dc, ring=ring, zp=zp, g_sb=g_sb: e.matmul(
                                            psZ2[tj][:, 512 * dc:512 * (dc + 1)],
                                            g_sb[:, 512 * ic + 128 * (2 * zp + tj): 512 * ic + 128 * (2 * zp + tj) + 128],
                                            sw3_r[ring][:, 512 * dc:512 * (dc + 1)],
                                            start=(ic == 0), stop=(ic == 15)),
                                            waits=w0,
                                            inc=sT if (tj == 1 and dc == 1) else None)
                                rd[("sw3r", ring)] = last
                            tz = sT.tok()
                            for tj in range(2):
                                tt = 4 * tc + 2 * zp + tj
                                for dc in range(2):
                                    tzl = P.op("vector", lambda e, tt=tt, tj=tj, dc=dc: e.tensor_add(
                                        x_sb[:, 1024 * tt + 512 * dc: 1024 * tt + 512 * (dc + 1)],
                                        x_sb[:, 1024 * tt + 512 * dc: 1024 * tt + 512 * (dc + 1)],
                                        psZ2[tj][:, 512 * dc:512 * (dc + 1)]), waits=[tz], inc=sV)
                            rd["psZ"] = tzl
                        return tzl

                    if tc < 3:
                        tc_dep = emit_z(tc, tok_ab)
                    else:
                        ab3_tok = tok_ab
                for job in pend_oh:
                    emit_oh_pair(job)
                pend_oh.clear()
                tsc = P.op("vector", lambda e: e.tensor_tensor_scan(incl, oh_fm, oh_fm, 0.0,
                                                                    ALU.add, ALU.bypass),
                           waits=[sV.tok(), sRt.tok()], inc=sV)
                tge = P.op("vector", lambda e: e.tensor_single_scalar(rrow, c_thresh,
                                                                     incl[:, J - 1:J], ALU.is_lt),
                           waits=[tsc, sL.tok()], inc=sV)
                tnt = P.op("vector", lambda e: e.reduce_sum(rsm[0:8, 4:5], rrow, AX),
                           waits=[tge], inc=sV)
                tt1 = P.op("tensor", lambda e: e.transpose(psRf[0:1, 0:8], rsm[0:8, 4:5],
                                                           c_id8), waits=[tnt, rd.get("psRf")], inc=sT)
                tt2 = P.op("vector", lambda e: e.tensor_copy(tsrow[0:1, 0:8], psRf[0:1, 0:8]),
                           waits=[tt1], inc=sV)
                rd["psRf"] = tt2
                tt3 = P.op("vector", lambda e: e.tensor_tensor_scan(tsrow[0:1, 8:16], tsrow[0:1, 0:8],
                                                                    tsrow[0:1, 0:8], 0.0, ALU.add, ALU.bypass),
                           waits=[tt2], inc=sV)
                tt4 = P.op("vector", lambda e: e.tensor_sub(tsrow[0:1, 0:8], tsrow[0:1, 8:16],
                                                            tsrow[0:1, 0:8]), waits=[tt3], inc=sV)
                tt5 = P.op("scalar", lambda e: e.copy(out=ts_i32, in_=tsrow[0:1, 0:8]),
                           waits=[tt4], inc=sS)
                tt6 = P.op("tensor", lambda e: e.transpose(psRf[0:8, 0:1], tsrow[0:1, 0:8],
                                                           c_id8[0:1, 0:1]), waits=[tt4, rd.get("psRf")], inc=sT)
                tt7 = P.op("vector", lambda e: e.tensor_copy(rsm[0:8, 5:6], psRf[0:8, 0:1]),
                           waits=[tt6], inc=sV)
                rd["psRf"] = tt7
                tt8 = P.op("vector", lambda e: e.tensor_single_scalar(rrow, c_gcmp, rsm[0:8, 5:6],
                                                                      ALU.is_ge), waits=[tt7, tge], inc=sV)
                tt9 = P.op("tensor", lambda e: e.matmul(psRf[0:1, 0:NSEG], c_ones8, rrow,
                                                        start=True, stop=True),
                           waits=[tt8, sL.tok(), rd.get("psRf")], inc=sT)
                tta = P.op("vector", lambda e: e.tensor_scalar(eof_f, psRf[0:1, 0:NSEG], -1.0, None,
                                                               ALU.add), waits=[tt9], inc=sV)
                rd["psRf"] = tta
                ttb = P.op("scalar", lambda e: e.copy(out=eof_i32, in_=eof_f), waits=[tta], inc=sS)
                ttc = P.op("vector", lambda e: e.tensor_scalar(rsm[0:8, 6:7], rsm[0:8, 5:6],
                                                               float(MOE_CHUNK), -1.0, ALU.mult, ALU.add),
                           waits=[tt7], inc=sV)
                ttd = P.op("vector", lambda e: e.scalar_tensor_tensor(incl, incl, rsm[0:8, 6:7],
                                                                      oh_fm, ALU.add, ALU.mult),
                           waits=[ttc, tsc, tge], inc=sV)
                rd["psIdx"] = rd.get("psRf")
                tprev = ttd
                idx_toks = []
                for d2 in range(32):
                    ti1 = P.op("tensor", lambda e, d2=d2: e.matmul(
                        psRf[:, 8 + d2:9 + d2], incl[0:8, 128 * d2:128 * (d2 + 1)], c_ones8,
                        start=True, stop=True), waits=[tprev, rd.get("psIdx")], inc=sT)
                    tix = P.op("scalar", lambda e, d2=d2: e.copy(out=idx_sb[:, d2:d2 + 1],
                                                                 in_=psRf[:, 8 + d2:9 + d2]),
                               waits=[ti1], inc=sS)
                    rd["psIdx"] = tix
                    idx_toks.append(tix)
                tok_idx = sS.tok()
                if DEBUG_DUMP:
                    for nm2, ap2 in (("ts%d" % ridx, ts_i32), ("eof%d" % ridx, eof_i32),
                                     ("idx%d" % ridx, idx_sb), ("lgr%d" % ridx, logits)):
                        dd = nc.dram_tensor("dbg_" + nm2, list(ap2.shape), ap2.dtype,
                                            kind="ExternalOutput").ap()
                        P.dma("sync", dd, ap2, sO, waits=[sV.tok(), sS.tok(), sT.tok()])
                txs = None
                for k2 in range(2):
                    for tt in range(NTILE):
                        ring = (k2 * NTILE + tt) % 2
                        txs = P.op("vector", lambda e, k2=k2, tt=tt, ring=ring: e.tensor_scalar(
                            xs_pp[ring], nh_tm[:, 1024 * tt:1024 * (tt + 1)],
                            w_sb[:, 2 * tt + k2:2 * tt + k2 + 1], None, ALU.mult),
                            waits=[sGs.tok(), wsb_toks[tt]], inc=sV)
                        P.op("gpsimd", lambda e, k2=k2, tt=tt, ring=ring: e.indirect_dma_start(
                            out=xs_scr, out_offset=IndirectOffsetOnAxis(idx_sb[:, 16 * k2 + tt:16 * k2 + tt + 1], 0),
                            in_=xs_pp[ring], in_offset=None),
                            waits=[txs, idx_toks[16 * k2 + tt]], inc=sGs, inc_k=16)
                tok_scat = sGs.tok()
                txs_last = txs
                soff = pool.marks[1]

                def slot_alloc(off, shape, dtype, name):
                    pool.uid += 1
                    return nc.alloc_sbuf_tensor_at(f"pl{pool.uid}_{name}", list(shape), dtype,
                                                   offset=pool.base + off).ap()
                w1s = slot_alloc(soff, [128, 8192], BF16, "w1s")
                w3s = slot_alloc(soff + 16384, [128, 8192], BF16, "w3s")
                w2s = slot_alloc(soff + 32768, [128, 8192], BF16, "w2s")
                bst = slot_alloc(soff + 49152, [128, 24], F32, "bst")
                cbase, vbase = NSEG * ridx, NSEG * ridx
                w1base, w3base, w2base, wbbase = 224 * ridx, 224 * ridx, 224 * ridx, 112 * ridx
                emit_z(3, ab3_tok)
                e0_waits = [txs_last, tok_idx, (sC, cbase), (sCv, vbase), (sCsc, cbase),
                            (sC1, cbase), (sC3, cbase)]
                for kind, dst, wsem in ((0, w1s, sE1), (1, w3s, sE3), (2, w2s, sE2)):
                    for h in range(2):
                        P.dma("sync", dst[:, 4096 * h:4096 * (h + 1)], wmoe[0, kind, h],
                              wsem, waits=e0_waits)
                P.dma("sync", bst, dr["bstage"][0], sEb, waits=e0_waits)
            pool.release()
            pool.release()

            if "D" not in FFN_PHASES:
                pool.release()
                return
            fence()
            pool.mark()
            pool.off += 49248
            pool.hiwater = max(pool.hiwater, pool.off)
            xg = [pool.alloc([128, 2 * 1024], BF16, "xg0"), pool.alloc([128, 2 * 1024], BF16, "xg1")]
            xg_fm = [pool.alloc([128, 8 * 256], BF16, "xgf0"), pool.alloc([128, 8 * 256], BF16, "xgf1")]
            h_fm = pool.alloc([128, 8 * 256], BF16, "h_fm")
            x3_fm = pool.alloc([128, 8 * 256], F32, "x3_fm")
            y_fm = pool.alloc([128, 8 * 256], BF16, "y_fm")
            y_tm = pool.alloc([128, 2 * 1024], BF16, "y_tm")

            def loader(e, ridx=ridx):
                e.wait_ge(sE1.h, 32 * (ridx + 1))
                e.wait_ge(sE3.h, 32 * (ridx + 1))
                e.wait_ge(sE2.h, 32 * (ridx + 1))
                e.wait_ge(sEb.h, 16 * (ridx + 1))
                with e.register(f"ldr{ridx}") as r1, e.register(f"ldr2_{ridx}") as r2:
                    for ei in range(1, E):
                        if not MOE_STATIC_LOADER:
                            e.reg_load(r1, ts_i32[0:1, ei:ei + 1])
                            e.reg_alu(r2, r1, cbase, ALU.add)
                            e.wait_ge(sC1.h, r2)
                        for h in range(2):
                            e.dma_start(out=w1s[:, 4096 * h:4096 * (h + 1)],
                                        in_=wmoe[ei, 0, h]).then_inc(sW1.h, 16)
                        if not MOE_STATIC_LOADER:
                            e.reg_alu(r2, r1, vbase, ALU.add)
                            e.wait_ge(sCv.h, r2)
                            e.reg_alu(r2, r1, cbase, ALU.add)
                            e.wait_ge(sCsc.h, r2)
                        e.dma_start(out=bst, in_=dr["bstage"][ei]).then_inc(sWb.h, 16)
                        if not MOE_STATIC_LOADER:
                            e.wait_ge(sC3.h, r2)
                        for h in range(2):
                            e.dma_start(out=w3s[:, 4096 * h:4096 * (h + 1)],
                                        in_=wmoe[ei, 1, h]).then_inc(sW3.h, 16)
                        if not MOE_STATIC_LOADER:
                            e.wait_ge(sC.h, r2)
                        for h in range(2):
                            e.dma_start(out=w2s[:, 4096 * h:4096 * (h + 1)],
                                        in_=wmoe[ei, 2, h]).then_inc(sW2.h, 16)
            P.wait("gpsimd", (sems["sS"], sems["sS"].n))
            if not MOE_NO_LOADER:
                P.raw("gpsimd", loader)
            sW1.bump(224)
            sW3.bump(224)
            sW2.bump(224)
            sWb.bump(112)

            pcd = ExitStack()
            psTb = [mkps(pcd, "eT", [128, 512], F32) for _ in range(2)]
            psP = [mkps(pcd, f"eP{i2}", [128, 256], F32) for i2 in range(4)]
            for g in (range(NSEG) if MOE_D_LEVEL >= 0 else []):
                pp = g % 2
                for half in (range(2) if not MOE_NO_GATHER else ()):
                    P.dma("sync", xg[pp][:, 1024 * half:1024 * (half + 1)],
                          xs_scr[MOE_CHUNK * g + 128 * half: MOE_CHUNK * g + 128 * (half + 1), :],
                          sX, waits=[tok_scat, rd.get(("xg", pp))])
                tok_xg = sX.tok()

                def mk_wwait(tag, wsem, esem, base2, mul, g=g, ridx=ridx):
                    def wfn(e, tag=tag, wsem=wsem, esem=esem, base2=base2, mul=mul):
                        e.wait_ge(esem.h, mul * (ridx + 1))
                        if MOE_STATIC_TWAIT:
                            e.wait_ge(wsem.h, base2 + mul * (E - 1))
                            return
                        with e.register(f"tw{tag}_{ridx}_{g}") as r1, \
                                e.register(f"tw{tag}b_{ridx}_{g}") as r2:
                            e.reg_load(r1, eof_i32[0:1, g:g + 1])
                            e.reg_alu(r2, r1, mul, ALU.mult)
                            e.reg_alu(r2, r2, base2, ALU.add)
                            e.wait_ge(wsem.h, r2)
                    return wfn
                if not MOE_NO_WAITS:
                    P.wait("tensor", tok_idx)
                P.wait("vector", rd.get(("xgf", pp)))
                if True:
                    if MOE_D_LEVEL >= 1:
                        xgv = xg_fm[pp].rearrange("p (a c) -> p a c", c=256)
                        transpose8g(lambda j, pp=pp: xg[pp][:, 128 * j:128 * (j + 1)],
                                    (xgv[0:128, 0:4, 0:128], xgv[0:128, 4:8, 0:128]),
                                    psTb, tok_xg, (("eT", 0), ("eT", 1)))
                        tok_tr = transpose8g(lambda j, pp=pp: xg[pp][:, 1024 + 128 * j:1024 + 128 * (j + 1)],
                                             (xgv[0:128, 0:4, 128:256], xgv[0:128, 4:8, 128:256]),
                                             psTb, tok_xg, (("eT", 0), ("eT", 1)))
                        rd[("xg", pp)] = tok_tr
                    else:
                        tok_tr = tok_xg
                    for wsl, dst, kindb, is_h in (((w1s, h_fm, 0, True), (w3s, x3_fm, 8, False)) if MOE_D_LEVEL >= 2 else ()):
                        if not MOE_NO_WAITS:
                            if is_h:
                                P.raw("tensor", mk_wwait("1", sW1, sE1, w1base, 32))
                                P.raw("tensor", mk_wwait("b", sWb, sEb, wbbase, 16))
                            else:
                                P.raw("tensor", mk_wwait("3", sW3, sE3, w3base, 32))
                        for hb in range(8):
                            pps = hb % 4
                            for a2 in range(8):
                                P.op("tensor", lambda e, wsl=wsl, hb=hb, a2=a2, pps=pps, pp=pp: e.matmul(
                                    psP[pps], wsl[:, a2 * 1024 + 128 * hb: a2 * 1024 + 128 * (hb + 1)],
                                    xg_fm[pp][:, 256 * a2:256 * (a2 + 1)], start=(a2 == 0), stop=(a2 == 7)),
                                    waits=([tok_tr, rd.get(("psP", pps))] if a2 == 0 else []),
                                    inc=sT if a2 == 7 else None)
                            tmm = sT.tok()
                            if is_h:
                                tev = P.op("scalar", lambda e, hb=hb, pps=pps, kindb=kindb: e.activation(
                                    out=h_fm[:, 256 * hb:256 * (hb + 1)], in_=psP[pps], func=AF.Silu,
                                    bias=bst[:, kindb + hb:kindb + hb + 1], scale=1.0),
                                    waits=[tmm, rd.get("h_fm")], inc=sS)
                            else:
                                tev = P.op("vector", lambda e, hb=hb, pps=pps, kindb=kindb: e.tensor_scalar(
                                    x3_fm[:, 256 * hb:256 * (hb + 1)], psP[pps],
                                    bst[:, kindb + hb:kindb + hb + 1], None, ALU.add),
                                    waits=[tmm, rd.get("x3_fm")], inc=sV)
                            rd[("psP", pps)] = tev
                        if not MOE_NO_NOPS:
                            P.op("tensor", lambda e: e.nop(), inc=sC1 if is_h else sC3)
                        else:
                            (sC1 if is_h else sC3).bump(1)
                    tok_h = (sS.tok(), sV.tok())
                    rd[("xgf", pp)] = sT.tok()
                    if not MOE_NO_WAITS and MOE_D_LEVEL >= 3:
                        P.raw("tensor", mk_wwait("2", sW2, sE2, w2base, 32))
                    for db in (range(8) if MOE_D_LEVEL >= 3 else ()):
                        pps = db % 4
                        for hc in range(8):
                            P.op("tensor", lambda e, db=db, hc=hc, pps=pps: e.matmul(
                                psP[pps], w2s[:, hc * 1024 + 128 * db: hc * 1024 + 128 * (db + 1)],
                                h_fm[:, 256 * hc:256 * (hc + 1)], start=(hc == 0), stop=(hc == 7)),
                                waits=([tok_h[0], tok_h[1], rd.get(("psP", pps))] if hc == 0 else []),
                                inc=sT if hc == 7 else None)
                        tmm = sT.tok()
                        tev = P.op("vector", lambda e, db=db, pps=pps: e.scalar_tensor_tensor(
                            y_fm[:, 256 * db:256 * (db + 1)], psP[pps], bst[:, 16 + db:16 + db + 1],
                            x3_fm[:, 256 * db:256 * (db + 1)], ALU.add, ALU.mult),
                            waits=[tmm, rd.get("y_fm")], inc=sV)
                        rd[("psP", pps)] = tev
                    rd["h_fm"] = sT.tok()
                    rd["x3_fm"] = sV.tok()
                    tok_y = sV.tok()
                    if not MOE_NO_NOPS:
                        P.op("tensor", lambda e: e.nop(), inc=sC)
                    else:
                        sC.bump(1)
                    ytoks = [None, None]
                    if MOE_D_LEVEL >= 4:
                        for rr in range(2):
                            ytoks[rr] = transpose8g(
                                lambda j, rr=rr: y_fm[:, 256 * j + 128 * rr: 256 * j + 128 * (rr + 1)],
                                (y_tm[:, 1024 * rr:1024 * rr + 512], y_tm[:, 1024 * rr + 512:1024 * (rr + 1)]),
                                psTb, tok_y, (("eT", 0), ("eT", 1)))
                    rd["y_fm"] = sT.tok()
                    for half in (range(2) if MOE_D_LEVEL >= 4 else ()):
                        P.dma("sync", y_scr[MOE_CHUNK * g + 128 * half: MOE_CHUNK * g + 128 * (half + 1), :],
                              y_tm[:, 1024 * half:1024 * (half + 1)], sY, waits=[ytoks[half]])
                    rd["y_tm"] = sY.tok()
                    rd[("xg", pp)] = rd.get(("xg", pp))
                    if not MOE_NO_NOPS:
                        P.op("vector", lambda e: e.nop(), waits=[tok_y], inc=sCv)
                        P.op("scalar", lambda e: e.nop(), waits=[tok_h[0]], inc=sCsc)
            pcd.close()
            tok_wb = sY.tok()

            if "E" not in FFN_PHASES:
                pool.release()
                pool.release()
                return
            yg = [pool.alloc([128, 1024], BF16, "yg0"), pool.alloc([128, 1024], BF16, "yg1")]
            for tt in range(NTILE):
                for k2 in range(2):
                    P.op("gpsimd", lambda e, tt=tt, k2=k2: e.indirect_dma_start(
                        out=yg[k2], out_offset=None, in_=y_scr,
                        in_offset=IndirectOffsetOnAxis(idx_sb[:, 16 * k2 + tt:16 * k2 + tt + 1], 0)),
                        waits=[tok_wb, tok_idx, rd.get(("yg", k2))], inc=sGs, inc_k=16)
                tok_g = sGs.tok()
                ty1 = P.op("vector", lambda e, tt=tt: e.tensor_add(
                    x_sb[:, 1024 * tt:1024 * (tt + 1)], x_sb[:, 1024 * tt:1024 * (tt + 1)], yg[0]),
                    waits=[tok_g], inc=sV)
                ty2 = P.op("vector", lambda e, tt=tt: e.tensor_add(
                    x_sb[:, 1024 * tt:1024 * (tt + 1)], x_sb[:, 1024 * tt:1024 * (tt + 1)], yg[1]),
                    waits=[ty1], inc=sV)
                rd[("yg", 0)] = ty2
                rd[("yg", 1)] = ty2
            pool.release()
            pool.release()


        def sublayer_attn_b(b):
            fence()
            pool.mark()
            r_sb = pool.alloc([128, 2 * T], BF16, "r_sb")
            qp_sb = pool.alloc([128, 8 * T], BF16, "qp_sb")
            os_sb = pool.alloc([128, 8 * T], BF16, "os_sb")
            ctx_sb = pool.alloc([128, 8 * 65], BF16, "ctx_sb")

            pool.mark()
            wdb_sb = pool.alloc([128, 2048], BF16, "wdb")
            wqb_sb = pool.alloc([128, 8192], BF16, "wqb")
            fm2 = [pool.alloc([128, 8 * 512], BF16, "bfmA"), pool.alloc([128, 8 * 512], BF16, "bfmB")]
            nh4 = pool.alloc([128, 2048], BF16, "bnh4")
            esc = [pool.alloc([128, 512], F32, "esc0"), pool.alloc([128, 512], F32, "esc1"),
                   pool.alloc([128, 512], F32, "esc2")]
            lns = mk_ln(pool)
            for tt in range(NTILE):
                ln_stats_tile(lns, tt, x_sb[:, 1024 * tt:1024 * (tt + 1)],
                              extra_waits=[sX.tok()] if tt == 0 else ())
            ln_finish(lns)
            P.dma("sync", wdb_sb, wdb_c, sL, waits=[tok_attn["wdb"]])
            for h in range(2):
                P.dma("sync", wqb_sb[:, h * 4096:(h + 1) * 4096], wat_c["wqb"][h], sL,
                      waits=[tok_attn["wqb"]])
            tok_w = sL.tok()
            with ExitStack() as pc1:
                psTb = [mkps(pc1, "bT", [128, 512], F32) for _ in range(2)]
                psD = mkps(pc1, "bD", [128, 512], F32)
                psQb = [mkps(pc1, f"bQ{i3}", [128, 512], F32) for i3 in range(3)]

                def emit_trs_b(tc):
                    par = tc % 2
                    fm = fm2[par]
                    P.wait("vector", rd.get(("fmwb", par)))
                    tok_fm = None
                    for ti in range(4):
                        tt = 4 * tc + ti
                        np_ = ti % 2
                        th, _ = ln_norm(lns, tt, x_sb[:, 1024 * tt:1024 * (tt + 1)],
                                        nh4[:, 1024 * np_:1024 * (np_ + 1)],
                                        extra_waits=[rd.get(("bnh4", np_))])
                        fm_v = fm.rearrange("p (a c) -> p a c", c=512)[:, :, 128 * ti:128 * (ti + 1)]
                        tok_fm = transpose8g(
                            lambda j, np_=np_, nh4=nh4: nh4[:, 1024 * np_ + 128 * j: 1024 * np_ + 128 * (j + 1)],
                            (fm_v[:, 0:4], fm_v[:, 4:8]), psTb, th, (("bT", 0), ("bT", 1)))
                        rd[("bnh4", np_)] = sT.tok()
                    return tok_fm

                def emit_mms_b(tc, tok_fm):
                    par = tc % 2
                    fm = fm2[par]
                    for rb in range(2):
                        for a in range(8):
                            P.op("tensor", lambda e, rb=rb, a=a, fm=fm: e.matmul(
                                psD, wdb_sb[:, a * 256 + 128 * rb: a * 256 + 128 * (rb + 1)],
                                fm[:, 512 * a:512 * (a + 1)], start=(a == 0), stop=(a == 7)),
                                waits=([tok_fm, tok_w, rd.get("psDb")] if a == 0 else []),
                                inc=sT if a == 7 else None)
                        tmm = sT.tok()
                        tev = P.op("vector", lambda e, rb=rb, tc=tc: e.tensor_scalar(
                            r_sb[:, rb * T + 512 * tc: rb * T + 512 * (tc + 1)], psD,
                            c_battn[:, 24 + rb:25 + rb], None, ALU.add),
                            waits=[tmm, sL.tok()], inc=sV)
                        rd["psDb"] = tev
                    for ob in range(8):
                        ps = psQb[ob % 3]
                        for a in range(8):
                            P.op("tensor", lambda e, ob=ob, a=a, ps=ps, fm=fm: e.matmul(
                                ps, wqb_sb[:, a * 1024 + 128 * ob: a * 1024 + 128 * (ob + 1)],
                                fm[:, 512 * a:512 * (a + 1)], start=(a == 0), stop=(a == 7)),
                                waits=([tok_fm, tok_w, rd.get(("psQb", ob % 3))] if a == 0 else []),
                                inc=sT if a == 7 else None)
                        tmm = sT.tok()
                        t1 = P.op("vector", lambda e, ps=ps, ob=ob: e.tensor_scalar(
                            esc[0], ps, c_battn[:, 16 + ob:17 + ob], 0.0, ALU.add, ALU.min),
                            waits=[tmm, sL.tok(), rd.get("esc")], inc=sV)
                        t2 = P.op("vector", lambda e, ps=ps, ob=ob: e.tensor_scalar(
                            esc[1], ps, c_battn[:, 16 + ob:17 + ob], 0.0, ALU.add, ALU.max),
                            waits=[tmm, t1], inc=sV)
                        rd[("psQb", ob % 3)] = t2
                        t3 = P.op("scalar", lambda e: e.activation(out=esc[2], in_=esc[0],
                                                                   func=AF.Exp, bias=0.0, scale=1.0),
                                  waits=[t1, rd.get("esc2")], inc=sS)
                        t4 = P.op("vector", lambda e, ob=ob, tc=tc: e.tensor_add(
                            qp_sb[:, ob * T + 512 * tc: ob * T + 512 * (tc + 1)], esc[2], esc[1]),
                            waits=[t3, t2], inc=sV)
                        rd["esc"] = t4
                        rd["esc2"] = t4
                    rd[("fmwb", par)] = sT.tok()

                toks_b = {}
                for tc in range(4):
                    toks_b[tc] = emit_trs_b(tc)
                    if tc > 0:
                        emit_mms_b(tc - 1, toks_b[tc - 1])
                emit_mms_b(3, toks_b[3])
            pool.release()

            pool.mark()
            wub_sb = pool.alloc([128, 4096], BF16, "wub")
            kext = pool.alloc([128, 1040], BF16, "kext")
            vext = pool.alloc([128, 1040], BF16, "vext")
            ksc = [pool.alloc([128, 256], F32, "ksc0"), pool.alloc([128, 256], F32, "ksc1"),
                   pool.alloc([128, 256], F32, "ksc2")]
            P.dma("sync", wub_sb, wub_c, sL,
                  waits=[tok_attn["wub"], sT.tok(), sV.tok(), sS.tok()])
            tok_w = sL.tok()
            with ExitStack() as pc2:
                psKV = [mkps(pc2, "bKV0", [128, 512], F32), mkps(pc2, "bKV1", [128, 512], F32)]
                psCtx = mkps(pc2, "bCtx", [65, 2048], F32)
                kext4 = kext.rearrange("p (h j) -> p h j", j=65)
                vext4 = vext.rearrange("p (h j) -> p h j", j=65)
                tko = P.op("vector", lambda e: e.memset(kext4[:, :, 64:65], 1.0), inc=sV)
                tvo = P.op("vector", lambda e: e.memset(vext4[:, :, 64:65], 1.0), waits=[tko], inc=sV)
                rd["kvone"] = tvo
                for tt in range(NTILE):
                    for oc in range(4):
                        ps = psKV[oc % 2]
                        for ch in range(2):
                            P.op("tensor", lambda e, tt=tt, oc=oc, ch=ch, ps=ps: e.matmul(
                                ps, r_sb[:, ch * T + 128 * tt: ch * T + 128 * (tt + 1)],
                                wub_sb[:, ch * 2048 + 512 * oc: ch * 2048 + 512 * (oc + 1)],
                                start=(ch == 0), stop=False),
                                waits=([sV.tok(), tok_w, rd.get(("psKV", oc % 2))] if ch == 0 else []))
                        P.op("tensor", lambda e, oc=oc, ps=ps: e.matmul(
                            ps, c_onesb[32:33, :], c_rows[32:33, 512 * oc:512 * (oc + 1)],
                            start=False, stop=True), inc=sT)
                        tmm = sT.tok()
                        pk = bass.AP(ps.tensor, ps.offset, [ps.ap[0], [128, 4], [1, 64]])
                        pv = bass.AP(ps.tensor, ps.offset + 64, [ps.ap[0], [128, 4], [1, 64]])
                        k1 = P.op("vector", lambda e, pk=pk: e.tensor_scalar(
                            ksc[0].rearrange("p (h j) -> p h j", j=64), pk, 0.0, None, ALU.min),
                            waits=[tmm, rd.get("ksc")], inc=sV)
                        k2 = P.op("vector", lambda e, pk=pk: e.tensor_scalar(
                            ksc[1].rearrange("p (h j) -> p h j", j=64), pk, 0.0, None, ALU.max),
                            waits=[tmm, k1], inc=sV)
                        rd[("psKV", oc % 2)] = k2
                        k3 = P.op("scalar", lambda e: e.activation(out=ksc[2], in_=ksc[0],
                                                                   func=AF.Exp, bias=0.0, scale=1.0),
                                  waits=[k1, rd.get("ksc2")], inc=sS)
                        k4 = P.op("vector", lambda e, oc=oc: e.tensor_add(
                            kext4[:, 4 * oc:4 * (oc + 1), 0:64],
                            ksc[2].rearrange("p (h j) -> p h j", j=64),
                            ksc[1].rearrange("p (h j) -> p h j", j=64)),
                            waits=[k3, k2, tvo, rd.get("kext")], inc=sV)
                        kv = P.op("vector", lambda e, pv=pv, oc=oc: e.tensor_copy(
                            vext4[:, 4 * oc:4 * (oc + 1), 0:64], pv),
                            waits=[tmm, tvo, rd.get("kext")], inc=sV)
                        rd["ksc"] = k4
                        rd["ksc2"] = k4
                        rd[("kvoc", oc)] = (k4, kv)
                    toks = [rd[("kvoc", o2)] for o2 in range(4)]
                    tok_kv = [t for pair in toks for t in pair]
                    for h in range(NH):
                        P.op("tensor", lambda e, h=h, tt=tt: e.matmul(
                            psCtx[:, h * 128:h * 128 + 65], kext[:, h * 65:(h + 1) * 65],
                            vext[:, h * 65:(h + 1) * 65],
                            start=(tt == 0 and h % 4 == 0), stop=(tt == NTILE - 1 and h % 4 == 3)),
                            waits=(tok_kv if h == 0 else []), inc=sT if h == NH - 1 else None)
                    rd["kext"] = sT.tok()
                tok_ctx_mm = sT.tok()
                for h in range(NH):
                    tev = P.op("vector", lambda e, h=h: e.tensor_copy(
                        ctx_sb[64 * (h % 2):64 * (h % 2) + 64, (h // 2) * 65:(h // 2) * 65 + 65],
                        psCtx[0:64, h * 128:h * 128 + 65]), waits=[tok_ctx_mm], inc=sV)
                tok_ctx = sV.tok()
            pool.release()

            pool.mark()
            rlr = [pool.alloc([1, 512], F32, "brl0"), pool.alloc([1, 512], F32, "brl1")]
            rlbc = [pool.alloc([64, 512], F32, "brlbc0"), pool.alloc([64, 512], F32, "brlbc1")]
            with ExitStack() as pc3:
                psO = [mkps(pc3, f"bO{i3}", [65, 512], F32) for i3 in range(4)]
                psBc2 = [mkps(pc3, "bBcA", [64, 512], F32), mkps(pc3, "bBcB", [64, 512], F32)]

                def emit_bnormA(job):
                    tok_mm, h2, tc2, poi, ri = job
                    tr1 = P.op("vector", lambda e, poi=poi, ri=ri: e.tensor_scalar(
                        rlr[ri], psO[poi][64:65, :], 1e-6, None, ALU.add),
                        waits=[tok_mm, rd.get(("brlr", ri))], inc=sV)
                    tr2 = P.op("vector", lambda e, ri=ri: e.reciprocal(rlr[ri], rlr[ri]),
                               waits=[tr1], inc=sV)
                    return tr2

                def emit_bnormB(job, tr2):
                    tok_mm, h2, tc2, poi, ri = job
                    base2 = 64 * (h2 % 2)
                    tb1 = P.op("tensor", lambda e, ri=ri: e.matmul(psBc2[ri], c_ones[0:1, 0:64],
                                                                   rlr[ri], start=True, stop=True),
                               waits=[tr2, rd.get(("psBcb", ri))], inc=sT)
                    rd[("brlr", ri)] = tb1
                    tb2 = P.op("scalar", lambda e, ri=ri: e.copy(out=rlbc[ri], in_=psBc2[ri]),
                               waits=[tb1, rd.get(("brlbc", ri))], inc=sS)
                    rd[("psBcb", ri)] = tb2
                    ty = P.op("vector", lambda e, h=h2, tc=tc2, base=base2, poi=poi, ri=ri: e.tensor_mul(
                        os_sb[base:base + 64, (h // 2) * T + 512 * tc: (h // 2) * T + 512 * (tc + 1)],
                        psO[poi][0:64, :], rlbc[ri]), waits=[tb2], inc=sV)
                    rd[("psOb", poi)] = ty
                    rd[("brlbc", ri)] = ty

                pendA, pendB = [], []
                ctr3 = 0
                for h in range(NH):
                    base = 64 * (h % 2)
                    for tc in range(4):
                        poi, ri = ctr3 % 4, ctr3 % 2
                        tmm = P.op("tensor", lambda e, h=h, tc=tc, base=base, poi=poi: e.matmul(
                            psO[poi], ctx_sb[base:base + 64, (h // 2) * 65:(h // 2) * 65 + 65],
                            qp_sb[base:base + 64, (h // 2) * T + 512 * tc: (h // 2) * T + 512 * (tc + 1)],
                            start=True, stop=True, tile_position=(base, 0)),
                            waits=[tok_ctx, rd.get(("psOb", poi))], inc=sT)
                        pendA.append((tmm, h, tc, poi, ri))
                        ctr3 += 1
                        if len(pendA) > 1:
                            jobA = pendA.pop(0)
                            pendB.append((jobA, emit_bnormA(jobA)))
                        if len(pendB) > 1:
                            jb, tr2b = pendB.pop(0)
                            emit_bnormB(jb, tr2b)
                for jobA in pendA:
                    pendB.append((jobA, emit_bnormA(jobA)))
                for jb, tr2b in pendB:
                    emit_bnormB(jb, tr2b)
            pool.release()

            pool.mark()
            wob_sb = pool.alloc([128, 8192], BF16, "wob")
            for h in range(2):
                P.dma("sync", wob_sb[:, h * 4096:(h + 1) * 4096], wat_c["wob"][h], sL,
                      waits=[tok_attn["wob"], sT.tok(), sV.tok(), sS.tok()])
            tok_w = sL.tok()
            with ExitStack() as pc4:
                psW = [mkps(pc4, "bW0", [128, 512], F32), mkps(pc4, "bW1", [128, 512], F32)]
                x_war = sV.tok()
                for tt in range(NTILE):
                    for oc in range(2):
                        pw_i = (2 * tt + oc) % 2
                        psw = psW[pw_i]
                        for ch in range(8):
                            P.op("tensor", lambda e, tt=tt, oc=oc, ch=ch, psw=psw: e.matmul(
                                psw, os_sb[:, ch * T + 128 * tt: ch * T + 128 * (tt + 1)],
                                wob_sb[:, ch * 1024 + 512 * oc: ch * 1024 + 512 * (oc + 1)],
                                start=(ch == 0), stop=False),
                                waits=([sV.tok(), tok_w, rd.get(("psWb", pw_i))] if ch == 0 else []))
                        P.op("tensor", lambda e, oc=oc, psw=psw: e.matmul(
                            psw, c_onesb[64:65, :], c_rows[64:65, 512 * oc:512 * (oc + 1)],
                            start=False, stop=True), inc=sT)
                        tmm = sT.tok()
                        tx = P.op("vector", lambda e, tt=tt, oc=oc, psw=psw: e.tensor_add(
                            x_sb[:, 1024 * tt + 512 * oc: 1024 * tt + 512 * (oc + 1)],
                            x_sb[:, 1024 * tt + 512 * oc: 1024 * tt + 512 * (oc + 1)], psw),
                            waits=[tmm], inc=sV)
                        rd[("psWb", pw_i)] = tx
            pool.release()
            pool.release()

        rctr = [0]
        for b in range(B_RUN):
            if b > 0:
                P.dma("sync", x_sb3, x_r[b], sX,
                      waits=[rd.get("x_stored"), sV.tok()])
            if N_SUBLAYERS >= 1:
                sublayer_attn_a(b, b == 0)
            if N_SUBLAYERS >= 2:
                sublayer_ffn(b, rctr[0]); rctr[0] += 1
            if N_SUBLAYERS >= 3:
                sublayer_attn_b(b)
            if N_SUBLAYERS >= 4:
                sublayer_ffn(b, rctr[0]); rctr[0] += 1
            tst = P.dma("sync", out_r[b], x_sb3, sO, waits=[sV.tok(), sT.tok(), sS.tok()])
            rd["x_stored"] = tst
        if DEBUG_DUMP:
            for nm2, ap2 in list(DBG.items()):
                dt2 = ap2.dtype
                dd = nc.dram_tensor("dbg_" + nm2, list(ap2.shape), dt2,
                                    kind="ExternalOutput").ap()
                P.dma("sync", dd, ap2, sO, waits=[sV.tok(), sT.tok(), sS.tok()])
        P.wait("sync", sO.tok())

        with nc.Block() as block:
            @block.tensor
            def _(e):
                for f in P.streams["tensor"]:
                    f(e)

            @block.vector
            def _(e):
                for f in P.streams["vector"]:
                    f(e)

            @block.scalar
            def _(e):
                for f in P.streams["scalar"]:
                    f(e)

            @block.gpsimd
            def _(e):
                for f in P.streams["gpsimd"]:
                    f(e)

            @block.sync
            def _(e):
                for f in P.streams["sync"]:
                    f(e)

    import sys
    if "--semdump" in sys.argv or True:
        try:
            print("SEMS:", {nm: sems[nm].n for nm in sems}, file=sys.stderr)
        except Exception:
            pass
    return nc


def _host_inputs(ins, core):
    d = {}
    d["x_in"] = np.ascontiguousarray(np.asarray(ins["L0_attn_a_x"])[2 * core:2 * core + 2])

    def T_(a):
        return np.ascontiguousarray(np.asarray(a).T)

    perm = np.concatenate([np.arange(0, D, 2), np.arange(1, D, 2)])
    d["wqaTp"] = np.ascontiguousarray(np.asarray(ins["L0_attn_a_Wq"]).T[:, perm])
    d["wkaTp"] = np.ascontiguousarray(np.asarray(ins["L0_attn_a_Wk"]).T[:, perm])
    d["wvaT"] = T_(ins["L0_attn_a_Wv"])
    d["woaT"] = T_(ins["L0_attn_a_Wo"])
    bq = np.asarray(ins["L0_attn_a_bq"])[perm]
    bk = np.asarray(ins["L0_attn_a_bk"])[perm]
    d["bqe_s"] = np.ascontiguousarray(bq[:512].reshape(4, 128).T)
    d["bqo_s"] = np.ascontiguousarray(bq[512:].reshape(4, 128).T)
    d["bke_s"] = np.ascontiguousarray(bk[:512].reshape(4, 128).T)
    d["bko_s"] = np.ascontiguousarray(bk[512:].reshape(4, 128).T)
    d["bv_row"] = np.asarray(ins["L0_attn_a_bv"]).reshape(1, D)
    d["boa_row"] = np.asarray(ins["L0_attn_a_bo"]).reshape(1, D)
    d["wqb"] = np.ascontiguousarray(np.asarray(ins["L0_attn_b_Wq"]))
    d["wdb"] = np.ascontiguousarray(np.asarray(ins["L0_attn_b_Wd"]))
    d["wub"] = np.ascontiguousarray(np.asarray(ins["L0_attn_b_Wu"]))
    d["wob"] = np.ascontiguousarray(np.asarray(ins["L0_attn_b_Wo"]))
    d["bq2_s"] = np.ascontiguousarray(np.asarray(ins["L0_attn_b_bq"]).reshape(8, 128).T)
    d["bd_s"] = np.ascontiguousarray(np.asarray(ins["L0_attn_b_bd"]).reshape(2, 128).T)
    d["bu_row"] = np.asarray(ins["L0_attn_b_bu"]).reshape(1, 2048)
    d["bob_row"] = np.asarray(ins["L0_attn_b_bo"]).reshape(1, D)
    d["gwT"] = T_(ins["L0_ffn_gate_w"])
    d["w1T"] = np.ascontiguousarray(np.asarray(ins["L0_ffn_w1"]).transpose(0, 2, 1))
    d["w2T"] = np.ascontiguousarray(np.asarray(ins["L0_ffn_w2"]).transpose(0, 2, 1))
    d["w3T"] = np.ascontiguousarray(np.asarray(ins["L0_ffn_w3"]).transpose(0, 2, 1))
    bst = np.zeros((E, 128, 24), np.float32)
    for kind, nm in ((0, "L0_ffn_b1"), (1, "L0_ffn_b3"), (2, "L0_ffn_b2")):
        bb = np.asarray(ins[nm])
        bst[:, :, kind * 8:(kind + 1) * 8] = bb.reshape(E, 8, 128).transpose(0, 2, 1)
    d["bstage"] = bst
    d["sw1T"] = T_(ins["L0_ffn_sw1"])
    d["sw2T"] = T_(ins["L0_ffn_sw2"])
    d["sw3T"] = T_(ins["L0_ffn_sw3"])
    d["id128"] = np.eye(128, dtype=BF)
    d["id128f"] = np.eye(128, dtype=np.float32)
    d["id8"] = np.eye(8, dtype=np.float32)
    onr = np.zeros((65, 128), np.float32)
    onr[0] = 1.0; onr[32] = 1.0; onr[64] = 1.0
    d["ones_row"] = onr
    d["ones8"] = np.ones((8, 1), np.float32)
    bd4 = np.zeros((128, 4), np.float32)
    for m in range(4):
        bd4[32 * m:32 * (m + 1), m] = 1.0
    d["bd4"] = bd4.astype(BF)
    d["mask2"] = np.where(np.arange(128)[:, None] <= np.arange(128)[None, :],
                          np.float32(0.0), np.float32(-1e10)).astype(np.float32)
    d["thresh"] = np.broadcast_to((np.arange(NSEG) * MOE_CHUNK).astype(np.float32), (8, NSEG)).copy()
    d["gcmp"] = np.broadcast_to(np.arange(NSEG).astype(np.float32), (8, NSEG)).copy()
    tpos = np.arange(1, T + 1, dtype=np.float32)[:, None]
    theta = (1.0 / 10000.0) ** (2.0 * np.arange(32, dtype=np.float32) / 64.0)
    ang = tpos * theta
    rows = np.arange(128) % 32
    d["cos4"] = np.cos(ang)[:, rows].T.astype(BF)
    d["sin4"] = np.sin(ang)[:, rows].T.astype(BF)
    return d


def kernel(**inputs):
    nc = build_nc()
    in_maps = [_host_inputs(inputs, c) for c in range(8)]
    res = run_bass_kernel_spmd(nc, in_maps, core_ids=list(range(8)))
    outs = [np.asarray(r["out"]) for r in res.results]
    return np.concatenate(outs, axis=0).reshape(16, T, D).astype(np.float32)


if __name__ == "__main__":
    nc = build_nc()
    print("built ok, instructions:", nc.n_instructions())


# version 24
# speedup vs baseline: 1.0045x; 1.0045x over previous
import itertools
import numpy as np
import ml_dtypes
from contextlib import ExitStack

import concourse.bass as bass
import concourse.mybir as mybir
from concourse.bass_utils import run_bass_kernel_spmd
from concourse.bass_core_rust import IndirectOffsetOnAxis

F32 = mybir.dt.float32
BF16 = mybir.dt.bfloat16
I32 = mybir.dt.int32
AF = mybir.ActivationFunctionType
ALU = mybir.AluOpType
AX = mybir.AxisListType.X
BF = ml_dtypes.bfloat16

B_PER_CORE = 2
T = 2048
D = 1024
NH = 16
NTILE = T // 128
E = 8
MOE_CHUNK = 256
NSEG = 24
EPS = 1e-5

N_SUBLAYERS = 4
B_RUN = B_PER_CORE
DEBUG_DUMP = False
DBG = {}

ENGINES = ("tensor", "vector", "scalar", "gpsimd", "sync")


class Sem:
    __slots__ = ("h", "name", "n")

    def __init__(self, handle, name):
        self.h, self.name, self.n = handle, name, 0

    def bump(self, k):
        self.n += k
        return self.n

    def tok(self):
        return (self, self.n)


class Prog:
    def __init__(self, nc):
        self.nc = nc
        self.streams = {e: [] for e in ENGINES}

    def raw(self, eng, fn):
        self.streams[eng].append(fn)

    def op(self, eng, emit, waits=(), inc=None, inc_k=1):
        waits = tuple(w for w in waits if w is not None)
        for s, v in waits:
            assert v <= s.n, f"wait on future token {s.name}:{v} > {s.n}"
        if inc is not None:
            inc.bump(inc_k)

        def fn(e, waits=waits, emit=emit, inc=inc, inc_k=inc_k):
            for s, v in waits:
                if v > 0:
                    e.wait_ge(s.h, v)
            ins = emit(e)
            if inc is not None:
                ins.then_inc(inc.h, inc_k)
        self.raw(eng, fn)
        return inc.tok() if inc is not None else None

    def dma(self, q, out_ap, in_ap, sem, waits=(), **kw):
        return self.op(q, lambda e: e.dma_start(out=out_ap, in_=in_ap, **kw),
                       waits=waits, inc=sem, inc_k=16)

    def wait(self, eng, *toks):
        toks = tuple(t for t in toks if t is not None)
        for s, v in toks:
            assert v <= s.n

        def fn(e, toks=toks):
            for s, v in toks:
                if v > 0:
                    e.wait_ge(s.h, v)
        self.raw(eng, fn)


class Pool:

    def __init__(self, nc, base_off, nbytes):
        self.nc, self.base, self.size = nc, base_off, nbytes
        self.off, self.marks, self.uid, self.hiwater = 0, [], 0, 0

    def alloc(self, shape, dtype, name=None):
        isz = {F32: 4, BF16: 2, I32: 4}[dtype]
        nb = shape[1] * isz
        self.off = (self.off + 31) // 32 * 32
        o = self.off
        self.off += nb
        self.hiwater = max(self.hiwater, self.off)
        assert self.off <= self.size, f"pool overflow at {name}: {self.off} > {self.size}"
        self.uid += 1
        h = self.nc.alloc_sbuf_tensor_at(f"pl{self.uid}_{name or 'b'}", list(shape), dtype,
                                         offset=self.base + o)
        return h.ap()

    def mark(self):
        self.marks.append(self.off)

    def release(self):
        self.off = self.marks.pop()


RACE_DETECT = True
GATE_ONEPASS = False
MOE_STATIC_LOADER = False
MOE_STATIC_TWAIT = False
MOE_D_LEVEL = 4
MOE_NO_LOADER = False
MOE_NO_GATHER = False
MOE_NO_NOPS = False
MOE_LOADER_SC_ONLY = False
MOE_NO_WAITS = False
FFN_PHASES = "ABCDE"


def build_nc():
    nc = bass.Bass("TRN2", target_bir_lowering=False, debug=False,
                   detect_race_conditions=RACE_DETECT)
    dr = {}

    def din(name, shape, dtype=F32):
        dr[name] = nc.dram_tensor(name, list(shape), dtype, kind="ExternalInput").ap()
        return dr[name]

    x_in = din("x_in", [B_PER_CORE, T, D])
    out = nc.dram_tensor("out", [B_PER_CORE, T, D], F32, kind="ExternalOutput").ap()

    din("wqaTp", [D, D]); din("wkaTp", [D, D]); din("wvaT", [D, D]); din("woaT", [D, D])
    din("bqe_s", [128, 4]); din("bqo_s", [128, 4]); din("bke_s", [128, 4]); din("bko_s", [128, 4])
    din("bv_row", [1, D]); din("boa_row", [1, D])
    din("wqb", [D, D]); din("wdb", [D, 256]); din("wub", [256, 2048]); din("wob", [D, D])
    din("bq2_s", [128, 8]); din("bd_s", [128, 2]); din("bu_row", [1, 2048]); din("bob_row", [1, D])
    din("gwT", [D, 8])
    din("w1T", [E, D, D]); din("w2T", [E, D, D]); din("w3T", [E, D, D])
    din("bstage", [E, 128, 24])
    din("sw1T", [D, 2048]); din("sw2T", [D, 2048]); din("sw3T", [2048, D])
    din("id128", [128, 128], BF16)
    din("id128f", [128, 128])
    din("id8", [8, 8])
    din("ones_row", [65, 128])
    din("ones8", [8, 1])
    din("bd4", [128, 4], BF16)
    din("mask2", [128, 128])
    din("thresh", [8, NSEG])
    din("gcmp", [8, NSEG])
    din("cos4", [128, T], BF16)
    din("sin4", [128, T], BF16)

    xs_scr = nc.dram_tensor("xs_scr", [NSEG * MOE_CHUNK, D], BF16, kind="Internal").ap()
    y_scr = nc.dram_tensor("y_scr", [NSEG * MOE_CHUNK, D], BF16, kind="Internal").ap()
    wmoe = nc.dram_tensor("wmoe", [E, 3, 2, 128, 4096], BF16, kind="Internal").ap()
    sw12_scr = nc.dram_tensor("sw12_scr", [2, 16, 128, 1024], BF16, kind="Internal").ap()
    sw3_scr = nc.dram_tensor("sw3_scr", [16, 128, 1024], BF16, kind="Internal").ap()
    wat_c = {nm: nc.dram_tensor(nm + "_c", [2, 128, 4096], BF16, kind="Internal").ap()
             for nm in ("wqaTp", "wkaTp", "wvaT", "woaT", "wqb", "wob")}
    wdb_c = nc.dram_tensor("wdb_c", [128, 2048], BF16, kind="Internal").ap()
    wub_c = nc.dram_tensor("wub_c", [128, 4096], BF16, kind="Internal").ap()

    with ExitStack() as ctx:
        def sbuf(name, shape, dtype):
            return ctx.enter_context(nc.sbuf_tensor(name, list(shape), dtype)).ap()

        x_sb = sbuf("x_sb", [128, NTILE * D], F32)
        c_id128 = sbuf("c_id128", [128, 128], BF16)
        c_id128f = sbuf("c_id128f", [128, 128], F32)
        c_id8 = sbuf("c_id8", [8, 8], F32)
        c_ones = sbuf("c_ones", [65, 128], F32)
        c_ones8 = sbuf("c_ones8", [8, 1], F32)
        c_bd4 = sbuf("c_bd4", [128, 4], BF16)
        c_mask2 = sbuf("c_mask2", [128, 128], F32)
        c_thresh = sbuf("c_thresh", [8, NSEG], F32)
        c_gcmp = sbuf("c_gcmp", [8, NSEG], F32)
        c_gw = sbuf("c_gw", [128, 64], F32)
        c_gh = sbuf("c_gh", [128, 64], BF16)
        c_gl = sbuf("c_gl", [128, 64], BF16)
        c_battn = sbuf("c_battn", [128, 26], F32)
        c_rows = sbuf("c_rows", [128, 2048], BF16)
        c_onesb = sbuf("c_onesb", [65, 128], BF16)
        tc_bb = [sbuf("tc_bb0", [128, 2048], BF16), sbuf("tc_bb1", [128, 2048], BF16)]

        rem = nc.sbuf_bytes_remaining
        pool_bytes = (rem - 3072) // 32 * 32
        pool_h = ctx.enter_context(nc.sbuf_tensor("pool", [128, pool_bytes], mybir.dt.uint8))
        pool = Pool(nc, nc.lookup_mloc(pool_h).addr, pool_bytes)

        sems = {nm: Sem(ctx.enter_context(nc.semaphore(nm)), nm)
                for nm in ("sL", "sG", "sT", "sV", "sS", "sO", "sW1", "sW3", "sW2", "sWb",
                           "sC", "sCv", "sC1", "sC3", "sRt",
                           "sE1", "sE3", "sE2", "sEb",
                           "sX", "sGs", "sTci", "sTco", "sCs", "sSw", "sSwB", "sSw3", "sSw3B", "sY", "sCsc", "sM")}
        sL, sG, sT, sV, sS, sO = (sems[k] for k in ("sL", "sG", "sT", "sV", "sS", "sO"))
        sW1, sW3, sW2, sWb = sems["sW1"], sems["sW3"], sems["sW2"], sems["sWb"]
        sC, sCv = sems["sC"], sems["sCv"]
        sC1, sC3 = sems["sC1"], sems["sC3"]
        sRt = sems["sRt"]
        sE1, sE3, sE2, sEb = sems["sE1"], sems["sE3"], sems["sE2"], sems["sEb"]
        sX, sGs, sTci, sTco = (sems[k] for k in ("sX", "sGs", "sTci", "sTco"))
        sCs = sems["sCs"]
        sSw, sY = sems["sSw"], sems["sY"]
        sCsc = sems["sCsc"]
        sw_sems = [sems["sSw"], sems["sSwB"]]
        sw3_sems = [sems["sSw3"], sems["sSw3B"]]

        P = Prog(nc)
        rd = {}

        def fence():
            toks = [sm.tok() for sm in (sems["sT"], sems["sV"], sems["sS"], sems["sG"],
                                        sems["sL"], sems["sCs"], sems["sX"], sems["sGs"],
                                        sems["sW1"], sems["sW3"], sems["sW2"], sems["sWb"],
                                        sems["sO"], sems["sC"], sems["sCv"],
                                        sems["sC1"], sems["sC3"], sems["sRt"],
                                        sems["sE1"], sems["sE3"], sems["sE2"], sems["sEb"],
                                        sems["sSw"], sems["sSwB"], sems["sSw3"],
                                        sems["sSw3B"], sems["sY"], sems["sCsc"])]
            for eng2 in ENGINES:
                P.wait(eng2, *toks)
        _uid = itertools.count()

        def mkps(stack, nm, shape, dt2):
            return stack.enter_context(
                nc.psum_tensor(f"{nm}_{next(_uid)}", list(shape), dt2)).ap()

        x_r = x_in.rearrange("b (tl p) c -> b p tl c", p=128)
        out_r = out.rearrange("b (tl p) c -> b p tl c", p=128)
        x_sb3 = x_sb.rearrange("p (tl c) -> p tl c", c=D)
        P.dma("sync", x_sb3, x_r[0], sX)
        for dst, src in [(c_id128, dr["id128"]), (c_id128f, dr["id128f"]),
                         (c_id8, dr["id8"]), (c_ones, dr["ones_row"]),
                         (c_ones8, dr["ones8"]), (c_bd4, dr["bd4"]), (c_mask2, dr["mask2"]),
                         (c_thresh, dr["thresh"]), (c_gcmp, dr["gcmp"])]:
            P.dma("sync", dst, src, sL)
        P.dma("sync", c_gw.rearrange("p (ch e) -> p ch e", e=8), dr["gwT"].rearrange("(ch p) e -> p ch e", p=128), sL)
        for col, nm, w in [(0, "bqe_s", 4), (4, "bqo_s", 4), (8, "bke_s", 4), (12, "bko_s", 4),
                           (16, "bq2_s", 8), (24, "bd_s", 2)]:
            P.dma("sync", c_battn[:, col:col + w], dr[nm], sL)
        for r0, o, nm, w in [(0, 0, "bv_row", D), (0, 1024, "boa_row", D),
                             (32, 0, "bu_row", 2048), (64, 0, "bob_row", D)]:
            P.dma("gpsimd", c_rows[r0:r0 + 1, o:o + w], dr[nm], sG)
        P.dma("gpsimd", c_onesb, dr["ones_row"], sG)
        tok_consts = sL.tok()
        t1 = P.op("vector", lambda e: e.tensor_copy(c_gh, c_gw), waits=[tok_consts], inc=sV)
        P.op("vector", lambda e: e.tensor_sub(c_gl, c_gw, c_gh), waits=[t1], inc=sV)
        tok_gate = sV.tok()

        pool.mark()
        ztile = pool.alloc([128, 2048], BF16, "ztile")
        tz = P.op("gpsimd", lambda e: e.memset(ztile, 0.0), inc=sems["sM"])
        xs_r = xs_scr.rearrange("(n a p) c -> n p a c", p=128, a=2)
        for i in range(NSEG * MOE_CHUNK // 256):
            P.dma("gpsimd", xs_r[i], ztile.rearrange("p (a c) -> p a c", c=D), sG, waits=[tz])
        pool.release()

        tok_attn = {}

        def emit_transcode():
            i = [0]
            pend = [None]

            def flush():
                if pend[0] is not None:
                    dst_ap, bv = pend[0]
                    P.dma("sync", dst_ap, bv, sTco, waits=[sTci.tok()])
                    pend[0] = None

            def piece(dst_ap, src_ap, n):
                k = i[0] % 2
                bv = tc_bb[k][:, :n].rearrange("p (a c) -> p a c", c=src_ap.shape[-1])
                P.dma("gpsimd", bv, src_ap, sTci,
                      waits=[sTco.tok()] if i[0] >= 2 else [])
                flush()
                pend[0] = (dst_ap, bv)
                i[0] += 1

            for nm in ("wqaTp", "wkaTp", "wvaT", "woaT", "wqb", "wob"):
                src = dr[nm].rearrange("(h a p) c -> h p a c", p=128, a=4)
                dc = wat_c[nm].rearrange("h p (s a c) -> h s p a c", s=2, c=1024)
                for h in range(2):
                    for sub in range(2):
                        piece(dc[h, sub], src[h][:, 2 * sub:2 * sub + 2], 2048)
                flush()
                tok_attn[nm] = sTco.tok()
            piece(wdb_c.rearrange("p (a c) -> p a c", c=256),
                  dr["wdb"].rearrange("(a p) c -> p a c", p=128), 2048)
            flush()
            tok_attn["wdb"] = sTco.tok()
            wub_v = wub_c.rearrange("p (ch c) -> ch p c", ch=2)
            srcu = dr["wub"].rearrange("(ch p) c -> ch p c", p=128)
            for ch in range(2):
                piece(wub_v[ch].unsqueeze(1), srcu[ch].unsqueeze(1), 2048)
            flush()
            tok_attn["wub"] = sTco.tok()
            for ei in range(E):
                for kind, nm in ((0, "w1T"), (1, "w3T"), (2, "w2T")):
                    src = dr[nm].rearrange("e (h a p) c -> e h p a c", p=128, a=4)
                    dstr = wmoe.rearrange("e k h p (s a c) -> e k h s p a c", s=2, c=1024)
                    for h in range(2):
                        for sub in range(2):
                            piece(dstr[ei, kind, h, sub], src[ei, h][:, 2 * sub:2 * sub + 2], 2048)
            for m, nm in ((0, "sw1T"), (1, "sw2T")):
                src = dr[nm].rearrange("(a p) (blk i) -> blk p a i", p=128, blk=16)
                for blk in range(16):
                    piece(sw12_scr[m, blk], src[blk], 1024)
            src = dr["sw3T"].rearrange("(ic p) c -> ic p c", p=128)
            for ic in range(16):
                piece(sw3_scr[ic], src[ic], 1024)
            flush()
            return (sTco.tok(),)

        emit_transcode()

        def mk_ln(pl, n=16):
            d = {nm: pl.alloc([128, n], F32, "ln_" + nm)
                 for nm in ("sum", "sq", "m2", "var", "r", "nm")}
            d["sqs"] = pl.alloc([128, 1024], BF16, "ln_sqs")
            d["n"] = n
            return d

        def ln_stats_tile(sd, t, x_tile, extra_waits=()):
            tv = P.op("vector", lambda e, t=t: e.reduce_sum(sd["sum"][:, t:t + 1], x_tile, AX),
                      waits=extra_waits, inc=sV)
            ts2 = P.op("scalar", lambda e, t=t: e.activation(out=sd["sqs"], in_=x_tile, func=AF.Square,
                                                             accum_out=sd["sq"][:, t:t + 1]),
                       waits=tuple(extra_waits) + (rd.get(("sqs", id(sd))),), inc=sS)
            rd[("sqs", id(sd))] = ts2
            return tv, ts2

        def ln_finish(sd):
            t1 = P.op("vector", lambda e: e.tensor_mul(sd["m2"], sd["sum"], sd["sum"]),
                      waits=[sV.tok(), sS.tok()], inc=sV)
            t2 = P.op("vector", lambda e: e.scalar_tensor_tensor(sd["var"], sd["m2"], -1.0 / D,
                                                                 sd["sq"], ALU.mult, ALU.add),
                      waits=[t1], inc=sV)
            t3 = P.op("vector", lambda e: e.tensor_scalar(sd["var"], sd["var"], 1.0 / D, EPS,
                                                          ALU.mult, ALU.add), waits=[t2], inc=sV)
            t4 = P.op("scalar", lambda e: e.activation(out=sd["m2"], in_=sd["var"], func=AF.Sqrt,
                                                       bias=0.0, scale=1.0), waits=[t3], inc=sS)
            t5 = P.op("vector", lambda e: e.reciprocal(sd["r"], sd["m2"]), waits=[t4], inc=sV)
            t6 = P.op("vector", lambda e: e.tensor_scalar(sd["nm"], sd["sum"], -1.0 / D, None,
                                                          ALU.mult), waits=[sV.tok()], inc=sV)
            t7 = P.op("vector", lambda e: e.tensor_mul(sd["nm"], sd["nm"], sd["r"]),
                      waits=[t6, t5], inc=sV)
            sd["tok"] = t7
            return t7

        def ln_norm(sd, t, x_tile, dst_bf, dst_f32=None, dst_lo=None, extra_waits=()):
            w = list(extra_waits) + [sd["tok"]]
            if dst_f32 is not None:
                th = P.op("vector", lambda e, t=t: e.tensor_scalar(dst_bf, x_tile,
                                                                   sd["nm"][:, t:t + 1],
                                                                   sd["r"][:, t:t + 1],
                                                                   ALU.add, ALU.mult),
                          waits=w, inc=sV)
                tf = P.op("vector", lambda e, t=t: e.tensor_scalar(dst_f32, x_tile,
                                                                   sd["nm"][:, t:t + 1],
                                                                   sd["r"][:, t:t + 1],
                                                                   ALU.add, ALU.mult),
                          waits=w, inc=sV)
                tl = P.op("vector", lambda e: e.tensor_sub(dst_lo, dst_f32, dst_bf), waits=[tf], inc=sV)
                return th, tl
            th = P.op("vector", lambda e, t=t: e.tensor_scalar(dst_bf, x_tile, sd["nm"][:, t:t + 1],
                                                               sd["r"][:, t:t + 1], ALU.add, ALU.mult),
                      waits=w, inc=sV)
            return th, None

        def transpose8(src_ap, dst_fn, banks, wait_tok, n=8, key_pfx="psTt", alt=False):
            nb = len(banks)
            lastV = lastS = None
            for j in range(n):
                key = (key_pfx, j % nb)
                pt = banks[j % nb]
                tmm = P.op("tensor", lambda e, pt=pt, j=j: e.transpose(
                    pt[:, 0:128], src_ap[:, 128 * j:128 * (j + 1)], c_id128),
                    waits=[wait_tok, sL.tok(), rd.get(key)], inc=sT)
                if alt and j % 2 == 1:
                    lastS = P.op("scalar", lambda e, pt=pt, j=j: e.copy(out=dst_fn(j), in_=pt[:, 0:128]),
                                 waits=[tmm], inc=sS)
                    rd[key] = lastS
                else:
                    lastV = P.op("vector", lambda e, pt=pt, j=j: e.tensor_copy(dst_fn(j), pt[:, 0:128]),
                                 waits=[tmm], inc=sV)
                    rd[key] = lastV
            if alt and lastS is not None:
                lastV = P.op("vector", lambda e: e.nop(), waits=[lastS, lastV], inc=sV)
            return lastV

        def transpose8g(src_fn, dst_views, banks2, wait_tok, keys2):
            ev = None
            for grp in range(2):
                bank, key, dv = banks2[grp], keys2[grp], dst_views[grp]
                for jj in range(4):
                    j = 4 * grp + jj
                    P.op("tensor", lambda e, j=j, jj=jj, bank=bank: e.matmul(
                        bank[:, 128 * jj:128 * (jj + 1)], src_fn(j), c_id128,
                        start=(jj == 0), stop=(jj == 3)),
                        waits=([wait_tok, sL.tok(), rd.get(key)] if jj == 0 else []),
                        inc=sT if jj == 3 else None)
                tmm = sT.tok()
                bk = bank
                if len(dv.shape) == 3:
                    bk = bank.rearrange("p (a c) -> p a c", c=128)
                ev = P.op("vector", lambda e, bk=bk, dv=dv: e.tensor_copy(dv, bk),
                          waits=[tmm], inc=sV)
                rd[key] = ev
            return ev

        def sublayer_attn_a(b, first):
            pool.mark()
            Qr = pool.alloc([128, 4 * T], BF16, "Qr")
            Kr = pool.alloc([128, 4 * T], BF16, "Kr")
            Vx = pool.alloc([128, NTILE * NH * 65], BF16, "Vx")
            mhat = pool.alloc([128, 1], F32, "mhat")
            Vx4 = Vx.rearrange("p (t h j) -> p t h j", h=NH, j=65)
            pc = ExitStack()
            with pc:
                psTb = [mkps(pc, "psTa", [128, 512], F32) for _ in range(2)]
                psQ = mkps(pc, "psQ", [128, 512], F32)
                psQ2 = mkps(pc, "psQ2", [128, 512], F32)
                psM = mkps(pc, "psM", [128, 512], F32)
                psM2 = mkps(pc, "psM2", [128, 512], F32)
                psM3 = mkps(pc, "psM3", [128, 512], F32)
                psM4 = mkps(pc, "psM4", [128, 512], F32)

                fence()
                pool.mark()
                wqk = pool.alloc([128, 2 * 8192], BF16, "wqk")
                fm2 = [pool.alloc([128, 8 * 512], BF16, "fmA"), pool.alloc([128, 8 * 512], BF16, "fmB")]
                nh4 = pool.alloc([128, 2048], BF16, "nh4")
                cs2 = [pool.alloc([128, 1024], BF16, "csA"), pool.alloc([128, 1024], BF16, "csB")]
                lns = mk_ln(pool)
                ev = {k: pool.alloc([128, 512], BF16, "ev" + k) for k in ("qe", "qo")}
                for tt in range(NTILE):
                    ln_stats_tile(lns, tt, x_sb[:, 1024 * tt:1024 * (tt + 1)],
                                  extra_waits=[sX.tok()] if tt == 0 else ())
                ln_finish(lns)
                for i2, wnm in enumerate(("wqaTp", "wkaTp")):
                    for h in range(2):
                        P.dma("sync", wqk[:, i2 * 8192 + h * 4096: i2 * 8192 + (h + 1) * 4096],
                              wat_c[wnm][h], sL, waits=[tok_attn[wnm]])
                tok_w = sL.tok()

                def emit_trs_i(tc):
                    par = tc % 2
                    fm = fm2[par]
                    P.wait("vector", rd.get(("fmw", par)))
                    P.dma("sync", cs2[par][:, 0:512], dr["cos4"][:, 512 * tc:512 * (tc + 1)], sCs,
                          waits=[rd.get(("cs", par))])
                    P.dma("sync", cs2[par][:, 512:1024], dr["sin4"][:, 512 * tc:512 * (tc + 1)], sCs,
                          waits=[rd.get(("cs", par))])
                    tok_fm = None
                    for ti in range(4):
                        tt = 4 * tc + ti
                        np_ = ti % 2
                        th, _ = ln_norm(lns, tt, x_sb[:, 1024 * tt:1024 * (tt + 1)],
                                        nh4[:, 1024 * np_:1024 * (np_ + 1)],
                                        extra_waits=[rd.get(("nh4", np_))])
                        fm_v = fm.rearrange("p (a c) -> p a c", c=512)[:, :, 128 * ti:128 * (ti + 1)]
                        tok_fm = transpose8g(
                            lambda j, np_=np_, nh4=nh4: nh4[:, 1024 * np_ + 128 * j: 1024 * np_ + 128 * (j + 1)],
                            (fm_v[:, 0:4], fm_v[:, 4:8]), psTb, th, (("aT", 0), ("aT", 1)))
                        rd[("nh4", np_)] = sT.tok()
                    return tok_fm

                def emit_mms_i(tc, tok_fm):
                    par = tc % 2
                    fm = fm2[par]
                    cs = cs2[par]
                    tok_cs = sCs.tok()
                    for is_q, base_w, bce, bco, dstbuf in (
                            (True, 0, 0, 4, Qr), (False, 8192, 8, 12, Kr)):
                        for rb in range(4):
                            for half, ps, pskey, bcol, d2 in (
                                    (0, psQ, "psQ", bce, "qe"), (1, psQ2, "psQ2", bco, "qo")):
                                woff = base_w + 512 * half
                                for a in range(8):
                                    P.op("tensor", lambda e, ps=ps, a=a, rb=rb, woff=woff, fm=fm: e.matmul(
                                        ps, wqk[:, woff + a * 1024 + rb * 128: woff + a * 1024 + rb * 128 + 128],
                                        fm[:, 512 * a:512 * (a + 1)], start=(a == 0), stop=(a == 7)),
                                        waits=([tok_fm, tok_w, sL.tok(), rd.get(pskey)] if a == 0 else []),
                                        inc=sT if a == 7 else None)
                                tmm = sT.tok()
                                sc2 = 0.125 if is_q else 1.0
                                tev = P.op("vector", lambda e, ps=ps, d2=d2, bc=bcol, rb=rb, sc2=sc2:
                                           e.tensor_scalar(ev[d2], ps, c_battn[:, bc + rb:bc + rb + 1],
                                                           sc2, ALU.add, ALU.mult),
                                           waits=[tmm, rd.get("evw")], inc=sV)
                                rd[pskey] = tev
                            tu = P.op("vector", lambda e, cs=cs: e.tensor_mul(ev["qe"], ev["qe"], cs[:, 0:512]),
                                      waits=[tev, tok_cs], inc=sV)
                            tv = P.op("vector", lambda e, cs=cs: e.tensor_mul(ev["qo"], ev["qo"], cs[:, 512:1024]),
                                      waits=[tu], inc=sV)
                            tr = P.op("vector", lambda e, dstbuf=dstbuf, rb=rb, tc=tc: e.tensor_sub(
                                dstbuf[:, rb * T + 512 * tc: rb * T + 512 * (tc + 1)],
                                ev["qe"], ev["qo"]), waits=[tv], inc=sV)
                            rd[("cs", par)] = tr
                            rd["evw"] = tr
                    rd[("fmw", par)] = sT.tok()

                toks_i = {}
                for tc in range(4):
                    toks_i[tc] = emit_trs_i(tc)
                    if tc > 0:
                        emit_mms_i(tc - 1, toks_i[tc - 1])
                emit_mms_i(3, toks_i[3])
                pool.release()

                fence()
                pool.mark()
                wv = pool.alloc([128, 8192], BF16, "wv")
                fm2 = [pool.alloc([128, 8 * 512], BF16, "fm2A"), pool.alloc([128, 8 * 512], BF16, "fm2B")]
                nh4 = pool.alloc([128, 2048], BF16, "nh4v")
                lns = mk_ln(pool)
                for tt in range(NTILE):
                    ln_stats_tile(lns, tt, x_sb[:, 1024 * tt:1024 * (tt + 1)])
                ln_finish(lns)
                for h in range(2):
                    P.dma("sync", wv[:, h * 4096:(h + 1) * 4096], wat_c["wvaT"][h], sL,
                          waits=[tok_attn["wvaT"]])
                tok_w = sL.tok()
                tvx = P.op("vector", lambda e: e.memset(Vx4[:, :, :, 64:65], 1.0), inc=sV)

                def emit_trs_v(tc):
                    par = tc % 2
                    fm = fm2[par]
                    P.wait("vector", rd.get(("fmwv", par)))
                    tok_fm = None
                    for ti in range(4):
                        tt = 4 * tc + ti
                        np_ = ti % 2
                        th, _ = ln_norm(lns, tt, x_sb[:, 1024 * tt:1024 * (tt + 1)],
                                        nh4[:, 1024 * np_:1024 * (np_ + 1)],
                                        extra_waits=[rd.get(("nh4v", np_))])
                        fm_v = fm.rearrange("p (a c) -> p a c", c=512)[:, :, 128 * ti:128 * (ti + 1)]
                        tok_fm = transpose8g(
                            lambda j, np_=np_, nh4=nh4: nh4[:, 1024 * np_ + 128 * j: 1024 * np_ + 128 * (j + 1)],
                            (fm_v[:, 0:4], fm_v[:, 4:8]), psTb, th, (("aT", 0), ("aT", 1)))
                        rd[("nh4v", np_)] = sT.tok()
                    return tok_fm

                def emit_mms_v(tc, tok_fm):
                    par = tc % 2
                    fm = fm2[par]
                    pmb = [psM, psM2, psQ, psQ2, psM3, psM4]
                    for ti in range(4):
                        tt = 4 * tc + ti
                        for oc in range(2):
                            pm = pmb[(2 * ti + oc) % 6]
                            for a in range(8):
                                P.op("tensor", lambda e, a=a, ti=ti, oc=oc, fm=fm, pm=pm: e.matmul(
                                    pm, fm[:, 512 * a + 128 * ti:512 * a + 128 * (ti + 1)],
                                    wv[:, a * 1024 + 512 * oc: a * 1024 + 512 * (oc + 1)],
                                    start=(a == 0), stop=False),
                                    waits=([tok_fm, tok_w, rd.get(("psM", (2 * ti + oc) % 6)), tvx] if a == 0 else []))
                            P.op("tensor", lambda e, oc=oc, pm=pm: e.matmul(
                                pm, c_onesb[0:1, :], c_rows[0:1, 512 * oc:512 * (oc + 1)],
                                start=False, stop=True), inc=sT)
                            tmm = sT.tok()
                            tev = P.op("vector", lambda e, tt=tt, oc=oc, pm=pm: e.tensor_copy(
                                Vx4[:, tt, 8 * oc:8 * (oc + 1), 0:64],
                                pm.rearrange("p (h j) -> p h j", j=64)),
                                waits=[tmm], inc=sV)
                            rd[("psM", (2 * ti + oc) % 6)] = tev
                    rd[("fmwv", par)] = sT.tok()

                toks_v = {}
                for tc in range(4):
                    toks_v[tc] = emit_trs_v(tc)
                    if tc > 0:
                        emit_mms_v(tc - 1, toks_v[tc - 1])
                emit_mms_v(3, toks_v[3])
                pool.release()

                fence()
                pool.mark()
                sqb = pool.alloc([128, 2048], BF16, "sqb")
                nrm = pool.alloc([4, 40], F32, "nrm")
                srow = pool.alloc([1, 8], F32, "srow")
                tprev = sV.tok()
                tsq_war = None
                for wi, srcb in ((0, Qr), (1, Kr)):
                    for blk in range(4):
                        tsq = P.op("scalar", lambda e, srcb=srcb, blk=blk: e.activation(
                            out=sqb, in_=srcb[:, blk * T:(blk + 1) * T], func=AF.Square),
                            waits=[tprev, tsq_war], inc=sS)
                        for qc in range(4):
                            tmm = P.op("tensor", lambda e, qc=qc: e.matmul(
                                psM[0:4, :], c_bd4, sqb[:, 512 * qc:512 * (qc + 1)],
                                start=True, stop=True), waits=[tsq, rd.get("psM")], inc=sT)
                            tprev = P.op("vector", lambda e, wi=wi, blk=blk, qc=qc: e.reduce_max(
                                nrm[:, 16 * wi + 4 * blk + qc:16 * wi + 4 * blk + qc + 1],
                                psM[0:4, :], AX), waits=[tmm], inc=sV)
                            rd["psM"] = tprev
                        tsq_war = tprev
                t1 = P.op("vector", lambda e: e.reduce_max(nrm[:, 32:33], nrm[:, 0:16], AX),
                          waits=[tprev], inc=sV)
                t2 = P.op("vector", lambda e: e.reduce_max(nrm[:, 33:34], nrm[:, 16:32], AX),
                          waits=[t1], inc=sV)
                t3 = P.op("tensor", lambda e: e.transpose(psM[0:2, 0:4], nrm[0:4, 32:34],
                                                          c_id8[0:4, 0:4]),
                          waits=[t2, rd.get("psM")], inc=sT)
                t4 = P.op("vector", lambda e: e.reduce_max(nrm[0:2, 34:35], psM[0:2, 0:4], AX),
                          waits=[t3], inc=sV)
                rd["psM"] = t4
                t5 = P.op("tensor", lambda e: e.transpose(psM[0:1, 0:2], nrm[0:2, 34:35],
                                                          c_id8[0:2, 0:2]),
                          waits=[t4, rd.get("psM")], inc=sT)
                t6 = P.op("vector", lambda e: e.tensor_copy(srow[0:1, 0:2], psM[0:1, 0:2]),
                          waits=[t5], inc=sV)
                rd["psM"] = t6
                t7 = P.op("vector", lambda e: e.tensor_mul(srow[0:1, 2:3], srow[0:1, 0:1],
                                                           srow[0:1, 1:2]), waits=[t6], inc=sV)
                t8 = P.op("scalar", lambda e: e.activation(out=srow[0:1, 3:4], in_=srow[0:1, 2:3],
                                                           func=AF.Sqrt, bias=0.0, scale=1.0),
                          waits=[t7], inc=sS)
                t9 = P.op("vector", lambda e: e.tensor_scalar(srow[0:1, 4:5], srow[0:1, 3:4],
                                                              -1.0, None, ALU.mult),
                          waits=[t8], inc=sV)
                tb = P.op("tensor", lambda e: e.matmul(psM[:, 0:1], c_ones[0:1, :], srow[0:1, 4:5],
                                                       start=True, stop=True),
                          waits=[t9, rd.get("psM")], inc=sT)
                tok_mhat = P.op("vector", lambda e: e.tensor_copy(mhat, psM[:, 0:1]),
                                waits=[tb], inc=sV)
                rd["psM"] = tok_mhat
                pool.release()

            if True:
                fence()
                pool.mark()
                wo = pool.alloc([128, 8192], BF16, "wo")
                Ut = [pool.alloc([128, 512], BF16, "Ut0"), pool.alloc([128, 512], BF16, "Ut1")]
                rlr = [pool.alloc([1, 512], F32, "rlr0"), pool.alloc([1, 512], F32, "rlr1")]
                rlbc = [pool.alloc([64, 512], F32, "rlbc0"), pool.alloc([64, 512], F32, "rlbc1")]
                yq = [pool.alloc([128, T], BF16, "yq0"), pool.alloc([128, T], BF16, "yq1")]
                for h in range(2):
                    P.dma("sync", wo[:, h * 4096:(h + 1) * 4096], wat_c["woaT"][h], sL,
                          waits=[tok_attn["woaT"]])
                tok_w = sL.tok()
                with ExitStack() as pc2:
                    psSt = [mkps(pc2, "psSt0", [128, 512], F32), mkps(pc2, "psSt1", [128, 512], F32)]
                    psPV = [mkps(pc2, "psPV0", [65, 512], F32), mkps(pc2, "psPV1", [65, 512], F32),
                            mkps(pc2, "psPV2", [65, 512], F32)]
                    psBc = mkps(pc2, "psBc0", [64, 512], F32)
                    psWo = [mkps(pc2, "psWo0", [128, 512], F32), mkps(pc2, "psWo1", [128, 512], F32)]
                    x_war = sV.tok()

                    def emit_norm(job):
                        (pvi2, ysl2, yrow2, qg2, hh2) = job
                        ri = pvi2 % 2
                        tok_pv = job_tok[0]
                        tr1 = P.op("vector", lambda e, pvi=pvi2, ri=ri: e.reciprocal(rlr[ri], psPV[pvi][64:65, :]),
                                   waits=[tok_pv, rd.get(("rlr", ri))], inc=sV)
                        tb1 = P.op("tensor", lambda e, ri=ri: e.matmul(psBc, c_ones[0:1, 0:64],
                                                                      rlr[ri], start=True, stop=True),
                                   waits=[tr1, rd.get("psBc")], inc=sT)
                        rd[("rlr", ri)] = tb1
                        tb2 = P.op("vector", lambda e, ri=ri: e.tensor_copy(rlbc[ri], psBc),
                                   waits=[tb1, rd.get(("rlbc", ri))], inc=sV)
                        rd["psBc"] = tb2
                        ty = P.op("vector", lambda e, ysl=ysl2, yrow0=yrow2, qg=qg2, pvi=pvi2, ri=ri: e.tensor_mul(
                            ysl[yrow0:yrow0 + 64, 512 * qg:512 * (qg + 1)], psPV[pvi][0:64, :], rlbc[ri]),
                            waits=[tb2, rd.get(("yq", hh2 // 2))], inc=sV)
                        rd[("psPV", pvi2)] = ty
                        rd[("rlbc", ri)] = ty
                        return ty

                    pending_norm = [None]
                    job_tok = [None]
                    gctr = 0
                    for quad in range(4):
                        yq_tok = []
                        for hh in range(4):
                            h = 4 * quad + hh
                            blk, base = h // 4, 32 * (h % 4)
                            ysl, yrow0 = yq[hh // 2], 64 * (hh % 2)
                            for qg in range(4):
                                kts = 4 * qg + 4
                                pvi = gctr % 3
                                gctr += 1
                                pend = None
                                for kt in range(kts + 1):
                                    if kt < kts:
                                        si = kt % 2
                                        ps = psSt[si]
                                        tmm = P.op("tensor", lambda e, ps=ps, blk=blk, base=base, kt=kt, qg=qg:
                                                   e.matmul(ps, Kr[base:base + 32,
                                                                   blk * T + 128 * kt: blk * T + 128 * (kt + 1)],
                                                            Qr[base:base + 32,
                                                               blk * T + 512 * qg: blk * T + 512 * (qg + 1)],
                                                            start=True, stop=True, tile_position=(base, 0)),
                                                   waits=[tok_mhat, rd.get(("st", si))], inc=sT)
                                        tmask = tmm
                                        if kt >= 4 * qg:
                                            pw = 128 * kt - 512 * qg
                                            if pw > 0:
                                                tmask = P.op("vector", lambda e, ps=ps, pw=pw: e.memset(
                                                    ps[:, 0:pw], -1e10), waits=[tmask], inc=sV)
                                            tmask = P.op("vector", lambda e, ps=ps, pw=pw: e.tensor_add(
                                                ps[:, pw:pw + 128], ps[:, pw:pw + 128], c_mask2),
                                                waits=[tmask, sL.tok()], inc=sV)
                                        texp = P.op("scalar", lambda e, ps=ps, si=si: e.activation(
                                            out=Ut[si], in_=ps, func=AF.Exp, bias=mhat[:, 0:1], scale=1.0),
                                            waits=[tmask, rd.get(("ut", si))], inc=sS)
                                        rd[("st", si)] = texp
                                    if kt == 3 and pending_norm[0] is not None:
                                        yq_tok.append(emit_norm(pending_norm[0]))
                                        pending_norm[0] = None
                                    if pend is not None:
                                        ptexp, psi, pkt = pend
                                        tpv = P.op("tensor", lambda e, psi=psi, pkt=pkt, h=h, kts=kts, pvi=pvi: e.matmul(
                                            psPV[pvi], Vx4[:, pkt, h, :].squeeze(),
                                            Ut[psi], start=(pkt == 0), stop=(pkt == kts - 1)),
                                            waits=[ptexp] + ([rd.get(("psPV", pvi))] if pkt == 0 else []), inc=sT)
                                        rd[("ut", psi)] = tpv
                                    if kt < kts:
                                        pend = (texp, si, kt)
                                job_tok[0] = sT.tok()
                                pending_norm[0] = (pvi, ysl, yrow0, qg, hh)
                        if pending_norm[0] is not None:
                            yq_tok.append(emit_norm(pending_norm[0]))
                            pending_norm[0] = None
                        tok_y = yq_tok[-1]
                        for tt2 in range(NTILE):
                            for oc in range(2):
                                pw_i = (tt2 * 2 + oc) % 2
                                psw = psWo[pw_i]
                                w0 = [tok_y, tok_w] + ([x_war] if (tt2 == 0 and oc == 0) else []) + [rd.get(("psWo", pw_i))]
                                P.op("tensor", lambda e, tt2=tt2, oc=oc, psw=psw, quad=quad: e.matmul(
                                    psw, yq[0][:, 128 * tt2:128 * (tt2 + 1)],
                                    wo[:, (2 * quad) * 1024 + 512 * oc: (2 * quad) * 1024 + 512 * (oc + 1)],
                                    start=True, stop=False), waits=w0)
                                P.op("tensor", lambda e, tt2=tt2, oc=oc, psw=psw, quad=quad: e.matmul(
                                    psw, yq[1][:, 128 * tt2:128 * (tt2 + 1)],
                                    wo[:, (2 * quad + 1) * 1024 + 512 * oc: (2 * quad + 1) * 1024 + 512 * (oc + 1)],
                                    start=False, stop=(quad != 0)),
                                    inc=sT if quad != 0 else None)
                                if quad == 0:
                                    P.op("tensor", lambda e, oc=oc, psw=psw: e.matmul(
                                        psw, c_onesb[0:1, :], c_rows[0:1, 1024 + 512 * oc: 1024 + 512 * (oc + 1)],
                                        start=False, stop=True), inc=sT)
                                tmm = sT.tok()
                                tx = P.op("vector", lambda e, tt2=tt2, oc=oc, psw=psw: e.tensor_add(
                                    x_sb[:, 1024 * tt2 + 512 * oc: 1024 * tt2 + 512 * (oc + 1)],
                                    x_sb[:, 1024 * tt2 + 512 * oc: 1024 * tt2 + 512 * (oc + 1)], psw),
                                    waits=[tmm], inc=sV)
                                rd[("psWo", pw_i)] = tx
                                x_war = tx
                        rd[("yq", 0)] = x_war
                        rd[("yq", 1)] = x_war
                pool.release()
            pool.release()


        def sublayer_ffn(b, ridx):
            J = 2 * T
            fence()
            for eng2 in ENGINES:
                P.wait(eng2, sTco.tok(), sTci.tok())
            pool.mark()
            logits = pool.alloc([8, T], F32, "logits")
            w_sb = pool.alloc([128, 32], F32, "w_sb")
            idx_sb = pool.alloc([128, 32], I32, "idx_sb")
            rsm = pool.alloc([128, 16], F32, "rsm")
            rrow = pool.alloc([8, NSEG], F32, "rrow")
            tsrow = pool.alloc([1, 16], F32, "tsrow")
            ts_i32 = pool.alloc([1, 8], I32, "ts_i32")
            eof_f = pool.alloc([1, NSEG], F32, "eof_f")
            eof_i32 = pool.alloc([1, NSEG], I32, "eof_i32")
            lgt = pool.alloc([128, 8], F32, "lgt")
            sce = pool.alloc([128, 8], F32, "sce")
            sc = pool.alloc([128, 8], F32, "sc")
            top8 = pool.alloc([128, 8], F32, "top8")
            oh0 = pool.alloc([128, 8], F32, "oh0")
            oh1 = pool.alloc([128, 8], F32, "oh1")
            xs_pp = [pool.alloc([128, 1024], BF16, "xs0"), pool.alloc([128, 1024], BF16, "xs1")]
            pool.off = (pool.off + 31) // 32 * 32
            pool.mark()
            nh_tm = pool.alloc([128, NTILE * 1024], BF16, "nh_tm")
            oh_fm = pool.alloc([8, J], BF16, "oh_fm")
            incl = pool.alloc([8, J], F32, "incl")

            pool.mark()
            fm = pool.alloc([128, 8 * 512], BF16, "ffm")
            nf32 = pool.alloc([128, 1024], F32, "nf32")
            nl_t = pool.alloc([128, 1024], BF16, "nl_t")
            nlfm = pool.alloc([128, 8 * 128], BF16, "nlfm")
            a_sb = pool.alloc([128, 512], BF16, "a_sb")
            g_sb = pool.alloc([128, 16 * 512], BF16, "g_sb")
            sw_r = [pool.alloc([128, 2 * 1024], BF16, "swr0"),
                    pool.alloc([128, 2 * 1024], BF16, "swr1")]
            sw3_r = [pool.alloc([128, 1024], BF16, "sw3r0"),
                     pool.alloc([128, 1024], BF16, "sw3r1")]
            lns = mk_ln(pool)
            for tt in range(NTILE):
                ln_stats_tile(lns, tt, x_sb[:, 1024 * tt:1024 * (tt + 1)],
                              extra_waits=[sX.tok()] if tt == 0 else ())
            ln_finish(lns)
            tc_dep = None
            with ExitStack() as pca:
                psTb = [mkps(pca, "fT", [128, 512], F32) for _ in range(2)]
                psA = mkps(pca, "fA", [128, 512], F32)
                psB = mkps(pca, "fB", [128, 512], F32)
                psZ2 = [mkps(pca, "fZ0", [128, 1024], F32), mkps(pca, "fZ1", [128, 1024], F32)]
                psG = psB[0:8, 0:128]
                psRf = psA[:, 128:256]
                pend_oh = []

                def emit_oh_pair(job):
                    tt2, jta0, jta1 = job
                    for k2, (oh, tok) in enumerate(((oh0, jta0), (oh1, jta1))):
                        tb1 = P.op("tensor", lambda e, oh=oh: e.transpose(
                            psRf[0:8, :], oh, c_id128f),
                            waits=[tok, sL.tok(), rd.get("psRf")], inc=sT)
                        tb2 = P.op("vector", lambda e, k2=k2, tt=tt2: e.tensor_copy(
                            oh_fm[0:8, k2 * T + 128 * tt: k2 * T + 128 * (tt + 1)], psRf[0:8, :]),
                            waits=[tb1], inc=sV)
                        rd["psRf"] = tb2
                        rd["psA_rt"] = tb2
                        rd["oh0" if k2 == 0 else "oh1"] = tb2
                    jta0 = oh0

                def emit_route_tile(tt, tok_lg):
                    t1 = P.op("tensor", lambda e, tt=tt: e.transpose(
                        psRf[:, 0:8], logits[0:8, 128 * tt:128 * (tt + 1)], c_id8),
                        waits=[tok_lg, rd.get("psRf")], inc=sT)
                    t2 = P.op("vector", lambda e: e.tensor_copy(lgt, psRf[:, 0:8]),
                              waits=[t1, rd.get("lgt")], inc=sV)
                    rd["psRf"] = t2
                    rd["psA_rt"] = t2
                    if pend_oh:
                        emit_oh_pair(pend_oh.pop(0))
                    t3 = P.op("vector", lambda e: e.reduce_max(rsm[:, 0:1], lgt, AX),
                              waits=[t2], inc=sV)
                    t4 = P.op("vector", lambda e: e.tensor_scalar(rsm[:, 1:2], rsm[:, 0:1], -1.0, None, ALU.mult),
                              waits=[t3], inc=sV)
                    t5 = P.op("scalar", lambda e: e.activation(out=sce, in_=lgt, func=AF.Exp,
                                                               bias=rsm[:, 1:2], scale=1.0,
                                                               accum_out=rsm[:, 2:3]),
                              waits=[t4, rd.get("sce")], inc=sS)
                    t6 = P.op("vector", lambda e: e.reciprocal(rsm[:, 3:4], rsm[:, 2:3]),
                              waits=[t5], inc=sV)
                    t7 = P.op("vector", lambda e: e.tensor_scalar(sc, sce, rsm[:, 3:4], None, ALU.mult),
                              waits=[t6], inc=sV)
                    rd["sce"] = t7
                    t8 = P.op("vector", lambda e: e.max(out=top8, in_=sc), waits=[t7], inc=sV)
                    t9a = P.op("vector", lambda e, tt=tt: e.tensor_copy(w_sb[:, 2 * tt:2 * tt + 1], top8[:, 0:1]),
                               waits=[t8], inc=sV)
                    t9b = P.op("vector", lambda e, tt=tt: e.tensor_copy(w_sb[:, 2 * tt + 1:2 * tt + 2], top8[:, 1:2]),
                               waits=[t8], inc=sV)
                    wsb_toks.append(t9b)
                    ta0 = P.op("vector", lambda e: e.tensor_single_scalar(oh0, sc, top8[:, 0:1], ALU.is_equal),
                               waits=[t8, rd.get("oh0")], inc=sV)
                    ta1 = P.op("vector", lambda e: e.tensor_single_scalar(oh1, sc, top8[:, 1:2], ALU.is_equal),
                               waits=[t8, rd.get("oh1")], inc=sV)
                    rd["lgt"] = ta1
                    pend_oh.append((tt, ta0, ta1))

                wsb_toks = []
                for tc in range(4):
                    P.wait("tensor", tc_dep)
                    P.wait("scalar", tc_dep)
                    for ti in range(4):
                        tt = 4 * tc + ti
                        th, tl = ln_norm(lns, tt, x_sb[:, 1024 * tt:1024 * (tt + 1)],
                                         nh_tm[:, 1024 * tt:1024 * (tt + 1)],
                                         dst_f32=nf32, dst_lo=nl_t,
                                         extra_waits=[tc_dep, rd.get("nh_f")])
                        fm_v = fm.rearrange("p (a c) -> p a c", c=512)[:, :, 128 * ti:128 * (ti + 1)]
                        tok_fm = transpose8g(
                            lambda j, tt=tt: nh_tm[:, 1024 * tt + 128 * j: 1024 * tt + 128 * (j + 1)],
                            (fm_v[:, 0:4], fm_v[:, 4:8]), psTb, th, (("fT", 0), ("fT", 1)))
                        tok_nl = transpose8g(
                            lambda j: nl_t[:, 128 * j:128 * (j + 1)],
                            (nlfm[:, 0:512], nlfm[:, 512:1024]), psTb, tl, (("fT", 0), ("fT", 1)))
                        rd["nh_f"] = tok_nl
                        n_mm = 0
                        N_GATE_MM = 8 if GATE_ONEPASS else 24
                        passes = ((c_gh, lambda a2, fm=fm, ti=ti: fm[:, 512 * a2 + 128 * ti:512 * a2 + 128 * (ti + 1)]),
                                  (c_gl, lambda a2, fm=fm, ti=ti: fm[:, 512 * a2 + 128 * ti:512 * a2 + 128 * (ti + 1)]),
                                  (c_gh, lambda a2, nlfm=nlfm: nlfm[:, 128 * a2:128 * (a2 + 1)]))
                        if GATE_ONEPASS:
                            passes = passes[:1]
                        for gw, rhs_fn in passes:
                            for a2 in range(8):
                                P.op("tensor", lambda e, gw=gw, rhs=rhs_fn(a2), a2=a2, st=(n_mm == 0), sp=(n_mm == N_GATE_MM - 1):
                                     e.matmul(psG, gw[:, 8 * a2:8 * (a2 + 1)], rhs, start=st, stop=sp),
                                     waits=([tok_fm, tok_nl, sV.tok(), sL.tok(), rd.get("psG")] if n_mm == 0 else []),
                                     inc=sT if n_mm == N_GATE_MM - 1 else None)
                                n_mm += 1
                        tgl = P.op("scalar", lambda e, tt=tt: e.copy(
                            out=logits[0:8, 128 * tt:128 * (tt + 1)], in_=psG), waits=[sT.tok()], inc=sS)
                        rd["psG"] = tgl
                        emit_route_tile(tt, tgl)
                    tok_ab = None
                    tsa = None
                    for ib in range(16):
                        ring = ib % 2
                        P.dma("sync", sw_r[ring].rearrange("p (m c) -> p m c", m=2),
                              sw12_scr.transpose([1, 2, 0, 3])[ib], sw_sems[ring],
                              waits=[rd.get(("swr", ring))])
                        for wsel, ps in ((0, psA), (1, psB)):
                            for a2 in range(8):
                                P.op("tensor", lambda e, wsel=wsel, ps=ps, a2=a2, ring=ring, fm=fm: e.matmul(
                                    ps, sw_r[ring][:, wsel * 1024 + 128 * a2: wsel * 1024 + 128 * (a2 + 1)],
                                    fm[:, 512 * a2:512 * (a2 + 1)], start=(a2 == 0), stop=(a2 == 7)),
                                    waits=([sw_sems[ring].tok(), tok_fm, sS.tok(), rd.get(("psAB", wsel))]
                                           + ([rd.get("psA_rt")] if wsel == 0 else []) if a2 == 0 else []),
                                    inc=sT if a2 == 7 else None)
                            if wsel == 0:
                                ta = sT.tok()
                                tsa = P.op("scalar", lambda e: e.activation(
                                    out=a_sb, in_=psA, func=AF.Silu, bias=0.0, scale=1.0),
                                    waits=[ta, rd.get("a_sb")], inc=sS)
                                rd[("psAB", 0)] = tsa
                        tb = sT.tok()
                        tg = P.op("vector", lambda e, ib=ib, g_sb=g_sb: e.tensor_mul(
                            g_sb[:, 512 * ib:512 * (ib + 1)], a_sb, psB), waits=[tb, tsa], inc=sV)
                        rd[("psAB", 1)] = tg
                        rd["a_sb"] = tg
                        rd[("swr", ring)] = tg
                        tok_ab = tg
                    rd["psG"] = tok_ab

                    def emit_z(tc, tok_ab, tok_sS=None):
                        tzl = None
                        for zp in range(2):
                            for ic in range(16):
                                ring = ic % 2
                                P.dma("sync", sw3_r[ring], sw3_scr[ic], sw3_sems[ring],
                                      waits=[rd.get(("sw3r", ring))])
                                for tj in range(2):
                                    for dc in range(2):
                                        w0 = []
                                        if tj == 0 and dc == 0:
                                            w0 = [sw3_sems[ring].tok()]
                                            if ic == 0:
                                                w0 += [tok_ab, rd.get("psZ"),
                                                       sS.tok() if tok_sS is None else tok_sS]
                                        last = P.op("tensor", lambda e, ic=ic, tj=tj, dc=dc, ring=ring, zp=zp, g_sb=g_sb: e.matmul(
                                            psZ2[tj][:, 512 * dc:512 * (dc + 1)],
                                            g_sb[:, 512 * ic + 128 * (2 * zp + tj): 512 * ic + 128 * (2 * zp + tj) + 128],
                                            sw3_r[ring][:, 512 * dc:512 * (dc + 1)],
                                            start=(ic == 0), stop=(ic == 15)),
                                            waits=w0,
                                            inc=sT if (tj == 1 and dc == 1) else None)
                                rd[("sw3r", ring)] = last
                            tz = sT.tok()
                            for tj in range(2):
                                tt = 4 * tc + 2 * zp + tj
                                for dc in range(2):
                                    tzl = P.op("vector", lambda e, tt=tt, tj=tj, dc=dc: e.tensor_add(
                                        x_sb[:, 1024 * tt + 512 * dc: 1024 * tt + 512 * (dc + 1)],
                                        x_sb[:, 1024 * tt + 512 * dc: 1024 * tt + 512 * (dc + 1)],
                                        psZ2[tj][:, 512 * dc:512 * (dc + 1)]), waits=[tz], inc=sV)
                            rd["psZ"] = tzl
                        return tzl

                    if tc < 3:
                        tc_dep = emit_z(tc, tok_ab)
                    else:
                        ab3_tok = tok_ab
                        ab3_sS = sS.tok()
                for job in pend_oh:
                    emit_oh_pair(job)
                pend_oh.clear()
                tsc = P.op("vector", lambda e: e.tensor_tensor_scan(incl, oh_fm, oh_fm, 0.0,
                                                                    ALU.add, ALU.bypass),
                           waits=[sV.tok(), sRt.tok()], inc=sV)
                tge = P.op("vector", lambda e: e.tensor_single_scalar(rrow, c_thresh,
                                                                     incl[:, J - 1:J], ALU.is_lt),
                           waits=[tsc, sL.tok()], inc=sV)
                tnt = P.op("vector", lambda e: e.reduce_sum(rsm[0:8, 4:5], rrow, AX),
                           waits=[tge], inc=sV)
                tt1 = P.op("tensor", lambda e: e.transpose(psRf[0:1, 0:8], rsm[0:8, 4:5],
                                                           c_id8), waits=[tnt, rd.get("psRf")], inc=sT)
                tt2 = P.op("vector", lambda e: e.tensor_copy(tsrow[0:1, 0:8], psRf[0:1, 0:8]),
                           waits=[tt1], inc=sV)
                rd["psRf"] = tt2
                tt3 = P.op("vector", lambda e: e.tensor_tensor_scan(tsrow[0:1, 8:16], tsrow[0:1, 0:8],
                                                                    tsrow[0:1, 0:8], 0.0, ALU.add, ALU.bypass),
                           waits=[tt2], inc=sV)
                tt4 = P.op("vector", lambda e: e.tensor_sub(tsrow[0:1, 0:8], tsrow[0:1, 8:16],
                                                            tsrow[0:1, 0:8]), waits=[tt3], inc=sV)
                tt5 = P.op("scalar", lambda e: e.copy(out=ts_i32, in_=tsrow[0:1, 0:8]),
                           waits=[tt4], inc=sS)
                tt6 = P.op("tensor", lambda e: e.transpose(psRf[0:8, 0:1], tsrow[0:1, 0:8],
                                                           c_id8[0:1, 0:1]), waits=[tt4, rd.get("psRf")], inc=sT)
                tt7 = P.op("vector", lambda e: e.tensor_copy(rsm[0:8, 5:6], psRf[0:8, 0:1]),
                           waits=[tt6], inc=sV)
                rd["psRf"] = tt7
                tt8 = P.op("vector", lambda e: e.tensor_single_scalar(rrow, c_gcmp, rsm[0:8, 5:6],
                                                                      ALU.is_ge), waits=[tt7, tge], inc=sV)
                tt9 = P.op("tensor", lambda e: e.matmul(psRf[0:1, 0:NSEG], c_ones8, rrow,
                                                        start=True, stop=True),
                           waits=[tt8, sL.tok(), rd.get("psRf")], inc=sT)
                tta = P.op("vector", lambda e: e.tensor_scalar(eof_f, psRf[0:1, 0:NSEG], -1.0, None,
                                                               ALU.add), waits=[tt9], inc=sV)
                rd["psRf"] = tta
                ttb = P.op("scalar", lambda e: e.copy(out=eof_i32, in_=eof_f), waits=[tta], inc=sS)
                ttc = P.op("vector", lambda e: e.tensor_scalar(rsm[0:8, 6:7], rsm[0:8, 5:6],
                                                               float(MOE_CHUNK), -1.0, ALU.mult, ALU.add),
                           waits=[tt7], inc=sV)
                ttd = P.op("vector", lambda e: e.scalar_tensor_tensor(incl, incl, rsm[0:8, 6:7],
                                                                      oh_fm, ALU.add, ALU.mult),
                           waits=[ttc, tsc, tge], inc=sV)
                rd["psIdx"] = rd.get("psRf")
                tprev = ttd
                idx_toks = []
                for d2 in range(32):
                    ti1 = P.op("tensor", lambda e, d2=d2: e.matmul(
                        psRf[:, 8 + d2:9 + d2], incl[0:8, 128 * d2:128 * (d2 + 1)], c_ones8,
                        start=True, stop=True), waits=[tprev, rd.get("psIdx")], inc=sT)
                    tix = P.op("scalar", lambda e, d2=d2: e.copy(out=idx_sb[:, d2:d2 + 1],
                                                                 in_=psRf[:, 8 + d2:9 + d2]),
                               waits=[ti1], inc=sS)
                    rd["psIdx"] = tix
                    idx_toks.append(tix)
                tok_idx = sS.tok()
                if DEBUG_DUMP:
                    for nm2, ap2 in (("ts%d" % ridx, ts_i32), ("eof%d" % ridx, eof_i32),
                                     ("idx%d" % ridx, idx_sb), ("lgr%d" % ridx, logits)):
                        dd = nc.dram_tensor("dbg_" + nm2, list(ap2.shape), ap2.dtype,
                                            kind="ExternalOutput").ap()
                        P.dma("sync", dd, ap2, sO, waits=[sV.tok(), sS.tok(), sT.tok()])
                txs = None
                for k2 in range(2):
                    for tt in range(NTILE):
                        ring = (k2 * NTILE + tt) % 2
                        txs = P.op("vector", lambda e, k2=k2, tt=tt, ring=ring: e.tensor_scalar(
                            xs_pp[ring], nh_tm[:, 1024 * tt:1024 * (tt + 1)],
                            w_sb[:, 2 * tt + k2:2 * tt + k2 + 1], None, ALU.mult),
                            waits=[sGs.tok(), wsb_toks[tt]], inc=sV)
                        P.op("gpsimd", lambda e, k2=k2, tt=tt, ring=ring: e.indirect_dma_start(
                            out=xs_scr, out_offset=IndirectOffsetOnAxis(idx_sb[:, 16 * k2 + tt:16 * k2 + tt + 1], 0),
                            in_=xs_pp[ring], in_offset=None),
                            waits=[txs, idx_toks[16 * k2 + tt]], inc=sGs, inc_k=16)
                tok_scat = sGs.tok()
                txs_last = txs
                soff = pool.marks[1]

                def slot_alloc(off, shape, dtype, name):
                    pool.uid += 1
                    return nc.alloc_sbuf_tensor_at(f"pl{pool.uid}_{name}", list(shape), dtype,
                                                   offset=pool.base + off).ap()
                w1s = slot_alloc(soff, [128, 8192], BF16, "w1s")
                w3s = slot_alloc(soff + 16384, [128, 8192], BF16, "w3s")
                w2s = slot_alloc(soff + 32768, [128, 8192], BF16, "w2s")
                bst = slot_alloc(soff + 49152, [128, 24], F32, "bst")
                cbase, vbase = NSEG * ridx, NSEG * ridx
                w1base, w3base, w2base, wbbase = 224 * ridx, 224 * ridx, 224 * ridx, 112 * ridx
                emit_z(3, ab3_tok, tok_sS=ab3_sS)
                e0_waits = [txs_last, tok_idx, (sC, cbase), (sCv, vbase), (sCsc, cbase),
                            (sC1, cbase), (sC3, cbase)]
                for kind, dst, wsem in ((0, w1s, sE1), (1, w3s, sE3), (2, w2s, sE2)):
                    for h in range(2):
                        P.dma("sync", dst[:, 4096 * h:4096 * (h + 1)], wmoe[0, kind, h],
                              wsem, waits=e0_waits)
                P.dma("sync", bst, dr["bstage"][0], sEb, waits=e0_waits)
            pool.release()
            pool.release()

            if "D" not in FFN_PHASES:
                pool.release()
                return
            fence()
            pool.mark()
            pool.off += 49248
            pool.hiwater = max(pool.hiwater, pool.off)
            xg = [pool.alloc([128, 2 * 1024], BF16, "xg0"), pool.alloc([128, 2 * 1024], BF16, "xg1")]
            xg_fm = [pool.alloc([128, 8 * 256], BF16, "xgf0"), pool.alloc([128, 8 * 256], BF16, "xgf1")]
            h_fm = pool.alloc([128, 8 * 256], BF16, "h_fm")
            x3_fm = pool.alloc([128, 8 * 256], F32, "x3_fm")
            y_fm = pool.alloc([128, 8 * 256], BF16, "y_fm")
            y_tm = pool.alloc([128, 2 * 1024], BF16, "y_tm")

            def loader(e, ridx=ridx):
                e.wait_ge(sE1.h, 32 * (ridx + 1))
                e.wait_ge(sE3.h, 32 * (ridx + 1))
                e.wait_ge(sE2.h, 32 * (ridx + 1))
                e.wait_ge(sEb.h, 16 * (ridx + 1))
                with e.register(f"ldr{ridx}") as r1, e.register(f"ldr2_{ridx}") as r2:
                    for ei in range(1, E):
                        if not MOE_STATIC_LOADER:
                            e.reg_load(r1, ts_i32[0:1, ei:ei + 1])
                            e.reg_alu(r2, r1, cbase, ALU.add)
                            e.wait_ge(sC1.h, r2)
                        for h in range(2):
                            e.dma_start(out=w1s[:, 4096 * h:4096 * (h + 1)],
                                        in_=wmoe[ei, 0, h]).then_inc(sW1.h, 16)
                        if not MOE_STATIC_LOADER:
                            e.reg_alu(r2, r1, vbase, ALU.add)
                            e.wait_ge(sCv.h, r2)
                            e.reg_alu(r2, r1, cbase, ALU.add)
                            e.wait_ge(sCsc.h, r2)
                        e.dma_start(out=bst, in_=dr["bstage"][ei]).then_inc(sWb.h, 16)
                        if not MOE_STATIC_LOADER:
                            e.wait_ge(sC3.h, r2)
                        for h in range(2):
                            e.dma_start(out=w3s[:, 4096 * h:4096 * (h + 1)],
                                        in_=wmoe[ei, 1, h]).then_inc(sW3.h, 16)
                        if not MOE_STATIC_LOADER:
                            e.wait_ge(sC.h, r2)
                        for h in range(2):
                            e.dma_start(out=w2s[:, 4096 * h:4096 * (h + 1)],
                                        in_=wmoe[ei, 2, h]).then_inc(sW2.h, 16)
            P.wait("gpsimd", (sems["sS"], sems["sS"].n))
            if not MOE_NO_LOADER:
                P.raw("gpsimd", loader)
            sW1.bump(224)
            sW3.bump(224)
            sW2.bump(224)
            sWb.bump(112)

            pcd = ExitStack()
            psTb = [mkps(pcd, "eT", [128, 512], F32) for _ in range(2)]
            psP = [mkps(pcd, f"eP{i2}", [128, 256], F32) for i2 in range(4)]
            for g in (range(NSEG) if MOE_D_LEVEL >= 0 else []):
                pp = g % 2
                for half in (range(2) if not MOE_NO_GATHER else ()):
                    P.dma("sync", xg[pp][:, 1024 * half:1024 * (half + 1)],
                          xs_scr[MOE_CHUNK * g + 128 * half: MOE_CHUNK * g + 128 * (half + 1), :],
                          sX, waits=[tok_scat, rd.get(("xg", pp))])
                tok_xg = sX.tok()

                def mk_wwait(tag, wsem, esem, base2, mul, g=g, ridx=ridx):
                    def wfn(e, tag=tag, wsem=wsem, esem=esem, base2=base2, mul=mul):
                        e.wait_ge(esem.h, mul * (ridx + 1))
                        if MOE_STATIC_TWAIT:
                            e.wait_ge(wsem.h, base2 + mul * (E - 1))
                            return
                        with e.register(f"tw{tag}_{ridx}_{g}") as r1, \
                                e.register(f"tw{tag}b_{ridx}_{g}") as r2:
                            e.reg_load(r1, eof_i32[0:1, g:g + 1])
                            e.reg_alu(r2, r1, mul, ALU.mult)
                            e.reg_alu(r2, r2, base2, ALU.add)
                            e.wait_ge(wsem.h, r2)
                    return wfn
                if not MOE_NO_WAITS:
                    P.wait("tensor", tok_idx)
                P.wait("vector", rd.get(("xgf", pp)))
                if True:
                    if MOE_D_LEVEL >= 1:
                        xgv = xg_fm[pp].rearrange("p (a c) -> p a c", c=256)
                        transpose8g(lambda j, pp=pp: xg[pp][:, 128 * j:128 * (j + 1)],
                                    (xgv[0:128, 0:4, 0:128], xgv[0:128, 4:8, 0:128]),
                                    psTb, tok_xg, (("eT", 0), ("eT", 1)))
                        tok_tr = transpose8g(lambda j, pp=pp: xg[pp][:, 1024 + 128 * j:1024 + 128 * (j + 1)],
                                             (xgv[0:128, 0:4, 128:256], xgv[0:128, 4:8, 128:256]),
                                             psTb, tok_xg, (("eT", 0), ("eT", 1)))
                        rd[("xg", pp)] = tok_tr
                    else:
                        tok_tr = tok_xg
                    for wsl, dst, kindb, is_h in (((w1s, h_fm, 0, True), (w3s, x3_fm, 8, False)) if MOE_D_LEVEL >= 2 else ()):
                        if not MOE_NO_WAITS:
                            if is_h:
                                P.raw("tensor", mk_wwait("1", sW1, sE1, w1base, 32))
                                P.raw("tensor", mk_wwait("b", sWb, sEb, wbbase, 16))
                            else:
                                P.raw("tensor", mk_wwait("3", sW3, sE3, w3base, 32))
                        for hb in range(8):
                            pps = hb % 4
                            for a2 in range(8):
                                P.op("tensor", lambda e, wsl=wsl, hb=hb, a2=a2, pps=pps, pp=pp: e.matmul(
                                    psP[pps], wsl[:, a2 * 1024 + 128 * hb: a2 * 1024 + 128 * (hb + 1)],
                                    xg_fm[pp][:, 256 * a2:256 * (a2 + 1)], start=(a2 == 0), stop=(a2 == 7)),
                                    waits=([tok_tr, rd.get(("psP", pps))] if a2 == 0 else []),
                                    inc=sT if a2 == 7 else None)
                            tmm = sT.tok()
                            if is_h:
                                tev = P.op("scalar", lambda e, hb=hb, pps=pps, kindb=kindb: e.activation(
                                    out=h_fm[:, 256 * hb:256 * (hb + 1)], in_=psP[pps], func=AF.Silu,
                                    bias=bst[:, kindb + hb:kindb + hb + 1], scale=1.0),
                                    waits=[tmm, rd.get("h_fm")], inc=sS)
                            else:
                                tev = P.op("vector", lambda e, hb=hb, pps=pps, kindb=kindb: e.tensor_scalar(
                                    x3_fm[:, 256 * hb:256 * (hb + 1)], psP[pps],
                                    bst[:, kindb + hb:kindb + hb + 1], None, ALU.add),
                                    waits=[tmm, rd.get("x3_fm")], inc=sV)
                            rd[("psP", pps)] = tev
                        if not MOE_NO_NOPS:
                            P.op("tensor", lambda e: e.nop(), inc=sC1 if is_h else sC3)
                        else:
                            (sC1 if is_h else sC3).bump(1)
                    tok_h = (sS.tok(), sV.tok())
                    rd[("xgf", pp)] = sT.tok()
                    if not MOE_NO_WAITS and MOE_D_LEVEL >= 3:
                        P.raw("tensor", mk_wwait("2", sW2, sE2, w2base, 32))
                    for db in (range(8) if MOE_D_LEVEL >= 3 else ()):
                        pps = db % 4
                        for hc in range(8):
                            P.op("tensor", lambda e, db=db, hc=hc, pps=pps: e.matmul(
                                psP[pps], w2s[:, hc * 1024 + 128 * db: hc * 1024 + 128 * (db + 1)],
                                h_fm[:, 256 * hc:256 * (hc + 1)], start=(hc == 0), stop=(hc == 7)),
                                waits=([tok_h[0], tok_h[1], rd.get(("psP", pps))] if hc == 0 else []),
                                inc=sT if hc == 7 else None)
                        tmm = sT.tok()
                        tev = P.op("vector", lambda e, db=db, pps=pps: e.scalar_tensor_tensor(
                            y_fm[:, 256 * db:256 * (db + 1)], psP[pps], bst[:, 16 + db:16 + db + 1],
                            x3_fm[:, 256 * db:256 * (db + 1)], ALU.add, ALU.mult),
                            waits=[tmm, rd.get("y_fm")], inc=sV)
                        rd[("psP", pps)] = tev
                    rd["h_fm"] = sT.tok()
                    rd["x3_fm"] = sV.tok()
                    tok_y = sV.tok()
                    if not MOE_NO_NOPS:
                        P.op("tensor", lambda e: e.nop(), inc=sC)
                    else:
                        sC.bump(1)
                    ytoks = [None, None]
                    if MOE_D_LEVEL >= 4:
                        for rr in range(2):
                            ytoks[rr] = transpose8g(
                                lambda j, rr=rr: y_fm[:, 256 * j + 128 * rr: 256 * j + 128 * (rr + 1)],
                                (y_tm[:, 1024 * rr:1024 * rr + 512], y_tm[:, 1024 * rr + 512:1024 * (rr + 1)]),
                                psTb, tok_y, (("eT", 0), ("eT", 1)))
                    rd["y_fm"] = sT.tok()
                    for half in (range(2) if MOE_D_LEVEL >= 4 else ()):
                        P.dma("sync", y_scr[MOE_CHUNK * g + 128 * half: MOE_CHUNK * g + 128 * (half + 1), :],
                              y_tm[:, 1024 * half:1024 * (half + 1)], sY, waits=[ytoks[half]])
                    rd["y_tm"] = sY.tok()
                    rd[("xg", pp)] = rd.get(("xg", pp))
                    if not MOE_NO_NOPS:
                        P.op("vector", lambda e: e.nop(), waits=[tok_y], inc=sCv)
                        P.op("scalar", lambda e: e.nop(), waits=[tok_h[0]], inc=sCsc)
            pcd.close()
            tok_wb = sY.tok()

            if "E" not in FFN_PHASES:
                pool.release()
                pool.release()
                return
            yg = [pool.alloc([128, 1024], BF16, "yg0"), pool.alloc([128, 1024], BF16, "yg1")]
            for tt in range(NTILE):
                for k2 in range(2):
                    P.op("gpsimd", lambda e, tt=tt, k2=k2: e.indirect_dma_start(
                        out=yg[k2], out_offset=None, in_=y_scr,
                        in_offset=IndirectOffsetOnAxis(idx_sb[:, 16 * k2 + tt:16 * k2 + tt + 1], 0)),
                        waits=[tok_wb, tok_idx, rd.get(("yg", k2))], inc=sGs, inc_k=16)
                tok_g = sGs.tok()
                ty1 = P.op("vector", lambda e, tt=tt: e.tensor_add(
                    x_sb[:, 1024 * tt:1024 * (tt + 1)], x_sb[:, 1024 * tt:1024 * (tt + 1)], yg[0]),
                    waits=[tok_g], inc=sV)
                ty2 = P.op("vector", lambda e, tt=tt: e.tensor_add(
                    x_sb[:, 1024 * tt:1024 * (tt + 1)], x_sb[:, 1024 * tt:1024 * (tt + 1)], yg[1]),
                    waits=[ty1], inc=sV)
                rd[("yg", 0)] = ty2
                rd[("yg", 1)] = ty2
            pool.release()
            pool.release()


        def sublayer_attn_b(b):
            fence()
            pool.mark()
            r_sb = pool.alloc([128, 2 * T], BF16, "r_sb")
            qp_sb = pool.alloc([128, 8 * T], BF16, "qp_sb")
            os_sb = pool.alloc([128, 8 * T], BF16, "os_sb")
            ctx_sb = pool.alloc([128, 8 * 65], BF16, "ctx_sb")

            pool.mark()
            wdb_sb = pool.alloc([128, 2048], BF16, "wdb")
            wqb_sb = pool.alloc([128, 8192], BF16, "wqb")
            fm2 = [pool.alloc([128, 8 * 512], BF16, "bfmA"), pool.alloc([128, 8 * 512], BF16, "bfmB")]
            nh4 = pool.alloc([128, 2048], BF16, "bnh4")
            esc = [pool.alloc([128, 512], F32, "esc0"), pool.alloc([128, 512], F32, "esc1"),
                   pool.alloc([128, 512], F32, "esc2")]
            lns = mk_ln(pool)
            for tt in range(NTILE):
                ln_stats_tile(lns, tt, x_sb[:, 1024 * tt:1024 * (tt + 1)],
                              extra_waits=[sX.tok()] if tt == 0 else ())
            ln_finish(lns)
            P.dma("sync", wdb_sb, wdb_c, sL, waits=[tok_attn["wdb"]])
            for h in range(2):
                P.dma("sync", wqb_sb[:, h * 4096:(h + 1) * 4096], wat_c["wqb"][h], sL,
                      waits=[tok_attn["wqb"]])
            tok_w = sL.tok()
            with ExitStack() as pc1:
                psTb = [mkps(pc1, "bT", [128, 512], F32) for _ in range(2)]
                psD = mkps(pc1, "bD", [128, 512], F32)
                psQb = [mkps(pc1, f"bQ{i3}", [128, 512], F32) for i3 in range(3)]

                def emit_trs_b(tc):
                    par = tc % 2
                    fm = fm2[par]
                    P.wait("vector", rd.get(("fmwb", par)))
                    tok_fm = None
                    for ti in range(4):
                        tt = 4 * tc + ti
                        np_ = ti % 2
                        th, _ = ln_norm(lns, tt, x_sb[:, 1024 * tt:1024 * (tt + 1)],
                                        nh4[:, 1024 * np_:1024 * (np_ + 1)],
                                        extra_waits=[rd.get(("bnh4", np_))])
                        fm_v = fm.rearrange("p (a c) -> p a c", c=512)[:, :, 128 * ti:128 * (ti + 1)]
                        tok_fm = transpose8g(
                            lambda j, np_=np_, nh4=nh4: nh4[:, 1024 * np_ + 128 * j: 1024 * np_ + 128 * (j + 1)],
                            (fm_v[:, 0:4], fm_v[:, 4:8]), psTb, th, (("bT", 0), ("bT", 1)))
                        rd[("bnh4", np_)] = sT.tok()
                    return tok_fm

                def emit_mms_b(tc, tok_fm):
                    par = tc % 2
                    fm = fm2[par]
                    for rb in range(2):
                        for a in range(8):
                            P.op("tensor", lambda e, rb=rb, a=a, fm=fm: e.matmul(
                                psD, wdb_sb[:, a * 256 + 128 * rb: a * 256 + 128 * (rb + 1)],
                                fm[:, 512 * a:512 * (a + 1)], start=(a == 0), stop=(a == 7)),
                                waits=([tok_fm, tok_w, rd.get("psDb")] if a == 0 else []),
                                inc=sT if a == 7 else None)
                        tmm = sT.tok()
                        tev = P.op("vector", lambda e, rb=rb, tc=tc: e.tensor_scalar(
                            r_sb[:, rb * T + 512 * tc: rb * T + 512 * (tc + 1)], psD,
                            c_battn[:, 24 + rb:25 + rb], None, ALU.add),
                            waits=[tmm, sL.tok()], inc=sV)
                        rd["psDb"] = tev
                    for ob in range(8):
                        ps = psQb[ob % 3]
                        for a in range(8):
                            P.op("tensor", lambda e, ob=ob, a=a, ps=ps, fm=fm: e.matmul(
                                ps, wqb_sb[:, a * 1024 + 128 * ob: a * 1024 + 128 * (ob + 1)],
                                fm[:, 512 * a:512 * (a + 1)], start=(a == 0), stop=(a == 7)),
                                waits=([tok_fm, tok_w, rd.get(("psQb", ob % 3))] if a == 0 else []),
                                inc=sT if a == 7 else None)
                        tmm = sT.tok()
                        t1 = P.op("vector", lambda e, ps=ps, ob=ob: e.tensor_scalar(
                            esc[0], ps, c_battn[:, 16 + ob:17 + ob], 0.0, ALU.add, ALU.min),
                            waits=[tmm, sL.tok(), rd.get("esc")], inc=sV)
                        t2 = P.op("vector", lambda e, ps=ps, ob=ob: e.tensor_scalar(
                            esc[1], ps, c_battn[:, 16 + ob:17 + ob], 0.0, ALU.add, ALU.max),
                            waits=[tmm, t1], inc=sV)
                        rd[("psQb", ob % 3)] = t2
                        t3 = P.op("scalar", lambda e: e.activation(out=esc[2], in_=esc[0],
                                                                   func=AF.Exp, bias=0.0, scale=1.0),
                                  waits=[t1, rd.get("esc2")], inc=sS)
                        t4 = P.op("vector", lambda e, ob=ob, tc=tc: e.tensor_add(
                            qp_sb[:, ob * T + 512 * tc: ob * T + 512 * (tc + 1)], esc[2], esc[1]),
                            waits=[t3, t2], inc=sV)
                        rd["esc"] = t4
                        rd["esc2"] = t4
                    rd[("fmwb", par)] = sT.tok()

                toks_b = {}
                for tc in range(4):
                    toks_b[tc] = emit_trs_b(tc)
                    if tc > 0:
                        emit_mms_b(tc - 1, toks_b[tc - 1])
                emit_mms_b(3, toks_b[3])
            pool.release()

            pool.mark()
            wub_sb = pool.alloc([128, 4096], BF16, "wub")
            kext = pool.alloc([128, 1040], BF16, "kext")
            vext = pool.alloc([128, 1040], BF16, "vext")
            ksc = [pool.alloc([128, 256], F32, "ksc0"), pool.alloc([128, 256], F32, "ksc1"),
                   pool.alloc([128, 256], F32, "ksc2")]
            P.dma("sync", wub_sb, wub_c, sL,
                  waits=[tok_attn["wub"], sT.tok(), sV.tok(), sS.tok()])
            tok_w = sL.tok()
            with ExitStack() as pc2:
                psKV = [mkps(pc2, f"bKV{i3}", [128, 512], F32) for i3 in range(4)]
                psCtx = mkps(pc2, "bCtx", [65, 2048], F32)
                kext4 = kext.rearrange("p (h j) -> p h j", j=65)
                vext4 = vext.rearrange("p (h j) -> p h j", j=65)
                tko = P.op("vector", lambda e: e.memset(kext4[:, :, 64:65], 1.0), inc=sV)
                tvo = P.op("vector", lambda e: e.memset(vext4[:, :, 64:65], 1.0), waits=[tko], inc=sV)
                rd["kvone"] = tvo
                for tt in range(NTILE):
                    for oc in range(4):
                        ps = psKV[oc % 4]
                        for ch in range(2):
                            P.op("tensor", lambda e, tt=tt, oc=oc, ch=ch, ps=ps: e.matmul(
                                ps, r_sb[:, ch * T + 128 * tt: ch * T + 128 * (tt + 1)],
                                wub_sb[:, ch * 2048 + 512 * oc: ch * 2048 + 512 * (oc + 1)],
                                start=(ch == 0), stop=False),
                                waits=([sV.tok(), tok_w, rd.get(("psKV", oc % 4))] if ch == 0 else []))
                        P.op("tensor", lambda e, oc=oc, ps=ps: e.matmul(
                            ps, c_onesb[32:33, :], c_rows[32:33, 512 * oc:512 * (oc + 1)],
                            start=False, stop=True), inc=sT)
                        tmm = sT.tok()
                        pk = bass.AP(ps.tensor, ps.offset, [ps.ap[0], [128, 4], [1, 64]])
                        pv = bass.AP(ps.tensor, ps.offset + 64, [ps.ap[0], [128, 4], [1, 64]])
                        k1 = P.op("vector", lambda e, pk=pk: e.tensor_scalar(
                            ksc[0].rearrange("p (h j) -> p h j", j=64), pk, 0.0, None, ALU.min),
                            waits=[tmm, rd.get("ksc")], inc=sV)
                        k2 = P.op("vector", lambda e, pk=pk: e.tensor_scalar(
                            ksc[1].rearrange("p (h j) -> p h j", j=64), pk, 0.0, None, ALU.max),
                            waits=[tmm, k1], inc=sV)
                        rd[("psKV", oc % 4)] = k2
                        k3 = P.op("scalar", lambda e: e.activation(out=ksc[2], in_=ksc[0],
                                                                   func=AF.Exp, bias=0.0, scale=1.0),
                                  waits=[k1, rd.get("ksc2")], inc=sS)
                        k4 = P.op("vector", lambda e, oc=oc: e.tensor_add(
                            kext4[:, 4 * oc:4 * (oc + 1), 0:64],
                            ksc[2].rearrange("p (h j) -> p h j", j=64),
                            ksc[1].rearrange("p (h j) -> p h j", j=64)),
                            waits=[k3, k2, tvo, rd.get("kext")], inc=sV)
                        kv = P.op("vector", lambda e, pv=pv, oc=oc: e.tensor_copy(
                            vext4[:, 4 * oc:4 * (oc + 1), 0:64], pv),
                            waits=[tmm, tvo, rd.get("kext")], inc=sV)
                        rd["ksc"] = k4
                        rd["ksc2"] = k4
                        rd[("kvoc", oc)] = (k4, kv)
                    toks = [rd[("kvoc", o2)] for o2 in range(4)]
                    tok_kv = [t for pair in toks for t in pair]
                    for h in range(NH):
                        P.op("tensor", lambda e, h=h, tt=tt: e.matmul(
                            psCtx[:, h * 128:h * 128 + 65], kext[:, h * 65:(h + 1) * 65],
                            vext[:, h * 65:(h + 1) * 65],
                            start=(tt == 0 and h % 4 == 0), stop=(tt == NTILE - 1 and h % 4 == 3)),
                            waits=(tok_kv if h == 0 else []), inc=sT if h == NH - 1 else None)
                    rd["kext"] = sT.tok()
                tok_ctx_mm = sT.tok()
                for h in range(NH):
                    tev = P.op("vector", lambda e, h=h: e.tensor_copy(
                        ctx_sb[64 * (h % 2):64 * (h % 2) + 64, (h // 2) * 65:(h // 2) * 65 + 65],
                        psCtx[0:64, h * 128:h * 128 + 65]), waits=[tok_ctx_mm], inc=sV)
                tok_ctx = sV.tok()
            pool.release()

            pool.mark()
            rlr = [pool.alloc([1, 512], F32, "brl0"), pool.alloc([1, 512], F32, "brl1")]
            rlbc = [pool.alloc([64, 512], F32, "brlbc0"), pool.alloc([64, 512], F32, "brlbc1")]
            with ExitStack() as pc3:
                psO = [mkps(pc3, f"bO{i3}", [65, 512], F32) for i3 in range(4)]
                psBc2 = [mkps(pc3, "bBcA", [64, 512], F32), mkps(pc3, "bBcB", [64, 512], F32)]

                def emit_bnormA(job):
                    tok_mm, h2, tc2, poi, ri = job
                    tr1 = P.op("vector", lambda e, poi=poi, ri=ri: e.tensor_scalar(
                        rlr[ri], psO[poi][64:65, :], 1e-6, None, ALU.add),
                        waits=[tok_mm, rd.get(("brlr", ri))], inc=sV)
                    tr2 = P.op("vector", lambda e, ri=ri: e.reciprocal(rlr[ri], rlr[ri]),
                               waits=[tr1], inc=sV)
                    return tr2

                def emit_bnormB(job, tr2):
                    tok_mm, h2, tc2, poi, ri = job
                    base2 = 64 * (h2 % 2)
                    tb1 = P.op("tensor", lambda e, ri=ri: e.matmul(psBc2[ri], c_ones[0:1, 0:64],
                                                                   rlr[ri], start=True, stop=True),
                               waits=[tr2, rd.get(("psBcb", ri))], inc=sT)
                    rd[("brlr", ri)] = tb1
                    tb2 = P.op("scalar", lambda e, ri=ri: e.copy(out=rlbc[ri], in_=psBc2[ri]),
                               waits=[tb1, rd.get(("brlbc", ri))], inc=sS)
                    rd[("psBcb", ri)] = tb2
                    ty = P.op("vector", lambda e, h=h2, tc=tc2, base=base2, poi=poi, ri=ri: e.tensor_mul(
                        os_sb[base:base + 64, (h // 2) * T + 512 * tc: (h // 2) * T + 512 * (tc + 1)],
                        psO[poi][0:64, :], rlbc[ri]), waits=[tb2], inc=sV)
                    rd[("psOb", poi)] = ty
                    rd[("brlbc", ri)] = ty

                pendA, pendB = [], []
                ctr3 = 0
                for h in range(NH):
                    base = 64 * (h % 2)
                    for tc in range(4):
                        poi, ri = ctr3 % 4, ctr3 % 2
                        tmm = P.op("tensor", lambda e, h=h, tc=tc, base=base, poi=poi: e.matmul(
                            psO[poi], ctx_sb[base:base + 64, (h // 2) * 65:(h // 2) * 65 + 65],
                            qp_sb[base:base + 64, (h // 2) * T + 512 * tc: (h // 2) * T + 512 * (tc + 1)],
                            start=True, stop=True, tile_position=(base, 0)),
                            waits=[tok_ctx, rd.get(("psOb", poi))], inc=sT)
                        pendA.append((tmm, h, tc, poi, ri))
                        ctr3 += 1
                        if len(pendA) > 1:
                            jobA = pendA.pop(0)
                            pendB.append((jobA, emit_bnormA(jobA)))
                        if len(pendB) > 1:
                            jb, tr2b = pendB.pop(0)
                            emit_bnormB(jb, tr2b)
                for jobA in pendA:
                    pendB.append((jobA, emit_bnormA(jobA)))
                for jb, tr2b in pendB:
                    emit_bnormB(jb, tr2b)
            pool.release()

            pool.mark()
            wob_sb = pool.alloc([128, 8192], BF16, "wob")
            for h in range(2):
                P.dma("sync", wob_sb[:, h * 4096:(h + 1) * 4096], wat_c["wob"][h], sL,
                      waits=[tok_attn["wob"], sT.tok(), sV.tok(), sS.tok()])
            tok_w = sL.tok()
            with ExitStack() as pc4:
                psW = [mkps(pc4, "bW0", [128, 512], F32), mkps(pc4, "bW1", [128, 512], F32)]
                x_war = sV.tok()
                for tt in range(NTILE):
                    for oc in range(2):
                        pw_i = (2 * tt + oc) % 2
                        psw = psW[pw_i]
                        for ch in range(8):
                            P.op("tensor", lambda e, tt=tt, oc=oc, ch=ch, psw=psw: e.matmul(
                                psw, os_sb[:, ch * T + 128 * tt: ch * T + 128 * (tt + 1)],
                                wob_sb[:, ch * 1024 + 512 * oc: ch * 1024 + 512 * (oc + 1)],
                                start=(ch == 0), stop=False),
                                waits=([sV.tok(), tok_w, rd.get(("psWb", pw_i))] if ch == 0 else []))
                        P.op("tensor", lambda e, oc=oc, psw=psw: e.matmul(
                            psw, c_onesb[64:65, :], c_rows[64:65, 512 * oc:512 * (oc + 1)],
                            start=False, stop=True), inc=sT)
                        tmm = sT.tok()
                        tx = P.op("vector", lambda e, tt=tt, oc=oc, psw=psw: e.tensor_add(
                            x_sb[:, 1024 * tt + 512 * oc: 1024 * tt + 512 * (oc + 1)],
                            x_sb[:, 1024 * tt + 512 * oc: 1024 * tt + 512 * (oc + 1)], psw),
                            waits=[tmm], inc=sV)
                        rd[("psWb", pw_i)] = tx
            pool.release()
            pool.release()

        rctr = [0]
        for b in range(B_RUN):
            if b > 0:
                P.dma("sync", x_sb3, x_r[b], sX,
                      waits=[rd.get("x_stored"), sV.tok()])
            if N_SUBLAYERS >= 1:
                sublayer_attn_a(b, b == 0)
            if N_SUBLAYERS >= 2:
                sublayer_ffn(b, rctr[0]); rctr[0] += 1
            if N_SUBLAYERS >= 3:
                sublayer_attn_b(b)
            if N_SUBLAYERS >= 4:
                sublayer_ffn(b, rctr[0]); rctr[0] += 1
            tst = P.dma("sync", out_r[b], x_sb3, sO, waits=[sV.tok(), sT.tok(), sS.tok()])
            rd["x_stored"] = tst
        if DEBUG_DUMP:
            for nm2, ap2 in list(DBG.items()):
                dt2 = ap2.dtype
                dd = nc.dram_tensor("dbg_" + nm2, list(ap2.shape), dt2,
                                    kind="ExternalOutput").ap()
                P.dma("sync", dd, ap2, sO, waits=[sV.tok(), sT.tok(), sS.tok()])
        P.wait("sync", sO.tok())

        with nc.Block() as block:
            @block.tensor
            def _(e):
                for f in P.streams["tensor"]:
                    f(e)

            @block.vector
            def _(e):
                for f in P.streams["vector"]:
                    f(e)

            @block.scalar
            def _(e):
                for f in P.streams["scalar"]:
                    f(e)

            @block.gpsimd
            def _(e):
                for f in P.streams["gpsimd"]:
                    f(e)

            @block.sync
            def _(e):
                for f in P.streams["sync"]:
                    f(e)

    import sys
    if "--semdump" in sys.argv or True:
        try:
            print("SEMS:", {nm: sems[nm].n for nm in sems}, file=sys.stderr)
        except Exception:
            pass
    return nc


def _host_inputs(ins, core):
    d = {}
    d["x_in"] = np.ascontiguousarray(np.asarray(ins["L0_attn_a_x"])[2 * core:2 * core + 2])

    def T_(a):
        return np.ascontiguousarray(np.asarray(a).T)

    perm = np.concatenate([np.arange(0, D, 2), np.arange(1, D, 2)])
    d["wqaTp"] = np.ascontiguousarray(np.asarray(ins["L0_attn_a_Wq"]).T[:, perm])
    d["wkaTp"] = np.ascontiguousarray(np.asarray(ins["L0_attn_a_Wk"]).T[:, perm])
    d["wvaT"] = T_(ins["L0_attn_a_Wv"])
    d["woaT"] = T_(ins["L0_attn_a_Wo"])
    bq = np.asarray(ins["L0_attn_a_bq"])[perm]
    bk = np.asarray(ins["L0_attn_a_bk"])[perm]
    d["bqe_s"] = np.ascontiguousarray(bq[:512].reshape(4, 128).T)
    d["bqo_s"] = np.ascontiguousarray(bq[512:].reshape(4, 128).T)
    d["bke_s"] = np.ascontiguousarray(bk[:512].reshape(4, 128).T)
    d["bko_s"] = np.ascontiguousarray(bk[512:].reshape(4, 128).T)
    d["bv_row"] = np.asarray(ins["L0_attn_a_bv"]).reshape(1, D)
    d["boa_row"] = np.asarray(ins["L0_attn_a_bo"]).reshape(1, D)
    d["wqb"] = np.ascontiguousarray(np.asarray(ins["L0_attn_b_Wq"]))
    d["wdb"] = np.ascontiguousarray(np.asarray(ins["L0_attn_b_Wd"]))
    d["wub"] = np.ascontiguousarray(np.asarray(ins["L0_attn_b_Wu"]))
    d["wob"] = np.ascontiguousarray(np.asarray(ins["L0_attn_b_Wo"]))
    d["bq2_s"] = np.ascontiguousarray(np.asarray(ins["L0_attn_b_bq"]).reshape(8, 128).T)
    d["bd_s"] = np.ascontiguousarray(np.asarray(ins["L0_attn_b_bd"]).reshape(2, 128).T)
    d["bu_row"] = np.asarray(ins["L0_attn_b_bu"]).reshape(1, 2048)
    d["bob_row"] = np.asarray(ins["L0_attn_b_bo"]).reshape(1, D)
    d["gwT"] = T_(ins["L0_ffn_gate_w"])
    d["w1T"] = np.ascontiguousarray(np.asarray(ins["L0_ffn_w1"]).transpose(0, 2, 1))
    d["w2T"] = np.ascontiguousarray(np.asarray(ins["L0_ffn_w2"]).transpose(0, 2, 1))
    d["w3T"] = np.ascontiguousarray(np.asarray(ins["L0_ffn_w3"]).transpose(0, 2, 1))
    bst = np.zeros((E, 128, 24), np.float32)
    for kind, nm in ((0, "L0_ffn_b1"), (1, "L0_ffn_b3"), (2, "L0_ffn_b2")):
        bb = np.asarray(ins[nm])
        bst[:, :, kind * 8:(kind + 1) * 8] = bb.reshape(E, 8, 128).transpose(0, 2, 1)
    d["bstage"] = bst
    d["sw1T"] = T_(ins["L0_ffn_sw1"])
    d["sw2T"] = T_(ins["L0_ffn_sw2"])
    d["sw3T"] = T_(ins["L0_ffn_sw3"])
    d["id128"] = np.eye(128, dtype=BF)
    d["id128f"] = np.eye(128, dtype=np.float32)
    d["id8"] = np.eye(8, dtype=np.float32)
    onr = np.zeros((65, 128), np.float32)
    onr[0] = 1.0; onr[32] = 1.0; onr[64] = 1.0
    d["ones_row"] = onr
    d["ones8"] = np.ones((8, 1), np.float32)
    bd4 = np.zeros((128, 4), np.float32)
    for m in range(4):
        bd4[32 * m:32 * (m + 1), m] = 1.0
    d["bd4"] = bd4.astype(BF)
    d["mask2"] = np.where(np.arange(128)[:, None] <= np.arange(128)[None, :],
                          np.float32(0.0), np.float32(-1e10)).astype(np.float32)
    d["thresh"] = np.broadcast_to((np.arange(NSEG) * MOE_CHUNK).astype(np.float32), (8, NSEG)).copy()
    d["gcmp"] = np.broadcast_to(np.arange(NSEG).astype(np.float32), (8, NSEG)).copy()
    tpos = np.arange(1, T + 1, dtype=np.float32)[:, None]
    theta = (1.0 / 10000.0) ** (2.0 * np.arange(32, dtype=np.float32) / 64.0)
    ang = tpos * theta
    rows = np.arange(128) % 32
    d["cos4"] = np.cos(ang)[:, rows].T.astype(BF)
    d["sin4"] = np.sin(ang)[:, rows].T.astype(BF)
    return d


def kernel(**inputs):
    nc = build_nc()
    in_maps = [_host_inputs(inputs, c) for c in range(8)]
    res = run_bass_kernel_spmd(nc, in_maps, core_ids=list(range(8)))
    outs = [np.asarray(r["out"]) for r in res.results]
    return np.concatenate(outs, axis=0).reshape(16, T, D).astype(np.float32)


if __name__ == "__main__":
    nc = build_nc()
    print("built ok, instructions:", nc.n_instructions())


# version 25
# speedup vs baseline: 1.0121x; 1.0075x over previous
import itertools
import numpy as np
import ml_dtypes
from contextlib import ExitStack

import concourse.bass as bass
import concourse.mybir as mybir
from concourse.bass_utils import run_bass_kernel_spmd
from concourse.bass_core_rust import IndirectOffsetOnAxis

F32 = mybir.dt.float32
BF16 = mybir.dt.bfloat16
I32 = mybir.dt.int32
AF = mybir.ActivationFunctionType
ALU = mybir.AluOpType
AX = mybir.AxisListType.X
BF = ml_dtypes.bfloat16

B_PER_CORE = 2
T = 2048
D = 1024
NH = 16
NTILE = T // 128
E = 8
MOE_CHUNK = 256
NSEG = 24
EPS = 1e-5

N_SUBLAYERS = 4
B_RUN = B_PER_CORE
DEBUG_DUMP = False
DBG = {}

ENGINES = ("tensor", "vector", "scalar", "gpsimd", "sync")


class Sem:
    __slots__ = ("h", "name", "n")

    def __init__(self, handle, name):
        self.h, self.name, self.n = handle, name, 0

    def bump(self, k):
        self.n += k
        return self.n

    def tok(self):
        return (self, self.n)


class Prog:
    def __init__(self, nc):
        self.nc = nc
        self.streams = {e: [] for e in ENGINES}

    def raw(self, eng, fn):
        self.streams[eng].append(fn)

    def op(self, eng, emit, waits=(), inc=None, inc_k=1):
        waits = tuple(w for w in waits if w is not None)
        for s, v in waits:
            assert v <= s.n, f"wait on future token {s.name}:{v} > {s.n}"
        if inc is not None:
            inc.bump(inc_k)

        def fn(e, waits=waits, emit=emit, inc=inc, inc_k=inc_k):
            for s, v in waits:
                if v > 0:
                    e.wait_ge(s.h, v)
            ins = emit(e)
            if inc is not None:
                ins.then_inc(inc.h, inc_k)
        self.raw(eng, fn)
        return inc.tok() if inc is not None else None

    def dma(self, q, out_ap, in_ap, sem, waits=(), **kw):
        return self.op(q, lambda e: e.dma_start(out=out_ap, in_=in_ap, **kw),
                       waits=waits, inc=sem, inc_k=16)

    def wait(self, eng, *toks):
        toks = tuple(t for t in toks if t is not None)
        for s, v in toks:
            assert v <= s.n

        def fn(e, toks=toks):
            for s, v in toks:
                if v > 0:
                    e.wait_ge(s.h, v)
        self.raw(eng, fn)


class Pool:

    def __init__(self, nc, base_off, nbytes):
        self.nc, self.base, self.size = nc, base_off, nbytes
        self.off, self.marks, self.uid, self.hiwater = 0, [], 0, 0

    def alloc(self, shape, dtype, name=None):
        isz = {F32: 4, BF16: 2, I32: 4}[dtype]
        nb = shape[1] * isz
        self.off = (self.off + 31) // 32 * 32
        o = self.off
        self.off += nb
        self.hiwater = max(self.hiwater, self.off)
        assert self.off <= self.size, f"pool overflow at {name}: {self.off} > {self.size}"
        self.uid += 1
        h = self.nc.alloc_sbuf_tensor_at(f"pl{self.uid}_{name or 'b'}", list(shape), dtype,
                                         offset=self.base + o)
        return h.ap()

    def mark(self):
        self.marks.append(self.off)

    def release(self):
        self.off = self.marks.pop()


RACE_DETECT = True
GATE_ONEPASS = False
MOE_STATIC_LOADER = False
MOE_STATIC_TWAIT = False
MOE_D_LEVEL = 4
MOE_NO_LOADER = False
MOE_NO_GATHER = False
MOE_NO_NOPS = False
MOE_LOADER_SC_ONLY = False
MOE_NO_WAITS = False
FFN_PHASES = "ABCDE"


def build_nc():
    nc = bass.Bass("TRN2", target_bir_lowering=False, debug=False,
                   detect_race_conditions=RACE_DETECT)
    dr = {}

    def din(name, shape, dtype=F32):
        dr[name] = nc.dram_tensor(name, list(shape), dtype, kind="ExternalInput").ap()
        return dr[name]

    x_in = din("x_in", [B_PER_CORE, T, D])
    out = nc.dram_tensor("out", [B_PER_CORE, T, D], F32, kind="ExternalOutput").ap()

    din("wqaTp", [D, D]); din("wkaTp", [D, D]); din("wvaT", [D, D]); din("woaT", [D, D])
    din("bqe_s", [128, 4]); din("bqo_s", [128, 4]); din("bke_s", [128, 4]); din("bko_s", [128, 4])
    din("bv_row", [1, D]); din("boa_row", [1, D])
    din("wqb", [D, D]); din("wdb", [D, 256]); din("wub", [256, 2048]); din("wob", [D, D])
    din("bq2_s", [128, 8]); din("bd_s", [128, 2]); din("bu_row", [1, 2048]); din("bob_row", [1, D])
    din("gwT", [D, 8])
    din("w1T", [E, D, D]); din("w2T", [E, D, D]); din("w3T", [E, D, D])
    din("bstage", [E, 128, 24])
    din("sw1T", [D, 2048]); din("sw2T", [D, 2048]); din("sw3T", [2048, D])
    din("id128", [128, 128], BF16)
    din("id128f", [128, 128])
    din("id8", [8, 8])
    din("ones_row", [65, 128])
    din("ones8", [8, 1])
    din("bd4", [128, 4], BF16)
    din("mask2", [128, 128])
    din("thresh", [8, NSEG])
    din("gcmp", [8, NSEG])
    din("cos4", [128, T], BF16)
    din("sin4", [128, T], BF16)

    xs_scr = nc.dram_tensor("xs_scr", [NSEG * MOE_CHUNK, D], BF16, kind="Internal").ap()
    y_scr = nc.dram_tensor("y_scr", [NSEG * MOE_CHUNK, D], BF16, kind="Internal").ap()
    wmoe = nc.dram_tensor("wmoe", [E, 3, 2, 128, 4096], BF16, kind="Internal").ap()
    sw12_scr = nc.dram_tensor("sw12_scr", [2, 16, 128, 1024], BF16, kind="Internal").ap()
    sw3_scr = nc.dram_tensor("sw3_scr", [16, 128, 1024], BF16, kind="Internal").ap()
    wat_c = {nm: nc.dram_tensor(nm + "_c", [2, 128, 4096], BF16, kind="Internal").ap()
             for nm in ("wqaTp", "wkaTp", "wvaT", "woaT", "wqb", "wob")}
    wdb_c = nc.dram_tensor("wdb_c", [128, 2048], BF16, kind="Internal").ap()
    wub_c = nc.dram_tensor("wub_c", [128, 4096], BF16, kind="Internal").ap()

    with ExitStack() as ctx:
        def sbuf(name, shape, dtype):
            return ctx.enter_context(nc.sbuf_tensor(name, list(shape), dtype)).ap()

        x_sb = sbuf("x_sb", [128, NTILE * D], F32)
        c_id128 = sbuf("c_id128", [128, 128], BF16)
        c_id128f = sbuf("c_id128f", [128, 128], F32)
        c_id8 = sbuf("c_id8", [8, 8], F32)
        c_ones = sbuf("c_ones", [65, 128], F32)
        c_ones8 = sbuf("c_ones8", [8, 1], F32)
        c_bd4 = sbuf("c_bd4", [128, 4], BF16)
        c_mask2 = sbuf("c_mask2", [128, 128], F32)
        c_thresh = sbuf("c_thresh", [8, NSEG], F32)
        c_gcmp = sbuf("c_gcmp", [8, NSEG], F32)
        c_gw = sbuf("c_gw", [128, 64], F32)
        c_gh = sbuf("c_gh", [128, 64], BF16)
        c_gl = sbuf("c_gl", [128, 64], BF16)
        c_battn = sbuf("c_battn", [128, 26], F32)
        c_rows = sbuf("c_rows", [128, 2048], BF16)
        c_onesb = sbuf("c_onesb", [65, 128], BF16)
        tc_bb = [sbuf("tc_bb0", [128, 2048], BF16), sbuf("tc_bb1", [128, 2048], BF16)]

        rem = nc.sbuf_bytes_remaining
        pool_bytes = (rem - 3072) // 32 * 32
        pool_h = ctx.enter_context(nc.sbuf_tensor("pool", [128, pool_bytes], mybir.dt.uint8))
        pool = Pool(nc, nc.lookup_mloc(pool_h).addr, pool_bytes)

        sems = {nm: Sem(ctx.enter_context(nc.semaphore(nm)), nm)
                for nm in ("sL", "sG", "sT", "sV", "sS", "sO", "sW1", "sW3", "sW2", "sWb",
                           "sC", "sCv", "sC1", "sC3", "sRt",
                           "sE1", "sE3", "sE2", "sEb",
                           "sX", "sGs", "sTci", "sTco", "sCs", "sSw", "sSwB", "sSw3", "sSw3B", "sY", "sCsc", "sM")}
        sL, sG, sT, sV, sS, sO = (sems[k] for k in ("sL", "sG", "sT", "sV", "sS", "sO"))
        sW1, sW3, sW2, sWb = sems["sW1"], sems["sW3"], sems["sW2"], sems["sWb"]
        sC, sCv = sems["sC"], sems["sCv"]
        sC1, sC3 = sems["sC1"], sems["sC3"]
        sRt = sems["sRt"]
        sE1, sE3, sE2, sEb = sems["sE1"], sems["sE3"], sems["sE2"], sems["sEb"]
        sX, sGs, sTci, sTco = (sems[k] for k in ("sX", "sGs", "sTci", "sTco"))
        sCs = sems["sCs"]
        sSw, sY = sems["sSw"], sems["sY"]
        sCsc = sems["sCsc"]
        sw_sems = [sems["sSw"], sems["sSwB"]]
        sw3_sems = [sems["sSw3"], sems["sSw3B"]]

        P = Prog(nc)
        rd = {}

        def fence():
            toks = [sm.tok() for sm in (sems["sT"], sems["sV"], sems["sS"], sems["sG"],
                                        sems["sL"], sems["sCs"], sems["sX"], sems["sGs"],
                                        sems["sW1"], sems["sW3"], sems["sW2"], sems["sWb"],
                                        sems["sO"], sems["sC"], sems["sCv"],
                                        sems["sC1"], sems["sC3"], sems["sRt"],
                                        sems["sE1"], sems["sE3"], sems["sE2"], sems["sEb"],
                                        sems["sSw"], sems["sSwB"], sems["sSw3"],
                                        sems["sSw3B"], sems["sY"], sems["sCsc"])]
            for eng2 in ENGINES:
                P.wait(eng2, *toks)
        _uid = itertools.count()

        def mkps(stack, nm, shape, dt2):
            return stack.enter_context(
                nc.psum_tensor(f"{nm}_{next(_uid)}", list(shape), dt2)).ap()

        x_r = x_in.rearrange("b (tl p) c -> b p tl c", p=128)
        out_r = out.rearrange("b (tl p) c -> b p tl c", p=128)
        x_sb3 = x_sb.rearrange("p (tl c) -> p tl c", c=D)
        P.dma("sync", x_sb3, x_r[0], sX)
        for dst, src in [(c_id128, dr["id128"]), (c_id128f, dr["id128f"]),
                         (c_id8, dr["id8"]), (c_ones, dr["ones_row"]),
                         (c_ones8, dr["ones8"]), (c_bd4, dr["bd4"]), (c_mask2, dr["mask2"]),
                         (c_thresh, dr["thresh"]), (c_gcmp, dr["gcmp"])]:
            P.dma("sync", dst, src, sL)
        P.dma("sync", c_gw.rearrange("p (ch e) -> p ch e", e=8), dr["gwT"].rearrange("(ch p) e -> p ch e", p=128), sL)
        for col, nm, w in [(0, "bqe_s", 4), (4, "bqo_s", 4), (8, "bke_s", 4), (12, "bko_s", 4),
                           (16, "bq2_s", 8), (24, "bd_s", 2)]:
            P.dma("sync", c_battn[:, col:col + w], dr[nm], sL)
        for r0, o, nm, w in [(0, 0, "bv_row", D), (0, 1024, "boa_row", D),
                             (32, 0, "bu_row", 2048), (64, 0, "bob_row", D)]:
            P.dma("gpsimd", c_rows[r0:r0 + 1, o:o + w], dr[nm], sG)
        P.dma("gpsimd", c_onesb, dr["ones_row"], sG)
        tok_consts = sL.tok()
        t1 = P.op("vector", lambda e: e.tensor_copy(c_gh, c_gw), waits=[tok_consts], inc=sV)
        P.op("vector", lambda e: e.tensor_sub(c_gl, c_gw, c_gh), waits=[t1], inc=sV)
        tok_gate = sV.tok()

        pool.mark()
        ztile = pool.alloc([128, 2048], BF16, "ztile")
        tz = P.op("gpsimd", lambda e: e.memset(ztile, 0.0), inc=sems["sM"])
        xs_r = xs_scr.rearrange("(n a p) c -> n p a c", p=128, a=2)
        for i in range(NSEG * MOE_CHUNK // 256):
            P.dma("gpsimd", xs_r[i], ztile.rearrange("p (a c) -> p a c", c=D), sG, waits=[tz])
        pool.release()

        tok_attn = {}

        def emit_transcode():
            i = [0]
            pend = [None]

            def flush():
                if pend[0] is not None:
                    dst_ap, bv = pend[0]
                    P.dma("sync", dst_ap, bv, sTco, waits=[sTci.tok()])
                    pend[0] = None

            def piece(dst_ap, src_ap, n):
                k = i[0] % 2
                bv = tc_bb[k][:, :n].rearrange("p (a c) -> p a c", c=src_ap.shape[-1])
                P.dma("gpsimd", bv, src_ap, sTci,
                      waits=[sTco.tok()] if i[0] >= 2 else [])
                flush()
                pend[0] = (dst_ap, bv)
                i[0] += 1

            for nm in ("wqaTp", "wkaTp", "wvaT", "woaT", "wqb", "wob"):
                src = dr[nm].rearrange("(h a p) c -> h p a c", p=128, a=4)
                dc = wat_c[nm].rearrange("h p (s a c) -> h s p a c", s=2, c=1024)
                for h in range(2):
                    for sub in range(2):
                        piece(dc[h, sub], src[h][:, 2 * sub:2 * sub + 2], 2048)
                flush()
                tok_attn[nm] = sTco.tok()
            piece(wdb_c.rearrange("p (a c) -> p a c", c=256),
                  dr["wdb"].rearrange("(a p) c -> p a c", p=128), 2048)
            flush()
            tok_attn["wdb"] = sTco.tok()
            wub_v = wub_c.rearrange("p (ch c) -> ch p c", ch=2)
            srcu = dr["wub"].rearrange("(ch p) c -> ch p c", p=128)
            for ch in range(2):
                piece(wub_v[ch].unsqueeze(1), srcu[ch].unsqueeze(1), 2048)
            flush()
            tok_attn["wub"] = sTco.tok()
            for ei in range(E):
                for kind, nm in ((0, "w1T"), (1, "w3T"), (2, "w2T")):
                    src = dr[nm].rearrange("e (h a p) c -> e h p a c", p=128, a=4)
                    dstr = wmoe.rearrange("e k h p (s a c) -> e k h s p a c", s=2, c=1024)
                    for h in range(2):
                        for sub in range(2):
                            piece(dstr[ei, kind, h, sub], src[ei, h][:, 2 * sub:2 * sub + 2], 2048)
            for m, nm in ((0, "sw1T"), (1, "sw2T")):
                src = dr[nm].rearrange("(a p) (blk i) -> blk p a i", p=128, blk=16)
                for blk in range(16):
                    piece(sw12_scr[m, blk], src[blk], 1024)
            src = dr["sw3T"].rearrange("(ic p) c -> ic p c", p=128)
            for ic in range(16):
                piece(sw3_scr[ic], src[ic], 1024)
            flush()
            return (sTco.tok(),)

        emit_transcode()

        def mk_ln(pl, n=16):
            d = {nm: pl.alloc([128, n], F32, "ln_" + nm)
                 for nm in ("sum", "sq", "m2", "var", "r", "nm")}
            d["sqs"] = pl.alloc([128, 1024], BF16, "ln_sqs")
            d["n"] = n
            return d

        def ln_stats_tile(sd, t, x_tile, extra_waits=()):
            tv = P.op("vector", lambda e, t=t: e.reduce_sum(sd["sum"][:, t:t + 1], x_tile, AX),
                      waits=extra_waits, inc=sV)
            ts2 = P.op("scalar", lambda e, t=t: e.activation(out=sd["sqs"], in_=x_tile, func=AF.Square,
                                                             accum_out=sd["sq"][:, t:t + 1]),
                       waits=tuple(extra_waits) + (rd.get(("sqs", id(sd))),), inc=sS)
            rd[("sqs", id(sd))] = ts2
            return tv, ts2

        def ln_finish(sd):
            t1 = P.op("vector", lambda e: e.tensor_mul(sd["m2"], sd["sum"], sd["sum"]),
                      waits=[sV.tok(), sS.tok()], inc=sV)
            t2 = P.op("vector", lambda e: e.scalar_tensor_tensor(sd["var"], sd["m2"], -1.0 / D,
                                                                 sd["sq"], ALU.mult, ALU.add),
                      waits=[t1], inc=sV)
            t3 = P.op("vector", lambda e: e.tensor_scalar(sd["var"], sd["var"], 1.0 / D, EPS,
                                                          ALU.mult, ALU.add), waits=[t2], inc=sV)
            t4 = P.op("scalar", lambda e: e.activation(out=sd["m2"], in_=sd["var"], func=AF.Sqrt,
                                                       bias=0.0, scale=1.0), waits=[t3], inc=sS)
            t5 = P.op("vector", lambda e: e.reciprocal(sd["r"], sd["m2"]), waits=[t4], inc=sV)
            t6 = P.op("vector", lambda e: e.tensor_scalar(sd["nm"], sd["sum"], -1.0 / D, None,
                                                          ALU.mult), waits=[sV.tok()], inc=sV)
            t7 = P.op("vector", lambda e: e.tensor_mul(sd["nm"], sd["nm"], sd["r"]),
                      waits=[t6, t5], inc=sV)
            sd["tok"] = t7
            return t7

        def ln_norm(sd, t, x_tile, dst_bf, dst_f32=None, dst_lo=None, extra_waits=()):
            w = list(extra_waits) + [sd["tok"]]
            if dst_f32 is not None:
                th = P.op("vector", lambda e, t=t: e.tensor_scalar(dst_bf, x_tile,
                                                                   sd["nm"][:, t:t + 1],
                                                                   sd["r"][:, t:t + 1],
                                                                   ALU.add, ALU.mult),
                          waits=w, inc=sV)
                tf = P.op("vector", lambda e, t=t: e.tensor_scalar(dst_f32, x_tile,
                                                                   sd["nm"][:, t:t + 1],
                                                                   sd["r"][:, t:t + 1],
                                                                   ALU.add, ALU.mult),
                          waits=w, inc=sV)
                tl = P.op("vector", lambda e: e.tensor_sub(dst_lo, dst_f32, dst_bf), waits=[tf], inc=sV)
                return th, tl
            th = P.op("vector", lambda e, t=t: e.tensor_scalar(dst_bf, x_tile, sd["nm"][:, t:t + 1],
                                                               sd["r"][:, t:t + 1], ALU.add, ALU.mult),
                      waits=w, inc=sV)
            return th, None

        def transpose8(src_ap, dst_fn, banks, wait_tok, n=8, key_pfx="psTt", alt=False):
            nb = len(banks)
            lastV = lastS = None
            for j in range(n):
                key = (key_pfx, j % nb)
                pt = banks[j % nb]
                tmm = P.op("tensor", lambda e, pt=pt, j=j: e.transpose(
                    pt[:, 0:128], src_ap[:, 128 * j:128 * (j + 1)], c_id128),
                    waits=[wait_tok, sL.tok(), rd.get(key)], inc=sT)
                if alt and j % 2 == 1:
                    lastS = P.op("scalar", lambda e, pt=pt, j=j: e.copy(out=dst_fn(j), in_=pt[:, 0:128]),
                                 waits=[tmm], inc=sS)
                    rd[key] = lastS
                else:
                    lastV = P.op("vector", lambda e, pt=pt, j=j: e.tensor_copy(dst_fn(j), pt[:, 0:128]),
                                 waits=[tmm], inc=sV)
                    rd[key] = lastV
            if alt and lastS is not None:
                lastV = P.op("vector", lambda e: e.nop(), waits=[lastS, lastV], inc=sV)
            return lastV

        def transpose8g(src_fn, dst_views, banks2, wait_tok, keys2, ev_waits=()):
            ev = None
            for grp in range(2):
                bank, key, dv = banks2[grp], keys2[grp], dst_views[grp]
                for jj in range(4):
                    j = 4 * grp + jj
                    P.op("tensor", lambda e, j=j, jj=jj, bank=bank: e.matmul(
                        bank[:, 128 * jj:128 * (jj + 1)], src_fn(j), c_id128,
                        start=(jj == 0), stop=(jj == 3)),
                        waits=([wait_tok, sL.tok(), rd.get(key)] if jj == 0 else []),
                        inc=sT if jj == 3 else None)
                tmm = sT.tok()
                bk = bank
                if len(dv.shape) == 3:
                    bk = bank.rearrange("p (a c) -> p a c", c=128)
                ev = P.op("vector", lambda e, bk=bk, dv=dv: e.tensor_copy(dv, bk),
                          waits=[tmm] + list(ev_waits), inc=sV)
                rd[key] = ev
            return ev

        def sublayer_attn_a(b, first):
            pool.mark()
            Qr = pool.alloc([128, 4 * T], BF16, "Qr")
            Kr = pool.alloc([128, 4 * T], BF16, "Kr")
            Vx = pool.alloc([128, NTILE * NH * 65], BF16, "Vx")
            mhat = pool.alloc([128, 1], F32, "mhat")
            Vx4 = Vx.rearrange("p (t h j) -> p t h j", h=NH, j=65)
            pc = ExitStack()
            with pc:
                psTb = [mkps(pc, "psTa", [128, 512], F32) for _ in range(2)]
                psQ = mkps(pc, "psQ", [128, 512], F32)
                psQ2 = mkps(pc, "psQ2", [128, 512], F32)
                psM = mkps(pc, "psM", [128, 512], F32)
                psM2 = mkps(pc, "psM2", [128, 512], F32)
                psM3 = mkps(pc, "psM3", [128, 512], F32)
                psM4 = mkps(pc, "psM4", [128, 512], F32)

                fence()
                pool.mark()
                wqk = pool.alloc([128, 2 * 8192], BF16, "wqk")
                fm2 = [pool.alloc([128, 8 * 512], BF16, "fmA"), pool.alloc([128, 8 * 512], BF16, "fmB")]
                nh4 = pool.alloc([128, 2048], BF16, "nh4")
                cs2 = [pool.alloc([128, 1024], BF16, "csA"), pool.alloc([128, 1024], BF16, "csB")]
                lns = mk_ln(pool)
                ev = {k: pool.alloc([128, 512], BF16, "ev" + k) for k in ("qe", "qo")}
                for tt in range(NTILE):
                    ln_stats_tile(lns, tt, x_sb[:, 1024 * tt:1024 * (tt + 1)],
                                  extra_waits=[sX.tok()] if tt == 0 else ())
                ln_finish(lns)
                for i2, wnm in enumerate(("wqaTp", "wkaTp")):
                    for h in range(2):
                        P.dma("sync", wqk[:, i2 * 8192 + h * 4096: i2 * 8192 + (h + 1) * 4096],
                              wat_c[wnm][h], sL, waits=[tok_attn[wnm]])
                tok_w = sL.tok()

                def emit_trs_i(tc):
                    par = tc % 2
                    fm = fm2[par]
                    P.wait("vector", rd.get(("fmw", par)))
                    P.dma("sync", cs2[par][:, 0:512], dr["cos4"][:, 512 * tc:512 * (tc + 1)], sCs,
                          waits=[rd.get(("cs", par))])
                    P.dma("sync", cs2[par][:, 512:1024], dr["sin4"][:, 512 * tc:512 * (tc + 1)], sCs,
                          waits=[rd.get(("cs", par))])
                    tok_fm = None
                    for ti in range(4):
                        tt = 4 * tc + ti
                        np_ = ti % 2
                        th, _ = ln_norm(lns, tt, x_sb[:, 1024 * tt:1024 * (tt + 1)],
                                        nh4[:, 1024 * np_:1024 * (np_ + 1)],
                                        extra_waits=[rd.get(("nh4", np_))])
                        fm_v = fm.rearrange("p (a c) -> p a c", c=512)[:, :, 128 * ti:128 * (ti + 1)]
                        tok_fm = transpose8g(
                            lambda j, np_=np_, nh4=nh4: nh4[:, 1024 * np_ + 128 * j: 1024 * np_ + 128 * (j + 1)],
                            (fm_v[:, 0:4], fm_v[:, 4:8]), psTb, th, (("aT", 0), ("aT", 1)))
                        rd[("nh4", np_)] = sT.tok()
                    return tok_fm

                def emit_mms_i(tc, tok_fm):
                    par = tc % 2
                    fm = fm2[par]
                    cs = cs2[par]
                    tok_cs = sCs.tok()
                    for is_q, base_w, bce, bco, dstbuf in (
                            (True, 0, 0, 4, Qr), (False, 8192, 8, 12, Kr)):
                        for rb in range(4):
                            for half, ps, pskey, bcol, d2 in (
                                    (0, psQ, "psQ", bce, "qe"), (1, psQ2, "psQ2", bco, "qo")):
                                woff = base_w + 512 * half
                                for a in range(8):
                                    P.op("tensor", lambda e, ps=ps, a=a, rb=rb, woff=woff, fm=fm: e.matmul(
                                        ps, wqk[:, woff + a * 1024 + rb * 128: woff + a * 1024 + rb * 128 + 128],
                                        fm[:, 512 * a:512 * (a + 1)], start=(a == 0), stop=(a == 7)),
                                        waits=([tok_fm, tok_w, sL.tok(), rd.get(pskey)] if a == 0 else []),
                                        inc=sT if a == 7 else None)
                                tmm = sT.tok()
                                sc2 = 0.125 if is_q else 1.0
                                tev = P.op("vector", lambda e, ps=ps, d2=d2, bc=bcol, rb=rb, sc2=sc2:
                                           e.tensor_scalar(ev[d2], ps, c_battn[:, bc + rb:bc + rb + 1],
                                                           sc2, ALU.add, ALU.mult),
                                           waits=[tmm, rd.get("evw")], inc=sV)
                                rd[pskey] = tev
                            tu = P.op("vector", lambda e, cs=cs: e.tensor_mul(ev["qe"], ev["qe"], cs[:, 0:512]),
                                      waits=[tev, tok_cs], inc=sV)
                            tv = P.op("vector", lambda e, cs=cs: e.tensor_mul(ev["qo"], ev["qo"], cs[:, 512:1024]),
                                      waits=[tu], inc=sV)
                            tr = P.op("vector", lambda e, dstbuf=dstbuf, rb=rb, tc=tc: e.tensor_sub(
                                dstbuf[:, rb * T + 512 * tc: rb * T + 512 * (tc + 1)],
                                ev["qe"], ev["qo"]), waits=[tv], inc=sV)
                            rd[("cs", par)] = tr
                            rd["evw"] = tr
                    rd[("fmw", par)] = sT.tok()

                toks_i = {}
                for tc in range(4):
                    toks_i[tc] = emit_trs_i(tc)
                    if tc > 0:
                        emit_mms_i(tc - 1, toks_i[tc - 1])
                emit_mms_i(3, toks_i[3])
                pool.release()

                fence()
                pool.mark()
                wv = pool.alloc([128, 8192], BF16, "wv")
                fm2 = [pool.alloc([128, 8 * 512], BF16, "fm2A"), pool.alloc([128, 8 * 512], BF16, "fm2B")]
                nh4 = pool.alloc([128, 2048], BF16, "nh4v")
                lns = mk_ln(pool)
                for tt in range(NTILE):
                    ln_stats_tile(lns, tt, x_sb[:, 1024 * tt:1024 * (tt + 1)])
                ln_finish(lns)
                for h in range(2):
                    P.dma("sync", wv[:, h * 4096:(h + 1) * 4096], wat_c["wvaT"][h], sL,
                          waits=[tok_attn["wvaT"]])
                tok_w = sL.tok()
                tvx = P.op("vector", lambda e: e.memset(Vx4[:, :, :, 64:65], 1.0), inc=sV)

                def emit_trs_v(tc):
                    par = tc % 2
                    fm = fm2[par]
                    P.wait("vector", rd.get(("fmwv", par)))
                    tok_fm = None
                    for ti in range(4):
                        tt = 4 * tc + ti
                        np_ = ti % 2
                        th, _ = ln_norm(lns, tt, x_sb[:, 1024 * tt:1024 * (tt + 1)],
                                        nh4[:, 1024 * np_:1024 * (np_ + 1)],
                                        extra_waits=[rd.get(("nh4v", np_))])
                        fm_v = fm.rearrange("p (a c) -> p a c", c=512)[:, :, 128 * ti:128 * (ti + 1)]
                        tok_fm = transpose8g(
                            lambda j, np_=np_, nh4=nh4: nh4[:, 1024 * np_ + 128 * j: 1024 * np_ + 128 * (j + 1)],
                            (fm_v[:, 0:4], fm_v[:, 4:8]), psTb, th, (("aT", 0), ("aT", 1)))
                        rd[("nh4v", np_)] = sT.tok()
                    return tok_fm

                def emit_mms_v(tc, tok_fm):
                    par = tc % 2
                    fm = fm2[par]
                    pmb = [psM, psM2, psQ, psQ2, psM3, psM4]
                    for ti in range(4):
                        tt = 4 * tc + ti
                        for oc in range(2):
                            pm = pmb[(2 * ti + oc) % 6]
                            for a in range(8):
                                P.op("tensor", lambda e, a=a, ti=ti, oc=oc, fm=fm, pm=pm: e.matmul(
                                    pm, fm[:, 512 * a + 128 * ti:512 * a + 128 * (ti + 1)],
                                    wv[:, a * 1024 + 512 * oc: a * 1024 + 512 * (oc + 1)],
                                    start=(a == 0), stop=False),
                                    waits=([tok_fm, tok_w, rd.get(("psM", (2 * ti + oc) % 6)), tvx] if a == 0 else []))
                            P.op("tensor", lambda e, oc=oc, pm=pm: e.matmul(
                                pm, c_onesb[0:1, :], c_rows[0:1, 512 * oc:512 * (oc + 1)],
                                start=False, stop=True), inc=sT)
                            tmm = sT.tok()
                            tev = P.op("vector", lambda e, tt=tt, oc=oc, pm=pm: e.tensor_copy(
                                Vx4[:, tt, 8 * oc:8 * (oc + 1), 0:64],
                                pm.rearrange("p (h j) -> p h j", j=64)),
                                waits=[tmm], inc=sV)
                            rd[("psM", (2 * ti + oc) % 6)] = tev
                    rd[("fmwv", par)] = sT.tok()

                toks_v = {}
                for tc in range(4):
                    toks_v[tc] = emit_trs_v(tc)
                    if tc > 0:
                        emit_mms_v(tc - 1, toks_v[tc - 1])
                emit_mms_v(3, toks_v[3])
                pool.release()

                fence()
                pool.mark()
                sqb = pool.alloc([128, 2048], BF16, "sqb")
                nrm = pool.alloc([4, 40], F32, "nrm")
                srow = pool.alloc([1, 8], F32, "srow")
                tprev = sV.tok()
                tsq_war = None
                for wi, srcb in ((0, Qr), (1, Kr)):
                    for blk in range(4):
                        tsq = P.op("scalar", lambda e, srcb=srcb, blk=blk: e.activation(
                            out=sqb, in_=srcb[:, blk * T:(blk + 1) * T], func=AF.Square),
                            waits=[tprev, tsq_war], inc=sS)
                        for qc in range(4):
                            tmm = P.op("tensor", lambda e, qc=qc: e.matmul(
                                psM[0:4, :], c_bd4, sqb[:, 512 * qc:512 * (qc + 1)],
                                start=True, stop=True), waits=[tsq, rd.get("psM")], inc=sT)
                            tprev = P.op("vector", lambda e, wi=wi, blk=blk, qc=qc: e.reduce_max(
                                nrm[:, 16 * wi + 4 * blk + qc:16 * wi + 4 * blk + qc + 1],
                                psM[0:4, :], AX), waits=[tmm], inc=sV)
                            rd["psM"] = tprev
                        tsq_war = tprev
                t1 = P.op("vector", lambda e: e.reduce_max(nrm[:, 32:33], nrm[:, 0:16], AX),
                          waits=[tprev], inc=sV)
                t2 = P.op("vector", lambda e: e.reduce_max(nrm[:, 33:34], nrm[:, 16:32], AX),
                          waits=[t1], inc=sV)
                t3 = P.op("tensor", lambda e: e.transpose(psM[0:2, 0:4], nrm[0:4, 32:34],
                                                          c_id8[0:4, 0:4]),
                          waits=[t2, rd.get("psM")], inc=sT)
                t4 = P.op("vector", lambda e: e.reduce_max(nrm[0:2, 34:35], psM[0:2, 0:4], AX),
                          waits=[t3], inc=sV)
                rd["psM"] = t4
                t5 = P.op("tensor", lambda e: e.transpose(psM[0:1, 0:2], nrm[0:2, 34:35],
                                                          c_id8[0:2, 0:2]),
                          waits=[t4, rd.get("psM")], inc=sT)
                t6 = P.op("vector", lambda e: e.tensor_copy(srow[0:1, 0:2], psM[0:1, 0:2]),
                          waits=[t5], inc=sV)
                rd["psM"] = t6
                t7 = P.op("vector", lambda e: e.tensor_mul(srow[0:1, 2:3], srow[0:1, 0:1],
                                                           srow[0:1, 1:2]), waits=[t6], inc=sV)
                t8 = P.op("scalar", lambda e: e.activation(out=srow[0:1, 3:4], in_=srow[0:1, 2:3],
                                                           func=AF.Sqrt, bias=0.0, scale=1.0),
                          waits=[t7], inc=sS)
                t9 = P.op("vector", lambda e: e.tensor_scalar(srow[0:1, 4:5], srow[0:1, 3:4],
                                                              -1.0, None, ALU.mult),
                          waits=[t8], inc=sV)
                tb = P.op("tensor", lambda e: e.matmul(psM[:, 0:1], c_ones[0:1, :], srow[0:1, 4:5],
                                                       start=True, stop=True),
                          waits=[t9, rd.get("psM")], inc=sT)
                tok_mhat = P.op("vector", lambda e: e.tensor_copy(mhat, psM[:, 0:1]),
                                waits=[tb], inc=sV)
                rd["psM"] = tok_mhat
                pool.release()

            if True:
                fence()
                pool.mark()
                wo = pool.alloc([128, 8192], BF16, "wo")
                Ut = [pool.alloc([128, 512], BF16, "Ut0"), pool.alloc([128, 512], BF16, "Ut1")]
                rlr = [pool.alloc([1, 512], F32, "rlr0"), pool.alloc([1, 512], F32, "rlr1")]
                rlbc = [pool.alloc([64, 512], F32, "rlbc0"), pool.alloc([64, 512], F32, "rlbc1")]
                yq = [pool.alloc([128, T], BF16, "yq0"), pool.alloc([128, T], BF16, "yq1")]
                for h in range(2):
                    P.dma("sync", wo[:, h * 4096:(h + 1) * 4096], wat_c["woaT"][h], sL,
                          waits=[tok_attn["woaT"]])
                tok_w = sL.tok()
                with ExitStack() as pc2:
                    psSt = [mkps(pc2, "psSt0", [128, 512], F32), mkps(pc2, "psSt1", [128, 512], F32)]
                    psPV = [mkps(pc2, "psPV0", [65, 512], F32), mkps(pc2, "psPV1", [65, 512], F32),
                            mkps(pc2, "psPV2", [65, 512], F32)]
                    psBc = mkps(pc2, "psBc0", [64, 512], F32)
                    psWo = [mkps(pc2, "psWo0", [128, 512], F32), mkps(pc2, "psWo1", [128, 512], F32)]
                    x_war = sV.tok()

                    def emit_norm(job):
                        (pvi2, ysl2, yrow2, qg2, hh2) = job
                        ri = pvi2 % 2
                        tok_pv = job_tok[0]
                        tr1 = P.op("vector", lambda e, pvi=pvi2, ri=ri: e.reciprocal(rlr[ri], psPV[pvi][64:65, :]),
                                   waits=[tok_pv, rd.get(("rlr", ri))], inc=sV)
                        tb1 = P.op("tensor", lambda e, ri=ri: e.matmul(psBc, c_ones[0:1, 0:64],
                                                                      rlr[ri], start=True, stop=True),
                                   waits=[tr1, rd.get("psBc")], inc=sT)
                        rd[("rlr", ri)] = tb1
                        tb2 = P.op("vector", lambda e, ri=ri: e.tensor_copy(rlbc[ri], psBc),
                                   waits=[tb1, rd.get(("rlbc", ri))], inc=sV)
                        rd["psBc"] = tb2
                        ty = P.op("vector", lambda e, ysl=ysl2, yrow0=yrow2, qg=qg2, pvi=pvi2, ri=ri: e.tensor_mul(
                            ysl[yrow0:yrow0 + 64, 512 * qg:512 * (qg + 1)], psPV[pvi][0:64, :], rlbc[ri]),
                            waits=[tb2, rd.get(("yq", hh2 // 2))], inc=sV)
                        rd[("psPV", pvi2)] = ty
                        rd[("rlbc", ri)] = ty
                        return ty

                    pending_norm = [None]
                    job_tok = [None]
                    gctr = 0
                    for quad in range(4):
                        yq_tok = []
                        for hh in range(4):
                            h = 4 * quad + hh
                            blk, base = h // 4, 32 * (h % 4)
                            ysl, yrow0 = yq[hh // 2], 64 * (hh % 2)
                            for qg in range(4):
                                kts = 4 * qg + 4
                                pvi = gctr % 3
                                gctr += 1
                                pend = None
                                for kt in range(kts + 1):
                                    if kt < kts:
                                        si = kt % 2
                                        ps = psSt[si]
                                        tmm = P.op("tensor", lambda e, ps=ps, blk=blk, base=base, kt=kt, qg=qg:
                                                   e.matmul(ps, Kr[base:base + 32,
                                                                   blk * T + 128 * kt: blk * T + 128 * (kt + 1)],
                                                            Qr[base:base + 32,
                                                               blk * T + 512 * qg: blk * T + 512 * (qg + 1)],
                                                            start=True, stop=True, tile_position=(base, 0)),
                                                   waits=[tok_mhat, rd.get(("st", si))], inc=sT)
                                        tmask = tmm
                                        if kt >= 4 * qg:
                                            pw = 128 * kt - 512 * qg
                                            if pw > 0:
                                                tmask = P.op("vector", lambda e, ps=ps, pw=pw: e.memset(
                                                    ps[:, 0:pw], -1e10), waits=[tmask], inc=sV)
                                            tmask = P.op("vector", lambda e, ps=ps, pw=pw: e.tensor_add(
                                                ps[:, pw:pw + 128], ps[:, pw:pw + 128], c_mask2),
                                                waits=[tmask, sL.tok()], inc=sV)
                                        texp = P.op("scalar", lambda e, ps=ps, si=si: e.activation(
                                            out=Ut[si], in_=ps, func=AF.Exp, bias=mhat[:, 0:1], scale=1.0),
                                            waits=[tmask, rd.get(("ut", si))], inc=sS)
                                        rd[("st", si)] = texp
                                    if kt == 3 and pending_norm[0] is not None:
                                        yq_tok.append(emit_norm(pending_norm[0]))
                                        pending_norm[0] = None
                                    if pend is not None:
                                        ptexp, psi, pkt = pend
                                        tpv = P.op("tensor", lambda e, psi=psi, pkt=pkt, h=h, kts=kts, pvi=pvi: e.matmul(
                                            psPV[pvi], Vx4[:, pkt, h, :].squeeze(),
                                            Ut[psi], start=(pkt == 0), stop=(pkt == kts - 1)),
                                            waits=[ptexp] + ([rd.get(("psPV", pvi))] if pkt == 0 else []), inc=sT)
                                        rd[("ut", psi)] = tpv
                                    if kt < kts:
                                        pend = (texp, si, kt)
                                job_tok[0] = sT.tok()
                                pending_norm[0] = (pvi, ysl, yrow0, qg, hh)
                        if pending_norm[0] is not None:
                            yq_tok.append(emit_norm(pending_norm[0]))
                            pending_norm[0] = None
                        tok_y = yq_tok[-1]
                        for tt2 in range(NTILE):
                            for oc in range(2):
                                pw_i = (tt2 * 2 + oc) % 2
                                psw = psWo[pw_i]
                                w0 = [tok_y, tok_w] + ([x_war] if (tt2 == 0 and oc == 0) else []) + [rd.get(("psWo", pw_i))]
                                P.op("tensor", lambda e, tt2=tt2, oc=oc, psw=psw, quad=quad: e.matmul(
                                    psw, yq[0][:, 128 * tt2:128 * (tt2 + 1)],
                                    wo[:, (2 * quad) * 1024 + 512 * oc: (2 * quad) * 1024 + 512 * (oc + 1)],
                                    start=True, stop=False), waits=w0)
                                P.op("tensor", lambda e, tt2=tt2, oc=oc, psw=psw, quad=quad: e.matmul(
                                    psw, yq[1][:, 128 * tt2:128 * (tt2 + 1)],
                                    wo[:, (2 * quad + 1) * 1024 + 512 * oc: (2 * quad + 1) * 1024 + 512 * (oc + 1)],
                                    start=False, stop=(quad != 0)),
                                    inc=sT if quad != 0 else None)
                                if quad == 0:
                                    P.op("tensor", lambda e, oc=oc, psw=psw: e.matmul(
                                        psw, c_onesb[0:1, :], c_rows[0:1, 1024 + 512 * oc: 1024 + 512 * (oc + 1)],
                                        start=False, stop=True), inc=sT)
                                tmm = sT.tok()
                                tx = P.op("vector", lambda e, tt2=tt2, oc=oc, psw=psw: e.tensor_add(
                                    x_sb[:, 1024 * tt2 + 512 * oc: 1024 * tt2 + 512 * (oc + 1)],
                                    x_sb[:, 1024 * tt2 + 512 * oc: 1024 * tt2 + 512 * (oc + 1)], psw),
                                    waits=[tmm], inc=sV)
                                rd[("psWo", pw_i)] = tx
                                x_war = tx
                        rd[("yq", 0)] = x_war
                        rd[("yq", 1)] = x_war
                pool.release()
            pool.release()


        def sublayer_ffn(b, ridx):
            J = 2 * T
            fence()
            for eng2 in ENGINES:
                P.wait(eng2, sTco.tok(), sTci.tok())
            pool.mark()
            logits = pool.alloc([8, T], F32, "logits")
            w_sb = pool.alloc([128, 32], F32, "w_sb")
            idx_sb = pool.alloc([128, 32], I32, "idx_sb")
            rsm = pool.alloc([128, 16], F32, "rsm")
            rrow = pool.alloc([8, NSEG], F32, "rrow")
            tsrow = pool.alloc([1, 16], F32, "tsrow")
            ts_i32 = pool.alloc([1, 8], I32, "ts_i32")
            eof_f = pool.alloc([1, NSEG], F32, "eof_f")
            eof_i32 = pool.alloc([1, NSEG], I32, "eof_i32")
            lgt = pool.alloc([128, 8], F32, "lgt")
            sce = pool.alloc([128, 8], F32, "sce")
            sc = pool.alloc([128, 8], F32, "sc")
            top8 = pool.alloc([128, 8], F32, "top8")
            oh0 = pool.alloc([128, 8], F32, "oh0")
            oh1 = pool.alloc([128, 8], F32, "oh1")
            xs_pp = [pool.alloc([128, 1024], BF16, "xs0"), pool.alloc([128, 1024], BF16, "xs1")]
            pool.off = (pool.off + 31) // 32 * 32
            pool.mark()
            nh_tm = pool.alloc([128, NTILE * 1024], BF16, "nh_tm")
            oh_fm = pool.alloc([8, J], BF16, "oh_fm")
            incl = pool.alloc([8, J], F32, "incl")

            pool.mark()
            fm = pool.alloc([128, 8 * 512], BF16, "ffm")
            nf32 = pool.alloc([128, 1024], F32, "nf32")
            nl_t = pool.alloc([128, 1024], BF16, "nl_t")
            nlfm = pool.alloc([128, 8 * 128], BF16, "nlfm")
            a_sb = pool.alloc([128, 512], BF16, "a_sb")
            g_sb = pool.alloc([128, 16 * 512], BF16, "g_sb")
            sw_r = [pool.alloc([128, 2 * 1024], BF16, "swr0"),
                    pool.alloc([128, 2 * 1024], BF16, "swr1")]
            sw3_r = [pool.alloc([128, 1024], BF16, "sw3r0"),
                     pool.alloc([128, 1024], BF16, "sw3r1")]
            lns = mk_ln(pool)
            for tt in range(NTILE):
                ln_stats_tile(lns, tt, x_sb[:, 1024 * tt:1024 * (tt + 1)],
                              extra_waits=[sX.tok()] if tt == 0 else ())
            ln_finish(lns)
            tc_dep = None
            with ExitStack() as pca:
                psTb = [mkps(pca, "fT", [128, 512], F32) for _ in range(2)]
                psA = mkps(pca, "fA", [128, 512], F32)
                psB = mkps(pca, "fB", [128, 512], F32)
                psZ2 = [mkps(pca, "fZ0", [128, 1024], F32), mkps(pca, "fZ1", [128, 1024], F32)]
                psG = psB[0:8, 0:128]
                psRf = psA[:, 128:256]
                pend_oh = []

                def emit_oh_pair(job):
                    tt2, jta0, jta1 = job
                    for k2, (oh, tok) in enumerate(((oh0, jta0), (oh1, jta1))):
                        tb1 = P.op("tensor", lambda e, oh=oh: e.transpose(
                            psRf[0:8, :], oh, c_id128f),
                            waits=[tok, sL.tok(), rd.get("psRf")], inc=sT)
                        tb2 = P.op("vector", lambda e, k2=k2, tt=tt2: e.tensor_copy(
                            oh_fm[0:8, k2 * T + 128 * tt: k2 * T + 128 * (tt + 1)], psRf[0:8, :]),
                            waits=[tb1], inc=sV)
                        rd["psRf"] = tb2
                        rd["psA_rt"] = tb2
                        rd["oh0" if k2 == 0 else "oh1"] = tb2
                    jta0 = oh0

                def emit_route_tile(tt, tok_lg):
                    t1 = P.op("tensor", lambda e, tt=tt: e.transpose(
                        psRf[:, 0:8], logits[0:8, 128 * tt:128 * (tt + 1)], c_id8),
                        waits=[tok_lg, rd.get("psRf")], inc=sT)
                    t2 = P.op("vector", lambda e: e.tensor_copy(lgt, psRf[:, 0:8]),
                              waits=[t1, rd.get("lgt")], inc=sV)
                    rd["psRf"] = t2
                    rd["psA_rt"] = t2
                    if pend_oh:
                        emit_oh_pair(pend_oh.pop(0))
                    t3 = P.op("vector", lambda e: e.reduce_max(rsm[:, 0:1], lgt, AX),
                              waits=[t2], inc=sV)
                    t4 = P.op("vector", lambda e: e.tensor_scalar(rsm[:, 1:2], rsm[:, 0:1], -1.0, None, ALU.mult),
                              waits=[t3], inc=sV)
                    t5 = P.op("scalar", lambda e: e.activation(out=sce, in_=lgt, func=AF.Exp,
                                                               bias=rsm[:, 1:2], scale=1.0,
                                                               accum_out=rsm[:, 2:3]),
                              waits=[t4, rd.get("sce")], inc=sS)
                    t6 = P.op("vector", lambda e: e.reciprocal(rsm[:, 3:4], rsm[:, 2:3]),
                              waits=[t5], inc=sV)
                    t7 = P.op("vector", lambda e: e.tensor_scalar(sc, sce, rsm[:, 3:4], None, ALU.mult),
                              waits=[t6], inc=sV)
                    rd["sce"] = t7
                    t8 = P.op("vector", lambda e: e.max(out=top8, in_=sc), waits=[t7], inc=sV)
                    t9a = P.op("vector", lambda e, tt=tt: e.tensor_copy(w_sb[:, 2 * tt:2 * tt + 1], top8[:, 0:1]),
                               waits=[t8], inc=sV)
                    t9b = P.op("vector", lambda e, tt=tt: e.tensor_copy(w_sb[:, 2 * tt + 1:2 * tt + 2], top8[:, 1:2]),
                               waits=[t8], inc=sV)
                    wsb_toks.append(t9b)
                    ta0 = P.op("vector", lambda e: e.tensor_single_scalar(oh0, sc, top8[:, 0:1], ALU.is_equal),
                               waits=[t8, rd.get("oh0")], inc=sV)
                    ta1 = P.op("vector", lambda e: e.tensor_single_scalar(oh1, sc, top8[:, 1:2], ALU.is_equal),
                               waits=[t8, rd.get("oh1")], inc=sV)
                    rd["lgt"] = ta1
                    pend_oh.append((tt, ta0, ta1))

                wsb_toks = []
                for tc in range(4):
                    P.wait("tensor", tc_dep)
                    P.wait("scalar", tc_dep)
                    for ti in range(4):
                        tt = 4 * tc + ti
                        th, tl = ln_norm(lns, tt, x_sb[:, 1024 * tt:1024 * (tt + 1)],
                                         nh_tm[:, 1024 * tt:1024 * (tt + 1)],
                                         dst_f32=nf32, dst_lo=nl_t,
                                         extra_waits=[tc_dep, rd.get("nh_f")])
                        fm_v = fm.rearrange("p (a c) -> p a c", c=512)[:, :, 128 * ti:128 * (ti + 1)]
                        tok_fm = transpose8g(
                            lambda j, tt=tt: nh_tm[:, 1024 * tt + 128 * j: 1024 * tt + 128 * (j + 1)],
                            (fm_v[:, 0:4], fm_v[:, 4:8]), psTb, th, (("fT", 0), ("fT", 1)))
                        tok_nl = transpose8g(
                            lambda j: nl_t[:, 128 * j:128 * (j + 1)],
                            (nlfm[:, 0:512], nlfm[:, 512:1024]), psTb, tl, (("fT", 0), ("fT", 1)))
                        rd["nh_f"] = tok_nl
                        n_mm = 0
                        N_GATE_MM = 8 if GATE_ONEPASS else 24
                        passes = ((c_gh, lambda a2, fm=fm, ti=ti: fm[:, 512 * a2 + 128 * ti:512 * a2 + 128 * (ti + 1)]),
                                  (c_gl, lambda a2, fm=fm, ti=ti: fm[:, 512 * a2 + 128 * ti:512 * a2 + 128 * (ti + 1)]),
                                  (c_gh, lambda a2, nlfm=nlfm: nlfm[:, 128 * a2:128 * (a2 + 1)]))
                        if GATE_ONEPASS:
                            passes = passes[:1]
                        for gw, rhs_fn in passes:
                            for a2 in range(8):
                                P.op("tensor", lambda e, gw=gw, rhs=rhs_fn(a2), a2=a2, st=(n_mm == 0), sp=(n_mm == N_GATE_MM - 1):
                                     e.matmul(psG, gw[:, 8 * a2:8 * (a2 + 1)], rhs, start=st, stop=sp),
                                     waits=([tok_fm, tok_nl, sV.tok(), sL.tok(), rd.get("psG")] if n_mm == 0 else []),
                                     inc=sT if n_mm == N_GATE_MM - 1 else None)
                                n_mm += 1
                        tgl = P.op("scalar", lambda e, tt=tt: e.copy(
                            out=logits[0:8, 128 * tt:128 * (tt + 1)], in_=psG), waits=[sT.tok()], inc=sS)
                        rd["psG"] = tgl
                        emit_route_tile(tt, tgl)
                    tok_ab = None
                    tsa = None
                    for ib in range(16):
                        ring = ib % 2
                        P.dma("sync", sw_r[ring].rearrange("p (m c) -> p m c", m=2),
                              sw12_scr.transpose([1, 2, 0, 3])[ib], sw_sems[ring],
                              waits=[rd.get(("swr", ring))])
                        for wsel, ps in ((0, psA), (1, psB)):
                            for a2 in range(8):
                                P.op("tensor", lambda e, wsel=wsel, ps=ps, a2=a2, ring=ring, fm=fm: e.matmul(
                                    ps, sw_r[ring][:, wsel * 1024 + 128 * a2: wsel * 1024 + 128 * (a2 + 1)],
                                    fm[:, 512 * a2:512 * (a2 + 1)], start=(a2 == 0), stop=(a2 == 7)),
                                    waits=([sw_sems[ring].tok(), tok_fm, sS.tok(), rd.get(("psAB", wsel))]
                                           + ([rd.get("psA_rt")] if wsel == 0 else []) if a2 == 0 else []),
                                    inc=sT if a2 == 7 else None)
                            if wsel == 0:
                                ta = sT.tok()
                                tsa = P.op("scalar", lambda e: e.activation(
                                    out=a_sb, in_=psA, func=AF.Silu, bias=0.0, scale=1.0),
                                    waits=[ta, rd.get("a_sb")], inc=sS)
                                rd[("psAB", 0)] = tsa
                        tb = sT.tok()
                        tg = P.op("vector", lambda e, ib=ib, g_sb=g_sb: e.tensor_mul(
                            g_sb[:, 512 * ib:512 * (ib + 1)], a_sb, psB), waits=[tb, tsa], inc=sV)
                        rd[("psAB", 1)] = tg
                        rd["a_sb"] = tg
                        rd[("swr", ring)] = tg
                        tok_ab = tg
                    rd["psG"] = tok_ab

                    def emit_z(tc, tok_ab, tok_sS=None):
                        tzl = None
                        for zp in range(2):
                            for ic in range(16):
                                ring = ic % 2
                                P.dma("sync", sw3_r[ring], sw3_scr[ic], sw3_sems[ring],
                                      waits=[rd.get(("sw3r", ring))])
                                for tj in range(2):
                                    for dc in range(2):
                                        w0 = []
                                        if tj == 0 and dc == 0:
                                            w0 = [sw3_sems[ring].tok()]
                                            if ic == 0:
                                                w0 += [tok_ab, rd.get("psZ"),
                                                       sS.tok() if tok_sS is None else tok_sS]
                                        last = P.op("tensor", lambda e, ic=ic, tj=tj, dc=dc, ring=ring, zp=zp, g_sb=g_sb: e.matmul(
                                            psZ2[tj][:, 512 * dc:512 * (dc + 1)],
                                            g_sb[:, 512 * ic + 128 * (2 * zp + tj): 512 * ic + 128 * (2 * zp + tj) + 128],
                                            sw3_r[ring][:, 512 * dc:512 * (dc + 1)],
                                            start=(ic == 0), stop=(ic == 15)),
                                            waits=w0,
                                            inc=sT if (tj == 1 and dc == 1) else None)
                                rd[("sw3r", ring)] = last
                            tz = sT.tok()
                            for tj in range(2):
                                tt = 4 * tc + 2 * zp + tj
                                for dc in range(2):
                                    tzl = P.op("vector", lambda e, tt=tt, tj=tj, dc=dc: e.tensor_add(
                                        x_sb[:, 1024 * tt + 512 * dc: 1024 * tt + 512 * (dc + 1)],
                                        x_sb[:, 1024 * tt + 512 * dc: 1024 * tt + 512 * (dc + 1)],
                                        psZ2[tj][:, 512 * dc:512 * (dc + 1)]), waits=[tz], inc=sV)
                            rd["psZ"] = tzl
                        return tzl

                    if tc < 3:
                        tc_dep = emit_z(tc, tok_ab)
                    else:
                        ab3_tok = tok_ab
                        ab3_sS = sS.tok()
                for job in pend_oh:
                    emit_oh_pair(job)
                pend_oh.clear()
                tsc = P.op("vector", lambda e: e.tensor_tensor_scan(incl, oh_fm, oh_fm, 0.0,
                                                                    ALU.add, ALU.bypass),
                           waits=[sV.tok(), sRt.tok()], inc=sV)
                tge = P.op("vector", lambda e: e.tensor_single_scalar(rrow, c_thresh,
                                                                     incl[:, J - 1:J], ALU.is_lt),
                           waits=[tsc, sL.tok()], inc=sV)
                tnt = P.op("vector", lambda e: e.reduce_sum(rsm[0:8, 4:5], rrow, AX),
                           waits=[tge], inc=sV)
                tt1 = P.op("tensor", lambda e: e.transpose(psRf[0:1, 0:8], rsm[0:8, 4:5],
                                                           c_id8), waits=[tnt, rd.get("psRf")], inc=sT)
                tt2 = P.op("vector", lambda e: e.tensor_copy(tsrow[0:1, 0:8], psRf[0:1, 0:8]),
                           waits=[tt1], inc=sV)
                rd["psRf"] = tt2
                tt3 = P.op("vector", lambda e: e.tensor_tensor_scan(tsrow[0:1, 8:16], tsrow[0:1, 0:8],
                                                                    tsrow[0:1, 0:8], 0.0, ALU.add, ALU.bypass),
                           waits=[tt2], inc=sV)
                tt4 = P.op("vector", lambda e: e.tensor_sub(tsrow[0:1, 0:8], tsrow[0:1, 8:16],
                                                            tsrow[0:1, 0:8]), waits=[tt3], inc=sV)
                tt5 = P.op("scalar", lambda e: e.copy(out=ts_i32, in_=tsrow[0:1, 0:8]),
                           waits=[tt4], inc=sS)
                tt6 = P.op("tensor", lambda e: e.transpose(psRf[0:8, 0:1], tsrow[0:1, 0:8],
                                                           c_id8[0:1, 0:1]), waits=[tt4, rd.get("psRf")], inc=sT)
                tt7 = P.op("vector", lambda e: e.tensor_copy(rsm[0:8, 5:6], psRf[0:8, 0:1]),
                           waits=[tt6], inc=sV)
                rd["psRf"] = tt7
                tt8 = P.op("vector", lambda e: e.tensor_single_scalar(rrow, c_gcmp, rsm[0:8, 5:6],
                                                                      ALU.is_ge), waits=[tt7, tge], inc=sV)
                tt9 = P.op("tensor", lambda e: e.matmul(psRf[0:1, 0:NSEG], c_ones8, rrow,
                                                        start=True, stop=True),
                           waits=[tt8, sL.tok(), rd.get("psRf")], inc=sT)
                tta = P.op("vector", lambda e: e.tensor_scalar(eof_f, psRf[0:1, 0:NSEG], -1.0, None,
                                                               ALU.add), waits=[tt9], inc=sV)
                rd["psRf"] = tta
                ttb = P.op("scalar", lambda e: e.copy(out=eof_i32, in_=eof_f), waits=[tta], inc=sS)
                ttc = P.op("vector", lambda e: e.tensor_scalar(rsm[0:8, 6:7], rsm[0:8, 5:6],
                                                               float(MOE_CHUNK), -1.0, ALU.mult, ALU.add),
                           waits=[tt7], inc=sV)
                ttd = P.op("vector", lambda e: e.scalar_tensor_tensor(incl, incl, rsm[0:8, 6:7],
                                                                      oh_fm, ALU.add, ALU.mult),
                           waits=[ttc, tsc, tge], inc=sV)
                rd["psIdx"] = rd.get("psRf")
                tprev = ttd
                idx_toks = []
                for d2 in range(32):
                    ti1 = P.op("tensor", lambda e, d2=d2: e.matmul(
                        psRf[:, 8 + d2:9 + d2], incl[0:8, 128 * d2:128 * (d2 + 1)], c_ones8,
                        start=True, stop=True), waits=[tprev, rd.get("psIdx")], inc=sT)
                    tix = P.op("scalar", lambda e, d2=d2: e.copy(out=idx_sb[:, d2:d2 + 1],
                                                                 in_=psRf[:, 8 + d2:9 + d2]),
                               waits=[ti1], inc=sS)
                    rd["psIdx"] = tix
                    idx_toks.append(tix)
                tok_idx = sS.tok()
                if DEBUG_DUMP:
                    for nm2, ap2 in (("ts%d" % ridx, ts_i32), ("eof%d" % ridx, eof_i32),
                                     ("idx%d" % ridx, idx_sb), ("lgr%d" % ridx, logits)):
                        dd = nc.dram_tensor("dbg_" + nm2, list(ap2.shape), ap2.dtype,
                                            kind="ExternalOutput").ap()
                        P.dma("sync", dd, ap2, sO, waits=[sV.tok(), sS.tok(), sT.tok()])
                txs = None
                for k2 in range(2):
                    for tt in range(NTILE):
                        ring = (k2 * NTILE + tt) % 2
                        txs = P.op("vector", lambda e, k2=k2, tt=tt, ring=ring: e.tensor_scalar(
                            xs_pp[ring], nh_tm[:, 1024 * tt:1024 * (tt + 1)],
                            w_sb[:, 2 * tt + k2:2 * tt + k2 + 1], None, ALU.mult),
                            waits=[sGs.tok(), wsb_toks[tt]], inc=sV)
                        P.op("gpsimd", lambda e, k2=k2, tt=tt, ring=ring: e.indirect_dma_start(
                            out=xs_scr, out_offset=IndirectOffsetOnAxis(idx_sb[:, 16 * k2 + tt:16 * k2 + tt + 1], 0),
                            in_=xs_pp[ring], in_offset=None),
                            waits=[txs, idx_toks[16 * k2 + tt]], inc=sGs, inc_k=16)
                tok_scat = sGs.tok()
                txs_last = txs
                soff = pool.marks[1]

                def slot_alloc(off, shape, dtype, name):
                    pool.uid += 1
                    return nc.alloc_sbuf_tensor_at(f"pl{pool.uid}_{name}", list(shape), dtype,
                                                   offset=pool.base + off).ap()
                w1s = slot_alloc(soff, [128, 8192], BF16, "w1s")
                w3s = slot_alloc(soff + 16384, [128, 8192], BF16, "w3s")
                w2s = slot_alloc(soff + 32768, [128, 8192], BF16, "w2s")
                bst = slot_alloc(soff + 49152, [128, 24], F32, "bst")
                cbase, vbase = NSEG * ridx, NSEG * ridx
                w1base, w3base, w2base, wbbase = 224 * ridx, 224 * ridx, 224 * ridx, 112 * ridx
                emit_z(3, ab3_tok, tok_sS=ab3_sS)
                e0_waits = [txs_last, tok_idx, (sC, cbase), (sCv, vbase), (sCsc, cbase),
                            (sC1, cbase), (sC3, cbase)]
                for kind, dst, wsem in ((0, w1s, sE1), (1, w3s, sE3), (2, w2s, sE2)):
                    for h in range(2):
                        P.dma("sync", dst[:, 4096 * h:4096 * (h + 1)], wmoe[0, kind, h],
                              wsem, waits=e0_waits)
                P.dma("sync", bst, dr["bstage"][0], sEb, waits=e0_waits)
            pool.release()
            pool.release()

            if "D" not in FFN_PHASES:
                pool.release()
                return
            fence()
            pool.mark()
            pool.off += 49248
            pool.hiwater = max(pool.hiwater, pool.off)
            xg = [pool.alloc([128, 2 * 1024], BF16, "xg0"), pool.alloc([128, 2 * 1024], BF16, "xg1")]
            xg_fm = [pool.alloc([128, 8 * 256], BF16, "xgf0"), pool.alloc([128, 8 * 256], BF16, "xgf1")]
            h_fm = pool.alloc([128, 8 * 256], BF16, "h_fm")
            x3_fm = pool.alloc([128, 8 * 256], F32, "x3_fm")
            y_fm = pool.alloc([128, 8 * 256], BF16, "y_fm")
            y_tm = pool.alloc([128, 2 * 1024], BF16, "y_tm")

            def loader(e, ridx=ridx):
                e.wait_ge(sE1.h, 32 * (ridx + 1))
                e.wait_ge(sE3.h, 32 * (ridx + 1))
                e.wait_ge(sE2.h, 32 * (ridx + 1))
                e.wait_ge(sEb.h, 16 * (ridx + 1))
                with e.register(f"ldr{ridx}") as r1, e.register(f"ldr2_{ridx}") as r2:
                    for ei in range(1, E):
                        if not MOE_STATIC_LOADER:
                            e.reg_load(r1, ts_i32[0:1, ei:ei + 1])
                            e.reg_alu(r2, r1, cbase, ALU.add)
                            e.wait_ge(sC1.h, r2)
                        for h in range(2):
                            e.dma_start(out=w1s[:, 4096 * h:4096 * (h + 1)],
                                        in_=wmoe[ei, 0, h]).then_inc(sW1.h, 16)
                        if not MOE_STATIC_LOADER:
                            e.reg_alu(r2, r1, vbase, ALU.add)
                            e.wait_ge(sCv.h, r2)
                            e.reg_alu(r2, r1, cbase, ALU.add)
                            e.wait_ge(sCsc.h, r2)
                        e.dma_start(out=bst, in_=dr["bstage"][ei]).then_inc(sWb.h, 16)
                        if not MOE_STATIC_LOADER:
                            e.wait_ge(sC3.h, r2)
                        for h in range(2):
                            e.dma_start(out=w3s[:, 4096 * h:4096 * (h + 1)],
                                        in_=wmoe[ei, 1, h]).then_inc(sW3.h, 16)
                        if not MOE_STATIC_LOADER:
                            e.wait_ge(sC.h, r2)
                        for h in range(2):
                            e.dma_start(out=w2s[:, 4096 * h:4096 * (h + 1)],
                                        in_=wmoe[ei, 2, h]).then_inc(sW2.h, 16)
            P.wait("gpsimd", (sems["sS"], sems["sS"].n))
            if not MOE_NO_LOADER:
                P.raw("gpsimd", loader)
            sW1.bump(224)
            sW3.bump(224)
            sW2.bump(224)
            sWb.bump(112)

            pcd = ExitStack()
            psTb = [mkps(pcd, "eT", [128, 512], F32) for _ in range(2)]
            psP = [mkps(pcd, f"eP{i2}", [128, 256], F32) for i2 in range(4)]
            for g in (range(NSEG) if MOE_D_LEVEL >= 0 else []):
                pp = g % 2
                for half in (range(2) if not MOE_NO_GATHER else ()):
                    P.dma("sync", xg[pp][:, 1024 * half:1024 * (half + 1)],
                          xs_scr[MOE_CHUNK * g + 128 * half: MOE_CHUNK * g + 128 * (half + 1), :],
                          sX, waits=[tok_scat, rd.get(("xg", pp))])
                tok_xg = sX.tok()

                def mk_wwait(tag, wsem, esem, base2, mul, g=g, ridx=ridx):
                    def wfn(e, tag=tag, wsem=wsem, esem=esem, base2=base2, mul=mul):
                        e.wait_ge(esem.h, mul * (ridx + 1))
                        if MOE_STATIC_TWAIT:
                            e.wait_ge(wsem.h, base2 + mul * (E - 1))
                            return
                        with e.register(f"tw{tag}_{ridx}_{g}") as r1, \
                                e.register(f"tw{tag}b_{ridx}_{g}") as r2:
                            e.reg_load(r1, eof_i32[0:1, g:g + 1])
                            e.reg_alu(r2, r1, mul, ALU.mult)
                            e.reg_alu(r2, r2, base2, ALU.add)
                            e.wait_ge(wsem.h, r2)
                    return wfn
                if not MOE_NO_WAITS:
                    P.wait("tensor", tok_idx)
                P.wait("vector", rd.get(("xgf", pp)))
                if True:
                    if MOE_D_LEVEL >= 1:
                        xgv = xg_fm[pp].rearrange("p (a c) -> p a c", c=256)
                        transpose8g(lambda j, pp=pp: xg[pp][:, 128 * j:128 * (j + 1)],
                                    (xgv[0:128, 0:4, 0:128], xgv[0:128, 4:8, 0:128]),
                                    psTb, tok_xg, (("eT", 0), ("eT", 1)))
                        tok_tr = transpose8g(lambda j, pp=pp: xg[pp][:, 1024 + 128 * j:1024 + 128 * (j + 1)],
                                             (xgv[0:128, 0:4, 128:256], xgv[0:128, 4:8, 128:256]),
                                             psTb, tok_xg, (("eT", 0), ("eT", 1)))
                        rd[("xg", pp)] = tok_tr
                    else:
                        tok_tr = tok_xg
                    for wsl, dst, kindb, is_h in (((w1s, h_fm, 0, True), (w3s, x3_fm, 8, False)) if MOE_D_LEVEL >= 2 else ()):
                        if not MOE_NO_WAITS:
                            if is_h:
                                P.raw("tensor", mk_wwait("1", sW1, sE1, w1base, 32))
                                P.raw("tensor", mk_wwait("b", sWb, sEb, wbbase, 16))
                            else:
                                P.raw("tensor", mk_wwait("3", sW3, sE3, w3base, 32))
                        for hb in range(8):
                            pps = hb % 4
                            for a2 in range(8):
                                P.op("tensor", lambda e, wsl=wsl, hb=hb, a2=a2, pps=pps, pp=pp: e.matmul(
                                    psP[pps], wsl[:, a2 * 1024 + 128 * hb: a2 * 1024 + 128 * (hb + 1)],
                                    xg_fm[pp][:, 256 * a2:256 * (a2 + 1)], start=(a2 == 0), stop=(a2 == 7)),
                                    waits=([tok_tr, rd.get(("psP", pps))] if a2 == 0 else []),
                                    inc=sT if a2 == 7 else None)
                            tmm = sT.tok()
                            if is_h:
                                tev = P.op("scalar", lambda e, hb=hb, pps=pps, kindb=kindb: e.activation(
                                    out=h_fm[:, 256 * hb:256 * (hb + 1)], in_=psP[pps], func=AF.Silu,
                                    bias=bst[:, kindb + hb:kindb + hb + 1], scale=1.0),
                                    waits=[tmm, rd.get("h_fm")], inc=sS)
                            else:
                                tev = P.op("vector", lambda e, hb=hb, pps=pps, kindb=kindb: e.tensor_scalar(
                                    x3_fm[:, 256 * hb:256 * (hb + 1)], psP[pps],
                                    bst[:, kindb + hb:kindb + hb + 1], None, ALU.add),
                                    waits=[tmm, rd.get("x3_fm")], inc=sV)
                            rd[("psP", pps)] = tev
                        if not MOE_NO_NOPS:
                            P.op("tensor", lambda e: e.nop(), inc=sC1 if is_h else sC3)
                        else:
                            (sC1 if is_h else sC3).bump(1)
                    tok_h = (sS.tok(), sV.tok())
                    rd[("xgf", pp)] = sT.tok()
                    if not MOE_NO_WAITS and MOE_D_LEVEL >= 3:
                        P.raw("tensor", mk_wwait("2", sW2, sE2, w2base, 32))
                    for db in (range(8) if MOE_D_LEVEL >= 3 else ()):
                        pps = db % 4
                        for hc in range(8):
                            P.op("tensor", lambda e, db=db, hc=hc, pps=pps: e.matmul(
                                psP[pps], w2s[:, hc * 1024 + 128 * db: hc * 1024 + 128 * (db + 1)],
                                h_fm[:, 256 * hc:256 * (hc + 1)], start=(hc == 0), stop=(hc == 7)),
                                waits=([tok_h[0], tok_h[1], rd.get(("psP", pps))] if hc == 0 else []),
                                inc=sT if hc == 7 else None)
                        tmm = sT.tok()
                        tev = P.op("vector", lambda e, db=db, pps=pps: e.scalar_tensor_tensor(
                            y_fm[:, 256 * db:256 * (db + 1)], psP[pps], bst[:, 16 + db:16 + db + 1],
                            x3_fm[:, 256 * db:256 * (db + 1)], ALU.add, ALU.mult),
                            waits=[tmm, rd.get("y_fm")], inc=sV)
                        rd[("psP", pps)] = tev
                    rd["h_fm"] = sT.tok()
                    rd["x3_fm"] = sV.tok()
                    tok_y = sV.tok()
                    if not MOE_NO_NOPS:
                        P.op("tensor", lambda e: e.nop(), inc=sC)
                    else:
                        sC.bump(1)
                    ytoks = [None, None]
                    if MOE_D_LEVEL >= 4:
                        for rr in range(2):
                            ytoks[rr] = transpose8g(
                                lambda j, rr=rr: y_fm[:, 256 * j + 128 * rr: 256 * j + 128 * (rr + 1)],
                                (y_tm[:, 1024 * rr:1024 * rr + 512], y_tm[:, 1024 * rr + 512:1024 * (rr + 1)]),
                                psTb, tok_y, (("eT", 0), ("eT", 1)),
                                ev_waits=[rd.get("y_tm")])
                    rd["y_fm"] = sT.tok()
                    for half in (range(2) if MOE_D_LEVEL >= 4 else ()):
                        P.dma("sync", y_scr[MOE_CHUNK * g + 128 * half: MOE_CHUNK * g + 128 * (half + 1), :],
                              y_tm[:, 1024 * half:1024 * (half + 1)], sY, waits=[ytoks[half]])
                    rd["y_tm"] = sY.tok()
                    rd[("xg", pp)] = rd.get(("xg", pp))
                    if not MOE_NO_NOPS:
                        P.op("vector", lambda e: e.nop(), waits=[tok_y], inc=sCv)
                        P.op("scalar", lambda e: e.nop(), waits=[tok_h[0]], inc=sCsc)
            pcd.close()
            tok_wb = sY.tok()

            if "E" not in FFN_PHASES:
                pool.release()
                pool.release()
                return
            yg = [pool.alloc([128, 1024], BF16, "yg0"), pool.alloc([128, 1024], BF16, "yg1")]
            for tt in range(NTILE):
                for k2 in range(2):
                    P.op("gpsimd", lambda e, tt=tt, k2=k2: e.indirect_dma_start(
                        out=yg[k2], out_offset=None, in_=y_scr,
                        in_offset=IndirectOffsetOnAxis(idx_sb[:, 16 * k2 + tt:16 * k2 + tt + 1], 0)),
                        waits=[tok_wb, tok_idx, rd.get(("yg", k2))], inc=sGs, inc_k=16)
                tok_g = sGs.tok()
                ty1 = P.op("vector", lambda e, tt=tt: e.tensor_add(
                    x_sb[:, 1024 * tt:1024 * (tt + 1)], x_sb[:, 1024 * tt:1024 * (tt + 1)], yg[0]),
                    waits=[tok_g], inc=sV)
                ty2 = P.op("vector", lambda e, tt=tt: e.tensor_add(
                    x_sb[:, 1024 * tt:1024 * (tt + 1)], x_sb[:, 1024 * tt:1024 * (tt + 1)], yg[1]),
                    waits=[ty1], inc=sV)
                rd[("yg", 0)] = ty2
                rd[("yg", 1)] = ty2
            pool.release()
            pool.release()


        def sublayer_attn_b(b):
            fence()
            pool.mark()
            r_sb = pool.alloc([128, 2 * T], BF16, "r_sb")
            qp_sb = pool.alloc([128, 8 * T], BF16, "qp_sb")
            os_sb = pool.alloc([128, 8 * T], BF16, "os_sb")
            ctx_sb = pool.alloc([128, 8 * 65], BF16, "ctx_sb")

            pool.mark()
            wdb_sb = pool.alloc([128, 2048], BF16, "wdb")
            wqb_sb = pool.alloc([128, 8192], BF16, "wqb")
            fm2 = [pool.alloc([128, 8 * 512], BF16, "bfmA"), pool.alloc([128, 8 * 512], BF16, "bfmB")]
            nh4 = pool.alloc([128, 2048], BF16, "bnh4")
            esc = [pool.alloc([128, 512], F32, "esc0"), pool.alloc([128, 512], F32, "esc1"),
                   pool.alloc([128, 512], F32, "esc2")]
            lns = mk_ln(pool)
            for tt in range(NTILE):
                ln_stats_tile(lns, tt, x_sb[:, 1024 * tt:1024 * (tt + 1)],
                              extra_waits=[sX.tok()] if tt == 0 else ())
            ln_finish(lns)
            P.dma("sync", wdb_sb, wdb_c, sL, waits=[tok_attn["wdb"]])
            for h in range(2):
                P.dma("sync", wqb_sb[:, h * 4096:(h + 1) * 4096], wat_c["wqb"][h], sL,
                      waits=[tok_attn["wqb"]])
            tok_w = sL.tok()
            with ExitStack() as pc1:
                psTb = [mkps(pc1, "bT", [128, 512], F32) for _ in range(2)]
                psD = mkps(pc1, "bD", [128, 512], F32)
                psQb = [mkps(pc1, f"bQ{i3}", [128, 512], F32) for i3 in range(3)]

                def emit_trs_b(tc):
                    par = tc % 2
                    fm = fm2[par]
                    P.wait("vector", rd.get(("fmwb", par)))
                    tok_fm = None
                    for ti in range(4):
                        tt = 4 * tc + ti
                        np_ = ti % 2
                        th, _ = ln_norm(lns, tt, x_sb[:, 1024 * tt:1024 * (tt + 1)],
                                        nh4[:, 1024 * np_:1024 * (np_ + 1)],
                                        extra_waits=[rd.get(("bnh4", np_))])
                        fm_v = fm.rearrange("p (a c) -> p a c", c=512)[:, :, 128 * ti:128 * (ti + 1)]
                        tok_fm = transpose8g(
                            lambda j, np_=np_, nh4=nh4: nh4[:, 1024 * np_ + 128 * j: 1024 * np_ + 128 * (j + 1)],
                            (fm_v[:, 0:4], fm_v[:, 4:8]), psTb, th, (("bT", 0), ("bT", 1)))
                        rd[("bnh4", np_)] = sT.tok()
                    return tok_fm

                def emit_mms_b(tc, tok_fm):
                    par = tc % 2
                    fm = fm2[par]
                    for rb in range(2):
                        for a in range(8):
                            P.op("tensor", lambda e, rb=rb, a=a, fm=fm: e.matmul(
                                psD, wdb_sb[:, a * 256 + 128 * rb: a * 256 + 128 * (rb + 1)],
                                fm[:, 512 * a:512 * (a + 1)], start=(a == 0), stop=(a == 7)),
                                waits=([tok_fm, tok_w, rd.get("psDb")] if a == 0 else []),
                                inc=sT if a == 7 else None)
                        tmm = sT.tok()
                        tev = P.op("vector", lambda e, rb=rb, tc=tc: e.tensor_scalar(
                            r_sb[:, rb * T + 512 * tc: rb * T + 512 * (tc + 1)], psD,
                            c_battn[:, 24 + rb:25 + rb], None, ALU.add),
                            waits=[tmm, sL.tok()], inc=sV)
                        rd["psDb"] = tev
                    for ob in range(8):
                        ps = psQb[ob % 3]
                        for a in range(8):
                            P.op("tensor", lambda e, ob=ob, a=a, ps=ps, fm=fm: e.matmul(
                                ps, wqb_sb[:, a * 1024 + 128 * ob: a * 1024 + 128 * (ob + 1)],
                                fm[:, 512 * a:512 * (a + 1)], start=(a == 0), stop=(a == 7)),
                                waits=([tok_fm, tok_w, rd.get(("psQb", ob % 3))] if a == 0 else []),
                                inc=sT if a == 7 else None)
                        tmm = sT.tok()
                        t1 = P.op("vector", lambda e, ps=ps, ob=ob: e.tensor_scalar(
                            esc[0], ps, c_battn[:, 16 + ob:17 + ob], 0.0, ALU.add, ALU.min),
                            waits=[tmm, sL.tok(), rd.get("esc")], inc=sV)
                        t2 = P.op("vector", lambda e, ps=ps, ob=ob: e.tensor_scalar(
                            esc[1], ps, c_battn[:, 16 + ob:17 + ob], 0.0, ALU.add, ALU.max),
                            waits=[tmm, t1], inc=sV)
                        rd[("psQb", ob % 3)] = t2
                        t3 = P.op("scalar", lambda e: e.activation(out=esc[2], in_=esc[0],
                                                                   func=AF.Exp, bias=0.0, scale=1.0),
                                  waits=[t1, rd.get("esc2")], inc=sS)
                        t4 = P.op("vector", lambda e, ob=ob, tc=tc: e.tensor_add(
                            qp_sb[:, ob * T + 512 * tc: ob * T + 512 * (tc + 1)], esc[2], esc[1]),
                            waits=[t3, t2], inc=sV)
                        rd["esc"] = t4
                        rd["esc2"] = t4
                    rd[("fmwb", par)] = sT.tok()

                toks_b = {}
                for tc in range(4):
                    toks_b[tc] = emit_trs_b(tc)
                    if tc > 0:
                        emit_mms_b(tc - 1, toks_b[tc - 1])
                emit_mms_b(3, toks_b[3])
            pool.release()

            pool.mark()
            wub_sb = pool.alloc([128, 4096], BF16, "wub")
            kext = pool.alloc([128, 1040], BF16, "kext")
            vext = pool.alloc([128, 1040], BF16, "vext")
            ksc = [pool.alloc([128, 256], F32, "ksc0"), pool.alloc([128, 256], F32, "ksc1"),
                   pool.alloc([128, 256], F32, "ksc2")]
            P.dma("sync", wub_sb, wub_c, sL,
                  waits=[tok_attn["wub"], sT.tok(), sV.tok(), sS.tok()])
            tok_w = sL.tok()
            with ExitStack() as pc2:
                psKV = [mkps(pc2, f"bKV{i3}", [128, 512], F32) for i3 in range(4)]
                psCtx = mkps(pc2, "bCtx", [65, 2048], F32)
                kext4 = kext.rearrange("p (h j) -> p h j", j=65)
                vext4 = vext.rearrange("p (h j) -> p h j", j=65)
                tko = P.op("vector", lambda e: e.memset(kext4[:, :, 64:65], 1.0), inc=sV)
                tvo = P.op("vector", lambda e: e.memset(vext4[:, :, 64:65], 1.0), waits=[tko], inc=sV)
                rd["kvone"] = tvo
                for tt in range(NTILE):
                    for oc in range(4):
                        ps = psKV[oc % 4]
                        for ch in range(2):
                            P.op("tensor", lambda e, tt=tt, oc=oc, ch=ch, ps=ps: e.matmul(
                                ps, r_sb[:, ch * T + 128 * tt: ch * T + 128 * (tt + 1)],
                                wub_sb[:, ch * 2048 + 512 * oc: ch * 2048 + 512 * (oc + 1)],
                                start=(ch == 0), stop=False),
                                waits=([sV.tok(), tok_w, rd.get(("psKV", oc % 4))] if ch == 0 else []))
                        P.op("tensor", lambda e, oc=oc, ps=ps: e.matmul(
                            ps, c_onesb[32:33, :], c_rows[32:33, 512 * oc:512 * (oc + 1)],
                            start=False, stop=True), inc=sT)
                        tmm = sT.tok()
                        pk = bass.AP(ps.tensor, ps.offset, [ps.ap[0], [128, 4], [1, 64]])
                        pv = bass.AP(ps.tensor, ps.offset + 64, [ps.ap[0], [128, 4], [1, 64]])
                        k1 = P.op("vector", lambda e, pk=pk: e.tensor_scalar(
                            ksc[0].rearrange("p (h j) -> p h j", j=64), pk, 0.0, None, ALU.min),
                            waits=[tmm, rd.get("ksc")], inc=sV)
                        k2 = P.op("vector", lambda e, pk=pk: e.tensor_scalar(
                            ksc[1].rearrange("p (h j) -> p h j", j=64), pk, 0.0, None, ALU.max),
                            waits=[tmm, k1], inc=sV)
                        rd[("psKV", oc % 4)] = k2
                        k3 = P.op("scalar", lambda e: e.activation(out=ksc[2], in_=ksc[0],
                                                                   func=AF.Exp, bias=0.0, scale=1.0),
                                  waits=[k1, rd.get("ksc2")], inc=sS)
                        k4 = P.op("vector", lambda e, oc=oc: e.tensor_add(
                            kext4[:, 4 * oc:4 * (oc + 1), 0:64],
                            ksc[2].rearrange("p (h j) -> p h j", j=64),
                            ksc[1].rearrange("p (h j) -> p h j", j=64)),
                            waits=[k3, k2, tvo, rd.get("kext")], inc=sV)
                        kv = P.op("vector", lambda e, pv=pv, oc=oc: e.tensor_copy(
                            vext4[:, 4 * oc:4 * (oc + 1), 0:64], pv),
                            waits=[tmm, tvo, rd.get("kext")], inc=sV)
                        rd["ksc"] = k4
                        rd["ksc2"] = k4
                        rd[("kvoc", oc)] = (k4, kv)
                    toks = [rd[("kvoc", o2)] for o2 in range(4)]
                    tok_kv = [t for pair in toks for t in pair]
                    for h in range(NH):
                        P.op("tensor", lambda e, h=h, tt=tt: e.matmul(
                            psCtx[:, h * 128:h * 128 + 65], kext[:, h * 65:(h + 1) * 65],
                            vext[:, h * 65:(h + 1) * 65],
                            start=(tt == 0 and h % 4 == 0), stop=(tt == NTILE - 1 and h % 4 == 3)),
                            waits=(tok_kv if h == 0 else []), inc=sT if h == NH - 1 else None)
                    rd["kext"] = sT.tok()
                tok_ctx_mm = sT.tok()
                for h in range(NH):
                    tev = P.op("vector", lambda e, h=h: e.tensor_copy(
                        ctx_sb[64 * (h % 2):64 * (h % 2) + 64, (h // 2) * 65:(h // 2) * 65 + 65],
                        psCtx[0:64, h * 128:h * 128 + 65]), waits=[tok_ctx_mm], inc=sV)
                tok_ctx = sV.tok()
            pool.release()

            pool.mark()
            rlr = [pool.alloc([1, 512], F32, "brl0"), pool.alloc([1, 512], F32, "brl1")]
            rlbc = [pool.alloc([64, 512], F32, "brlbc0"), pool.alloc([64, 512], F32, "brlbc1")]
            with ExitStack() as pc3:
                psO = [mkps(pc3, f"bO{i3}", [65, 512], F32) for i3 in range(4)]
                psBc2 = [mkps(pc3, "bBcA", [64, 512], F32), mkps(pc3, "bBcB", [64, 512], F32)]

                def emit_bnormA(job):
                    tok_mm, h2, tc2, poi, ri = job
                    tr1 = P.op("vector", lambda e, poi=poi, ri=ri: e.tensor_scalar(
                        rlr[ri], psO[poi][64:65, :], 1e-6, None, ALU.add),
                        waits=[tok_mm, rd.get(("brlr", ri))], inc=sV)
                    tr2 = P.op("vector", lambda e, ri=ri: e.reciprocal(rlr[ri], rlr[ri]),
                               waits=[tr1], inc=sV)
                    return tr2

                def emit_bnormB(job, tr2):
                    tok_mm, h2, tc2, poi, ri = job
                    base2 = 64 * (h2 % 2)
                    tb1 = P.op("tensor", lambda e, ri=ri: e.matmul(psBc2[ri], c_ones[0:1, 0:64],
                                                                   rlr[ri], start=True, stop=True),
                               waits=[tr2, rd.get(("psBcb", ri))], inc=sT)
                    rd[("brlr", ri)] = tb1
                    tb2 = P.op("scalar", lambda e, ri=ri: e.copy(out=rlbc[ri], in_=psBc2[ri]),
                               waits=[tb1, rd.get(("brlbc", ri))], inc=sS)
                    rd[("psBcb", ri)] = tb2
                    ty = P.op("vector", lambda e, h=h2, tc=tc2, base=base2, poi=poi, ri=ri: e.tensor_mul(
                        os_sb[base:base + 64, (h // 2) * T + 512 * tc: (h // 2) * T + 512 * (tc + 1)],
                        psO[poi][0:64, :], rlbc[ri]), waits=[tb2], inc=sV)
                    rd[("psOb", poi)] = ty
                    rd[("brlbc", ri)] = ty

                pendA, pendB = [], []
                ctr3 = 0
                for h in range(NH):
                    base = 64 * (h % 2)
                    for tc in range(4):
                        poi, ri = ctr3 % 4, ctr3 % 2
                        tmm = P.op("tensor", lambda e, h=h, tc=tc, base=base, poi=poi: e.matmul(
                            psO[poi], ctx_sb[base:base + 64, (h // 2) * 65:(h // 2) * 65 + 65],
                            qp_sb[base:base + 64, (h // 2) * T + 512 * tc: (h // 2) * T + 512 * (tc + 1)],
                            start=True, stop=True, tile_position=(base, 0)),
                            waits=[tok_ctx, rd.get(("psOb", poi))], inc=sT)
                        pendA.append((tmm, h, tc, poi, ri))
                        ctr3 += 1
                        if len(pendA) > 1:
                            jobA = pendA.pop(0)
                            pendB.append((jobA, emit_bnormA(jobA)))
                        if len(pendB) > 1:
                            jb, tr2b = pendB.pop(0)
                            emit_bnormB(jb, tr2b)
                for jobA in pendA:
                    pendB.append((jobA, emit_bnormA(jobA)))
                for jb, tr2b in pendB:
                    emit_bnormB(jb, tr2b)
            pool.release()

            pool.mark()
            wob_sb = pool.alloc([128, 8192], BF16, "wob")
            for h in range(2):
                P.dma("sync", wob_sb[:, h * 4096:(h + 1) * 4096], wat_c["wob"][h], sL,
                      waits=[tok_attn["wob"], sT.tok(), sV.tok(), sS.tok()])
            tok_w = sL.tok()
            with ExitStack() as pc4:
                psW = [mkps(pc4, "bW0", [128, 512], F32), mkps(pc4, "bW1", [128, 512], F32)]
                x_war = sV.tok()
                for tt in range(NTILE):
                    for oc in range(2):
                        pw_i = (2 * tt + oc) % 2
                        psw = psW[pw_i]
                        for ch in range(8):
                            P.op("tensor", lambda e, tt=tt, oc=oc, ch=ch, psw=psw: e.matmul(
                                psw, os_sb[:, ch * T + 128 * tt: ch * T + 128 * (tt + 1)],
                                wob_sb[:, ch * 1024 + 512 * oc: ch * 1024 + 512 * (oc + 1)],
                                start=(ch == 0), stop=False),
                                waits=([sV.tok(), tok_w, rd.get(("psWb", pw_i))] if ch == 0 else []))
                        P.op("tensor", lambda e, oc=oc, psw=psw: e.matmul(
                            psw, c_onesb[64:65, :], c_rows[64:65, 512 * oc:512 * (oc + 1)],
                            start=False, stop=True), inc=sT)
                        tmm = sT.tok()
                        tx = P.op("vector", lambda e, tt=tt, oc=oc, psw=psw: e.tensor_add(
                            x_sb[:, 1024 * tt + 512 * oc: 1024 * tt + 512 * (oc + 1)],
                            x_sb[:, 1024 * tt + 512 * oc: 1024 * tt + 512 * (oc + 1)], psw),
                            waits=[tmm], inc=sV)
                        rd[("psWb", pw_i)] = tx
            pool.release()
            pool.release()

        rctr = [0]
        for b in range(B_RUN):
            if b > 0:
                P.dma("sync", x_sb3, x_r[b], sX,
                      waits=[rd.get("x_stored"), sV.tok()])
            if N_SUBLAYERS >= 1:
                sublayer_attn_a(b, b == 0)
            if N_SUBLAYERS >= 2:
                sublayer_ffn(b, rctr[0]); rctr[0] += 1
            if N_SUBLAYERS >= 3:
                sublayer_attn_b(b)
            if N_SUBLAYERS >= 4:
                sublayer_ffn(b, rctr[0]); rctr[0] += 1
            tst = P.dma("sync", out_r[b], x_sb3, sO, waits=[sV.tok(), sT.tok(), sS.tok()])
            rd["x_stored"] = tst
        if DEBUG_DUMP:
            for nm2, ap2 in list(DBG.items()):
                dt2 = ap2.dtype
                dd = nc.dram_tensor("dbg_" + nm2, list(ap2.shape), dt2,
                                    kind="ExternalOutput").ap()
                P.dma("sync", dd, ap2, sO, waits=[sV.tok(), sT.tok(), sS.tok()])
        P.wait("sync", sO.tok())

        with nc.Block() as block:
            @block.tensor
            def _(e):
                for f in P.streams["tensor"]:
                    f(e)

            @block.vector
            def _(e):
                for f in P.streams["vector"]:
                    f(e)

            @block.scalar
            def _(e):
                for f in P.streams["scalar"]:
                    f(e)

            @block.gpsimd
            def _(e):
                for f in P.streams["gpsimd"]:
                    f(e)

            @block.sync
            def _(e):
                for f in P.streams["sync"]:
                    f(e)

    import sys
    if "--semdump" in sys.argv or True:
        try:
            print("SEMS:", {nm: sems[nm].n for nm in sems}, file=sys.stderr)
        except Exception:
            pass
    return nc


def _host_inputs(ins, core):
    d = {}
    d["x_in"] = np.ascontiguousarray(np.asarray(ins["L0_attn_a_x"])[2 * core:2 * core + 2])

    def T_(a):
        return np.ascontiguousarray(np.asarray(a).T)

    perm = np.concatenate([np.arange(0, D, 2), np.arange(1, D, 2)])
    d["wqaTp"] = np.ascontiguousarray(np.asarray(ins["L0_attn_a_Wq"]).T[:, perm])
    d["wkaTp"] = np.ascontiguousarray(np.asarray(ins["L0_attn_a_Wk"]).T[:, perm])
    d["wvaT"] = T_(ins["L0_attn_a_Wv"])
    d["woaT"] = T_(ins["L0_attn_a_Wo"])
    bq = np.asarray(ins["L0_attn_a_bq"])[perm]
    bk = np.asarray(ins["L0_attn_a_bk"])[perm]
    d["bqe_s"] = np.ascontiguousarray(bq[:512].reshape(4, 128).T)
    d["bqo_s"] = np.ascontiguousarray(bq[512:].reshape(4, 128).T)
    d["bke_s"] = np.ascontiguousarray(bk[:512].reshape(4, 128).T)
    d["bko_s"] = np.ascontiguousarray(bk[512:].reshape(4, 128).T)
    d["bv_row"] = np.asarray(ins["L0_attn_a_bv"]).reshape(1, D)
    d["boa_row"] = np.asarray(ins["L0_attn_a_bo"]).reshape(1, D)
    d["wqb"] = np.ascontiguousarray(np.asarray(ins["L0_attn_b_Wq"]))
    d["wdb"] = np.ascontiguousarray(np.asarray(ins["L0_attn_b_Wd"]))
    d["wub"] = np.ascontiguousarray(np.asarray(ins["L0_attn_b_Wu"]))
    d["wob"] = np.ascontiguousarray(np.asarray(ins["L0_attn_b_Wo"]))
    d["bq2_s"] = np.ascontiguousarray(np.asarray(ins["L0_attn_b_bq"]).reshape(8, 128).T)
    d["bd_s"] = np.ascontiguousarray(np.asarray(ins["L0_attn_b_bd"]).reshape(2, 128).T)
    d["bu_row"] = np.asarray(ins["L0_attn_b_bu"]).reshape(1, 2048)
    d["bob_row"] = np.asarray(ins["L0_attn_b_bo"]).reshape(1, D)
    d["gwT"] = T_(ins["L0_ffn_gate_w"])
    d["w1T"] = np.ascontiguousarray(np.asarray(ins["L0_ffn_w1"]).transpose(0, 2, 1))
    d["w2T"] = np.ascontiguousarray(np.asarray(ins["L0_ffn_w2"]).transpose(0, 2, 1))
    d["w3T"] = np.ascontiguousarray(np.asarray(ins["L0_ffn_w3"]).transpose(0, 2, 1))
    bst = np.zeros((E, 128, 24), np.float32)
    for kind, nm in ((0, "L0_ffn_b1"), (1, "L0_ffn_b3"), (2, "L0_ffn_b2")):
        bb = np.asarray(ins[nm])
        bst[:, :, kind * 8:(kind + 1) * 8] = bb.reshape(E, 8, 128).transpose(0, 2, 1)
    d["bstage"] = bst
    d["sw1T"] = T_(ins["L0_ffn_sw1"])
    d["sw2T"] = T_(ins["L0_ffn_sw2"])
    d["sw3T"] = T_(ins["L0_ffn_sw3"])
    d["id128"] = np.eye(128, dtype=BF)
    d["id128f"] = np.eye(128, dtype=np.float32)
    d["id8"] = np.eye(8, dtype=np.float32)
    onr = np.zeros((65, 128), np.float32)
    onr[0] = 1.0; onr[32] = 1.0; onr[64] = 1.0
    d["ones_row"] = onr
    d["ones8"] = np.ones((8, 1), np.float32)
    bd4 = np.zeros((128, 4), np.float32)
    for m in range(4):
        bd4[32 * m:32 * (m + 1), m] = 1.0
    d["bd4"] = bd4.astype(BF)
    d["mask2"] = np.where(np.arange(128)[:, None] <= np.arange(128)[None, :],
                          np.float32(0.0), np.float32(-1e10)).astype(np.float32)
    d["thresh"] = np.broadcast_to((np.arange(NSEG) * MOE_CHUNK).astype(np.float32), (8, NSEG)).copy()
    d["gcmp"] = np.broadcast_to(np.arange(NSEG).astype(np.float32), (8, NSEG)).copy()
    tpos = np.arange(1, T + 1, dtype=np.float32)[:, None]
    theta = (1.0 / 10000.0) ** (2.0 * np.arange(32, dtype=np.float32) / 64.0)
    ang = tpos * theta
    rows = np.arange(128) % 32
    d["cos4"] = np.cos(ang)[:, rows].T.astype(BF)
    d["sin4"] = np.sin(ang)[:, rows].T.astype(BF)
    return d


def kernel(**inputs):
    nc = build_nc()
    in_maps = [_host_inputs(inputs, c) for c in range(8)]
    res = run_bass_kernel_spmd(nc, in_maps, core_ids=list(range(8)))
    outs = [np.asarray(r["out"]) for r in res.results]
    return np.concatenate(outs, axis=0).reshape(16, T, D).astype(np.float32)


if __name__ == "__main__":
    nc = build_nc()
    print("built ok, instructions:", nc.n_instructions())
